# version 11
# baseline (speedup 1.0000x reference)
import numpy as np
import concourse.bass as bass
import concourse.mybir as mybir
from concourse.bass_utils import run_bass_kernel_spmd
from contextlib import ExitStack

F32 = mybir.dt.float32
BF16 = mybir.dt.bfloat16
AF = mybir.ActivationFunctionType
ALU = mybir.AluOpType

D = 1024
NCH = 8
NT = 4096
EPS_IN = 1e-6
EPS_OUT = 1e-5
DETECT_RACES = False


def emit_l0_attn(nc, es_s, pfx, xin, xout, debug=False, nsub=32, phase_b=True, precast=None, s_pc=None, bout=None, s_bout=None):
    wk_r = nc.dram_tensor(pfx + "wk_r", [NCH, 128, D], F32, kind="ExternalInput").ap()
    wv_r = nc.dram_tensor(pfx + "wv_r", [NCH, 128, D], F32, kind="ExternalInput").ap()
    wq_l = nc.dram_tensor(pfx + "wq_l", [128, 8192], F32, kind="ExternalInput").ap()
    wo_l = nc.dram_tensor(pfx + "wo_l", [128, 8192], F32, kind="ExternalInput").ap()
    bk_row = nc.dram_tensor(pfx + "bk_row", [1, D], F32, kind="ExternalInput").ap()
    bv_row = nc.dram_tensor(pfx + "bv_row", [1, D], F32, kind="ExternalInput").ap()
    bq_col = nc.dram_tensor(pfx + "bq_col", [128, NCH], F32, kind="ExternalInput").ap()
    bo_col = nc.dram_tensor(pfx + "bo_col", [128, NCH], F32, kind="ExternalInput").ap()
    gam_col = nc.dram_tensor(pfx + "gam_col", [128, NCH], F32, kind="ExternalInput").ap()
    bet_col = nc.dram_tensor(pfx + "bet_col", [128, NCH], F32, kind="ExternalInput").ap()
    ones1 = nc.dram_tensor(pfx + "ones1", [1, 128], F32, kind="ExternalInput").ap()
    onesc = nc.dram_tensor(pfx + "onesc", [128, 1], F32, kind="ExternalInput").ap()
    cst = nc.dram_tensor(pfx + "cst", [128, 4], F32, kind="ExternalInput").ap()
    cc_in = nc.dram_tensor(pfx + "cc_in", [64, 1024], F32)
    cc_out = nc.dram_tensor(pfx + "cc_out", [64, 1024], F32)
    if debug:
        d_S = nc.dram_tensor("d_S", [64, 1024], F32, kind="ExternalOutput").ap()
        d_sblk = nc.dram_tensor("d_sblk", [128, 1024], F32, kind="ExternalOutput").ap()
        d_kbf = nc.dram_tensor("d_kbf", [128, 1024], F32, kind="ExternalOutput").ap()
        d_vbf = nc.dram_tensor("d_vbf", [128, 1024], F32, kind="ExternalOutput").ap()
        d_qa = nc.dram_tensor("d_qa", [128, 2048], F32, kind="ExternalOutput").ap()
        d_y = nc.dram_tensor("d_y", [128, 2048], F32, kind="ExternalOutput").ap()
        d_rst = nc.dram_tensor("d_rst", [128, 16], F32, kind="ExternalOutput").ap()
        d_kq = nc.dram_tensor("d_kq", [128, 1024], F32, kind="ExternalOutput").ap()

    NSUB = nsub
    TB = 256
    NTB = NT // TB

    es = ExitStack()
    sb = lambda n, s, d=F32: es.enter_context(nc.sbuf_tensor(pfx + n, s, d))
    ps = lambda n, s, d=F32: es.enter_context(nc.psum_tensor(pfx + n, s, d))
    sem = lambda n: es_s.enter_context(nc.semaphore(pfx + n))

    X = sb("X", [128, NCH * NT])
    W1 = sb("W1", [128, 8192], BF16)
    W2 = sb("W2", [128, 8192], BF16)
    xbf = sb("xbf", [128, 2048], BF16)
    arena = sb("arena", [128, 2048])
    kbf = sb("kbf", [128, 1024], BF16)
    vbf = sb("vbf", [128, 1024], BF16)
    S_acc = sb("S_acc", [64, 1024])
    Sblk = sb("Sblk", [128, 1024], BF16)
    qa = sb("qa", [128, NCH * TB], BF16)
    y_sb = sb("y_sb", [128, NCH * TB])
    y2 = sb("y2", [128, 2 * TB])
    rowsA = sb("rowsA", [1, 2048])
    rows = rowsA[0:1, 0:4 * TB]
    rst = sb("rst", [128, 16])
    bkr = rowsA[0:1, 0:D]; bvr = rowsA[0:1, D:2 * D]
    bqc = sb("bqc", [128, NCH]); boc = sb("boc", [128, NCH])
    gmc = sb("gmc", [128, NCH]); btc = sb("btc", [128, NCH])
    on1 = sb("on1", [1, 128]); onc = sb("onc", [128, 1]); cstc = sb("cstc", [128, 4])

    ps_k = [ps(f"ps_k{i}", [128, 512]) for i in range(2)]
    ps_v = [ps(f"ps_v{i}", [128, 512]) for i in range(2)]
    ps_S = [ps(f"ps_S{i}", [128, 512]) for i in range(2)]
    ps_stat = ps("ps_stat", [1, 512])
    ps_bc = ps("ps_bc", [128, 512])

    s_x = sem("s_x"); s_wA = sem("s_wA"); s_wB = sem("s_wB"); s_mi = sem("s_mi")
    s_cast = sem("s_cast"); s_kv = sem("s_kv"); s_kfree = sem("s_kfree")
    s_vfree = sem("s_vfree"); s_kbf = sem("s_kbf"); s_dmin = sem("s_dmin")
    s_exp = sem("s_exp"); s_eps = sem("s_eps"); s_lnr = sem("s_lnr")
    s_half = sem("s_half"); s_exp2 = sem("s_exp2"); s_S = sem("s_S")
    s_sev = sem("s_sev"); s_dma = sem("s_dma"); s_cc = sem("s_cc")
    s_qmm = sem("s_qmm"); s_q1 = sem("s_q1"); s_exq = sem("s_exq")
    s_qfree = sem("s_qfree"); s_qa = sem("s_qa"); s_atmm = sem("s_atmm")
    s_atev = sem("s_atev"); s_omm = sem("s_omm"); s_yev = sem("s_yev")
    s_sq = sem("s_sq"); s_statm = sem("s_statm")
    s_rows = sem("s_rows"); s_eps2 = sem("s_eps2"); s_lnr2 = sem("s_lnr2")
    s_half2 = sem("s_half2"); s_arow = sem("s_arow"); s_bc = sem("s_bc")
    s_ap1 = sem("s_ap1"); s_ap2 = sem("s_ap2"); s_out = sem("s_out")
    s_sblk = sem("s_sblk"); s_sacc = sem("s_sacc")

    def Xtile(c, col0, w):
        return X[:, c * NT + col0: c * NT + col0 + w]

    def X3(col0, w):
        return X[:].rearrange("p (c n) -> p c n", n=NT)[:, :, col0:col0 + w]

    with es, nc.Block() as block:
        @block.sync
        def _(sync):
            sync.dma_start(out=X[:].rearrange("p (c n) -> p c n", n=NT),
                           in_=xin.rearrange("(c p) n -> p c n", p=128)).then_inc(s_x, 16)
            sync.dma_start(out=bkr, in_=bk_row).then_inc(s_mi, 16)
            sync.dma_start(out=bvr, in_=bv_row).then_inc(s_mi, 16)
            for apd, t in [(bq_col, bqc), (bo_col, boc),
                           (gam_col, gmc), (bet_col, btc), (ones1, on1), (onesc, onc), (cst, cstc)]:
                sync.dma_start(out=t[:], in_=apd).then_inc(s_mi, 16)
            sync.wait_ge(s_sacc, 2 * NSUB)
            sync.dma_start(out=cc_in.ap(), in_=S_acc[:]).then_inc(s_dma, 16)
            sync.wait_ge(s_cc, 1)
            sync.dma_start(out=S_acc[:], in_=cc_out.ap()).then_inc(s_dma, 32)
            sync.wait_ge(s_ap2, NCH * NTB if phase_b else 0)
            if not phase_b:
                sync.wait_ge(s_sblk, 256)
            sync.dma_start(out=xout.rearrange("(c p) n -> p c n", p=128),
                           in_=X[:].rearrange("p (c n) -> p c n", n=NT)).then_inc(s_out, 16)
            nout = 16
            if debug:
                sync.dma_start(out=d_S, in_=S_acc[:]).then_inc(s_out, 16)
                sync.dma_start(out=d_y, in_=y_sb[:]).then_inc(s_out, 16)
                nout += 32

            sync.wait_ge(s_out, nout)

        @block.gpsimd
        def _(gpsimd):
            gpsimd.dma_start(out=W1[:].rearrange("p (c n) -> p c n", n=1024), in_=wk_r.rearrange("c p n -> p c n")).then_inc(s_wA, 16)
            gpsimd.dma_start(out=W2[:].rearrange("p (c n) -> p c n", n=1024), in_=wv_r.rearrange("c p n -> p c n")).then_inc(s_wA, 16)
            gpsimd.wait_ge(s_kv, 4 * NSUB)
            gpsimd.dma_start(out=W1[:], in_=wq_l).then_inc(s_wB, 16)
            gpsimd.dma_start(out=W2[:], in_=wo_l).then_inc(s_wB, 16)
            gpsimd.wait_ge(s_dma, 16)
            gpsimd.collective_compute(
                "AllReduce", ALU.add,
                replica_groups=[[0, 1], [2, 3], [4, 5], [6, 7]],
                ins=[cc_in.ap().opt()], outs=[cc_out.ap().opt()],
            ).then_inc(s_cc)
            if precast:
                for dst, src_ in precast:
                    gpsimd.dma_start(out=dst, in_=src_).then_inc(s_pc, 16)
            gpsimd.wait_ge(s_sev, 1)
            for p in range(NCH):
                for i in range(2):
                    h = 2 * p + i
                    gpsimd.dma_start(
                        out=Sblk[i * 64:(i + 1) * 64, p * 128 + i * 64: p * 128 + (i + 1) * 64],
                        in_=S_acc[0:64, h * 64:(h + 1) * 64]).then_inc(s_sblk, 16)
            if bout is not None:
                gpsimd.wait_ge(s_ap2, NCH * NTB)
                gpsimd.dma_start(out=bout.rearrange("(c p) n -> p c n", p=128),
                                 in_=X[:].rearrange("p (c n) -> p c n", n=NT)).then_inc(s_bout, 16)
            if debug:
                gpsimd.wait_ge(s_sblk, 256)
                gpsimd.dma_start(out=d_sblk, in_=Sblk[:]).then_inc(s_sblk, 16)
                gpsimd.wait_ge(s_S, NSUB)
                gpsimd.dma_start(out=d_kbf, in_=kbf[:]).then_inc(s_sblk, 16)
                gpsimd.dma_start(out=d_rst, in_=rst[:]).then_inc(s_sblk, 16)
                gpsimd.dma_start(out=d_kq, in_=arena[:, 0:1024]).then_inc(s_sblk, 16)
                gpsimd.dma_start(out=d_vbf, in_=vbf[:]).then_inc(s_sblk, 16)
                gpsimd.wait_ge(s_ap2 if phase_b else s_sblk, NCH * NTB if phase_b else 272)
                gpsimd.dma_start(out=d_qa, in_=qa[:]).then_inc(s_sblk, 16)

        @block.tensor
        def _(tensor):
            tensor.wait_ge(s_x, 16)
            tensor.wait_ge(s_wA, 32)
            tensor.wait_ge(s_mi, 144)
            for t in range(NSUB):
                par = t % 2
                xof = par * 1024
                tensor.wait_ge(s_cast, t + 1)
                for half in range(2):
                    if t >= 1:
                        tensor.wait_ge(s_kfree, 2 * (t - 1) + half + 1)
                    tensor.matmul(ps_k[half][:], on1[0:1, :],
                                  bkr[0:1, half * 512:(half + 1) * 512], start=True, stop=False)
                    for cd in range(NCH):
                        ins = tensor.matmul(
                            ps_k[half][:],
                            xbf[:, xof + cd * 128: xof + (cd + 1) * 128],
                            W1[:, cd * 1024 + half * 512: cd * 1024 + (half + 1) * 512],
                            start=False, stop=(cd == NCH - 1))
                    ins.then_inc(s_kv, 1)
                if t >= 1:
                    tensor.wait_ge(s_kbf, 2 * t)
                    tensor.wait_ge(s_vfree, 2 * t)
                    if t >= 2:
                        tensor.wait_ge(s_sacc, 2 * (t - 1))
                    for h in range(16):
                        bank = h // 8
                        col = (h % 8) * 64
                        ins = tensor.matmul(ps_S[bank][0:64, col:col + 64],
                                            kbf[:, h * 64:(h + 1) * 64],
                                            vbf[:, h * 64:(h + 1) * 64],
                                            start=True, stop=True)
                    ins.then_inc(s_S, 1)
                for half in range(2):
                    if t >= 1:
                        tensor.wait_ge(s_vfree, 2 * (t - 1) + half + 1)
                    tensor.matmul(ps_v[half][:], on1[0:1, :],
                                  bvr[0:1, half * 512:(half + 1) * 512], start=True, stop=False)
                    for cd in range(NCH):
                        ins = tensor.matmul(
                            ps_v[half][:],
                            xbf[:, xof + cd * 128: xof + (cd + 1) * 128],
                            W2[:, cd * 1024 + half * 512: cd * 1024 + (half + 1) * 512],
                            start=False, stop=(cd == NCH - 1))
                    ins.then_inc(s_kv, 1)
            tensor.wait_ge(s_kbf, 2 * NSUB)
            tensor.wait_ge(s_vfree, 2 * NSUB)
            tensor.wait_ge(s_sacc, 2 * (NSUB - 1))
            for h in range(16):
                bank = h // 8
                col = (h % 8) * 64
                ins = tensor.matmul(ps_S[bank][0:64, col:col + 64],
                                    kbf[:, h * 64:(h + 1) * 64], vbf[:, h * 64:(h + 1) * 64],
                                    start=True, stop=True)
            ins.then_inc(s_S, 1)

            if not phase_b:
                return
            tensor.wait_ge(s_wB, 32)
            tensor.wait_ge(s_sblk, 256)
            tensor.wait_ge(s_sacc, 2 * NSUB)
            tensor.wait_ge(s_kfree, 2 * NSUB)
            tensor.wait_ge(s_vfree, 2 * NSUB)
            for tt in range(NTB):
                xof = 0
                tensor.wait_ge(s_cast, NSUB + tt + 1)
                for j in range(NCH):
                    k = NCH * tt + j
                    if k >= 2:
                        tensor.wait_ge(s_qfree, k - 1)
                    for cd in range(NCH):
                        ins = tensor.matmul(
                            ps_k[j % 2][:, 0:TB],
                            W1[:, (j * 8 + cd) * 128:(j * 8 + cd + 1) * 128],
                            xbf[:, xof + cd * TB: xof + (cd + 1) * TB],
                            start=(cd == 0), stop=(cd == NCH - 1))
                    ins.then_inc(s_qmm, 1)
                for p in range(NCH):
                    k = NCH * tt + p
                    tensor.wait_ge(s_qa, k + 1)
                    if k >= 2:
                        tensor.wait_ge(s_atev, k - 1)
                    tensor.matmul(ps_v[p % 2][:, 0:TB], Sblk[:, p * 128:(p + 1) * 128],
                                  qa[:, p * TB:(p + 1) * TB],
                                  start=True, stop=True).then_inc(s_atmm, 1)
                tensor.wait_ge(s_atev, NCH * (tt + 1))
                for o in range(NCH):
                    k = NCH * tt + o
                    if k >= 2:
                        tensor.wait_ge(s_yev, k - 1)
                    for cd in range(NCH):
                        ins = tensor.matmul(
                            ps_S[o % 2][:, 0:TB],
                            W2[:, (o * 8 + cd) * 128:(o * 8 + cd + 1) * 128],
                            qa[:, cd * TB:(cd + 1) * TB],
                            start=(cd == 0), stop=(cd == NCH - 1))
                    ins.then_inc(s_omm, 1)
                for rnd in range(2):
                    tensor.wait_ge(s_rows, 2 * tt + rnd)
                    ready = s_yev if rnd == 0 else s_ap1
                    for c in range(NCH):
                        tensor.wait_ge(ready, NCH * tt + c + 1)
                        tensor.matmul(ps_stat[0:1, 0:TB], onc[:, 0:1],
                                      y_sb[:, c * TB:(c + 1) * TB],
                                      start=(c == 0), stop=(c == NCH - 1))
                    for c in range(NCH):
                        tensor.wait_ge(s_sq, 16 * tt + 8 * rnd + c + 1)
                        tensor.matmul(ps_stat[0:1, TB:2 * TB], onc[:, 0:1],
                                      y2[:, (c % 2) * TB:(c % 2 + 1) * TB],
                                      start=(c == 0), stop=(c == NCH - 1)).then_inc(s_statm, 1)
                    tensor.wait_ge(s_arow, 2 * tt + rnd + 1)
                    if rnd == 0:
                        tensor.wait_ge(s_ap2, NCH * tt)
                    else:
                        tensor.wait_ge(s_ap1, NCH * (tt + 1))
                    tensor.matmul(ps_bc[:, 0:TB], on1[0:1, :], rows[0:1, 2 * TB:3 * TB],
                                  start=True, stop=True)
                    tensor.matmul(ps_bc[:, TB:2 * TB], on1[0:1, :], rows[0:1, 3 * TB:4 * TB],
                                  start=True, stop=True).then_inc(s_bc, 1)

        @block.vector
        def _(vector):
            vector.memset(S_acc[:], 0.0)
            vector.wait_ge(s_x, 16)
            for t in range(NSUB):
                par = t % 2
                xof = par * 1024
                if t >= 2:
                    vector.wait_ge(s_kv, 4 * (t - 1))
                vector.tensor_copy(
                    out=xbf[:, xof:xof + 1024].rearrange("p (c n) -> p c n", n=128),
                    in_=X3(t * 128, 128)).then_inc(s_cast, 1)
                for half in range(2):
                    vector.wait_ge(s_kv, 4 * t + half + 1)
                    vector.tensor_scalar(out=arena[:, half * 512:(half + 1) * 512],
                                         in0=ps_k[half][:], scalar1=0.0, scalar2=None,
                                         op0=ALU.min).then_inc(s_dmin, 1)
                for half in range(2):
                    hof = half * 512
                    vector.wait_ge(s_exp, 2 * t + half + 1)
                    vector.scalar_tensor_tensor(
                        out=arena[:, hof:hof + 512], in0=ps_k[half][:], scalar=0.0,
                        in1=arena[:, 1024 + hof:1024 + hof + 512],
                        op0=ALU.max, op1=ALU.add).then_inc(s_kfree, 1)
                    vector.tensor_tensor(out=arena[:, 1024 + hof:1024 + hof + 512],
                                         in0=arena[:, hof:hof + 512],
                                         in1=arena[:, hof:hof + 512], op=ALU.mult)
                    vector.reduce_sum(
                        out=rst[:, half * 8:(half + 1) * 8],
                        in_=arena[:, 1024 + hof:1024 + hof + 512].rearrange("p (h w) -> p h w", w=64),
                        axis=mybir.AxisListType.X).then_inc(s_eps, 1)
                if t >= 1:
                    vector.wait_ge(s_S, t)
                    vector.tensor_tensor(out=S_acc[0:64, 0:512], in0=S_acc[0:64, 0:512],
                                         in1=ps_S[0][0:64, :], op=ALU.add).then_inc(s_sacc, 1)
                    vector.tensor_tensor(out=S_acc[0:64, 512:1024], in0=S_acc[0:64, 512:1024],
                                         in1=ps_S[1][0:64, :], op=ALU.add).then_inc(s_sacc, 1)
                for half in range(2):
                    hof = half * 512
                    vector.wait_ge(s_exp2, 2 * t + half + 1)
                    for h in range(8):
                        gh = half * 8 + h
                        ins = vector.tensor_scalar(
                            out=kbf[:, gh * 64:(gh + 1) * 64],
                            in0=arena[:, hof + h * 64:hof + (h + 1) * 64],
                            scalar1=rst[:, gh:gh + 1], scalar2=None, op0=ALU.mult)
                    ins.then_inc(s_kbf, 1)
            vector.wait_ge(s_S, NSUB)
            vector.tensor_tensor(out=S_acc[0:64, 0:512], in0=S_acc[0:64, 0:512],
                                 in1=ps_S[0][0:64, :], op=ALU.add).then_inc(s_sacc, 1)
            vector.tensor_tensor(out=S_acc[0:64, 512:1024], in0=S_acc[0:64, 512:1024],
                                 in1=ps_S[1][0:64, :], op=ALU.add).then_inc(s_sacc, 1)
            vector.wait_ge(s_dma, 32)
            vector.memset(Sblk[:], 0.0).then_inc(s_sev, 1)

            if not phase_b:
                return
            for tt in range(NTB):
                xof = 0
                if tt >= 1:
                    vector.wait_ge(s_qmm, NCH * tt)
                vector.tensor_copy(
                    out=xbf[:, 0:2048].rearrange("p (c n) -> p c n", n=TB),
                    in_=X3(tt * TB, TB)).then_inc(s_cast, 1)
                for j in range(NCH):
                    k = NCH * tt + j
                    pp = j % 2
                    vector.wait_ge(s_qmm, k + 1)
                    if k >= 2:
                        vector.wait_ge(s_exq, k - 1)
                    vector.tensor_scalar(out=arena[:, pp * TB:(pp + 1) * TB],
                                         in0=ps_k[pp][:, 0:TB], scalar1=bqc[:, j:j + 1],
                                         scalar2=0.0, op0=ALU.add,
                                         op1=ALU.min).then_inc(s_q1, 1)
                    vector.tensor_scalar(out=arena[:, 1024:1024 + TB],
                                         in0=ps_k[pp][:, 0:TB], scalar1=bqc[:, j:j + 1],
                                         scalar2=0.0, op0=ALU.add,
                                         op1=ALU.max).then_inc(s_qfree, 1)
                    vector.wait_ge(s_exq, k + 1)
                    if tt >= 1:
                        vector.wait_ge(s_omm, NCH * tt)
                    vector.tensor_tensor(out=qa[:, j * TB:(j + 1) * TB],
                                         in0=arena[:, 1024:1024 + TB],
                                         in1=arena[:, 512 + pp * TB:512 + (pp + 1) * TB],
                                         op=ALU.add).then_inc(s_qa, 1)
                if tt >= 1:
                    vector.wait_ge(s_sq, 16 * tt)
                    vector.wait_ge(s_statm, 16 * tt)
                for o in range(NCH):
                    k = NCH * tt + o
                    vector.wait_ge(s_omm, k + 1)
                    vector.scalar_tensor_tensor(
                        out=y_sb[:, o * TB:(o + 1) * TB], in0=ps_S[o % 2][:, 0:TB],
                        scalar=boc[:, o:o + 1], in1=Xtile(o, tt * TB, TB),
                        op0=ALU.add, op1=ALU.add).then_inc(s_yev, 1)
                for rnd in range(2):
                    vector.wait_ge(s_statm, 16 * tt + 8 * rnd + 8)
                    vector.tensor_scalar(out=rows[0:1, 0:TB], in0=ps_stat[0:1, 0:TB],
                                         scalar1=1.0 / D, scalar2=None, op0=ALU.mult)
                    vector.tensor_scalar(out=rows[0:1, TB:2 * TB], in0=ps_stat[0:1, TB:2 * TB],
                                         scalar1=1.0 / D, scalar2=None,
                                         op0=ALU.mult).then_inc(s_rows, 1)
                    vector.tensor_tensor(out=rows[0:1, 2 * TB:3 * TB], in0=rows[0:1, 0:TB],
                                         in1=rows[0:1, 0:TB], op=ALU.mult)
                    vector.tensor_tensor(out=rows[0:1, 2 * TB:3 * TB], in0=rows[0:1, TB:2 * TB],
                                         in1=rows[0:1, 2 * TB:3 * TB],
                                         op=ALU.subtract).then_inc(s_eps2, 1)
                    vector.wait_ge(s_exp2, 2 * NSUB + 2 * tt + rnd + 1)
                    vector.tensor_tensor(out=rows[0:1, 3 * TB:4 * TB], in0=rows[0:1, 0:TB],
                                         in1=rows[0:1, 2 * TB:3 * TB], op=ALU.mult)
                    vector.tensor_scalar(out=rows[0:1, 3 * TB:4 * TB],
                                         in0=rows[0:1, 3 * TB:4 * TB], scalar1=-1.0,
                                         scalar2=None, op0=ALU.mult).then_inc(s_arow, 1)
                    vector.wait_ge(s_bc, 2 * tt + rnd + 1)
                    for c in range(NCH):
                        if rnd == 0:
                            vector.tensor_tensor(out=arena[:, 1280:1280 + TB],
                                                 in0=y_sb[:, c * TB:(c + 1) * TB],
                                                 in1=ps_bc[:, 0:TB], op=ALU.mult)
                            vector.tensor_scalar(out=arena[:, 1536:1536 + TB],
                                                 in0=ps_bc[:, TB:2 * TB],
                                                 scalar1=gmc[:, c:c + 1], scalar2=btc[:, c:c + 1],
                                                 op0=ALU.mult, op1=ALU.add)
                            vector.scalar_tensor_tensor(
                                out=arena[:, 1280:1280 + TB], in0=arena[:, 1280:1280 + TB],
                                scalar=gmc[:, c:c + 1], in1=arena[:, 1536:1536 + TB],
                                op0=ALU.mult, op1=ALU.add)
                            ins = vector.tensor_tensor(
                                out=y_sb[:, c * TB:(c + 1) * TB],
                                in0=arena[:, 1280:1280 + TB],
                                in1=Xtile(c, tt * TB, TB), op=ALU.add)
                            ins.then_inc(s_ap1, 1)
                        else:
                            vector.tensor_tensor(out=arena[:, 1280:1280 + TB],
                                                 in0=y_sb[:, c * TB:(c + 1) * TB],
                                                 in1=ps_bc[:, 0:TB], op=ALU.mult)
                            ins = vector.tensor_tensor(
                                out=Xtile(c, tt * TB, TB),
                                in0=arena[:, 1280:1280 + TB],
                                in1=ps_bc[:, TB:2 * TB], op=ALU.add)
                            ins.then_inc(s_ap2, 1)

        @block.scalar
        def _(scalar):
            for t in range(NSUB):
                for half in range(2):
                    hof = half * 512
                    scalar.wait_ge(s_dmin, 2 * t + half + 1)
                    scalar.activation(out=arena[:, 1024 + hof:1024 + hof + 512],
                                      in_=arena[:, hof:hof + 512],
                                      func=AF.Exp).then_inc(s_exp, 1)
                for half in range(2):
                    scalar.wait_ge(s_kv, 4 * t + 2 + half + 1)
                    if t >= 1:
                        scalar.wait_ge(s_S, t)
                    scalar.copy(out=vbf[:, half * 512:(half + 1) * 512],
                                in_=ps_v[half][:]).then_inc(s_vfree, 1)
                for half in range(2):
                    scalar.wait_ge(s_eps, 2 * t + half + 1)
                    scalar.activation(out=rst[:, half * 8:(half + 1) * 8],
                                      in_=rst[:, half * 8:(half + 1) * 8],
                                      func=AF.Ln, bias=cstc[:, 0:1])
                    scalar.activation(out=rst[:, half * 8:(half + 1) * 8],
                                      in_=rst[:, half * 8:(half + 1) * 8],
                                      func=AF.Exp, scale=cstc[:, 2:3]).then_inc(s_exp2, 1)
            if not phase_b:
                return
            for tt in range(NTB):
                for j in range(NCH):
                    k = NCH * tt + j
                    scalar.wait_ge(s_q1, k + 1)
                    if k >= 1:
                        scalar.wait_ge(s_qa, k)
                    scalar.activation(out=arena[:, 512 + (j % 2) * TB:512 + (j % 2 + 1) * TB],
                                      in_=arena[:, (j % 2) * TB:(j % 2 + 1) * TB],
                                      func=AF.Exp).then_inc(s_exq, 1)
                for p in range(NCH):
                    k = NCH * tt + p
                    scalar.wait_ge(s_atmm, k + 1)
                    scalar.copy(out=qa[:, p * TB:(p + 1) * TB],
                                in_=ps_v[p % 2][:, 0:TB]).then_inc(s_atev, 1)
                for rnd in range(2):
                    ready = s_yev if rnd == 0 else s_ap1
                    for c in range(NCH):
                        scalar.wait_ge(ready, NCH * tt + c + 1)
                        kk = 16 * tt + 8 * rnd + c
                        if kk >= 2:
                            scalar.wait_ge(s_statm, kk - 1)
                        scalar.activation(out=y2[:, (c % 2) * TB:(c % 2 + 1) * TB],
                                          in_=y_sb[:, c * TB:(c + 1) * TB],
                                          func=AF.Square).then_inc(s_sq, 1)
                    scalar.wait_ge(s_eps2, 2 * tt + rnd + 1)
                    scalar.activation(out=rows[0:1, 2 * TB:3 * TB], in_=rows[0:1, 2 * TB:3 * TB],
                                      func=AF.Ln, bias=(cstc[0:1, 0:1] if rnd == 0 else cstc[0:1, 1:2]))
                    scalar.activation(out=rows[0:1, 2 * TB:3 * TB], in_=rows[0:1, 2 * TB:3 * TB],
                                      func=AF.Exp, scale=cstc[0:1, 2:3]).then_inc(s_exp2, 1)


def build_l0_attn(debug=False, nsub=32, phase_b=True):
    nc = bass.Bass("TRN2", target_bir_lowering=False, debug=False, detect_race_conditions=DETECT_RACES)
    xin = nc.dram_tensor("xin", [D, NT], F32, kind="ExternalInput").ap()
    xout = nc.dram_tensor("xout", [D, NT], F32, kind="ExternalOutput").ap()
    es_s = ExitStack()
    with es_s:
        emit_l0_attn(nc, es_s, "", xin, xout, debug=debug, nsub=nsub, phase_b=phase_b)
    return nc


def _l0_attn_inputs(inputs, core):
    b, h = core // 2, core % 2
    x = inputs["L0_attn_x"]
    xc = np.ascontiguousarray(x[b, h * NT:(h + 1) * NT, :].T).astype(np.float32)
    Wq, Wk, Wv, Wo = (inputs["L0_attn_Wq"], inputs["L0_attn_Wk"],
                      inputs["L0_attn_Wv"], inputs["L0_attn_Wo"])
    d = {
        "xin": xc,
        "wk_r": np.ascontiguousarray(Wk.reshape(NCH, 128, D)),
        "wv_r": np.ascontiguousarray(Wv.reshape(NCH, 128, D)),
        "wq_l": np.ascontiguousarray(
            Wq.reshape(NCH, 128, NCH, 128).transpose(1, 2, 0, 3).reshape(128, 8192)),
        "wo_l": np.ascontiguousarray(
            Wo.reshape(NCH, 128, NCH, 128).transpose(1, 2, 0, 3).reshape(128, 8192)),
        "bk_row": inputs["L0_attn_bk"].reshape(1, D),
        "bv_row": inputs["L0_attn_bv"].reshape(1, D),
        "bq_col": np.ascontiguousarray(inputs["L0_attn_bq"].reshape(NCH, 128).T),
        "bo_col": np.ascontiguousarray(inputs["L0_attn_bo"].reshape(NCH, 128).T),
        "gam_col": np.ascontiguousarray(inputs["L0_attn_gamma"].reshape(NCH, 128).T),
        "bet_col": np.ascontiguousarray(inputs["L0_attn_beta"].reshape(NCH, 128).T),
        "ones1": np.ones((1, 128), np.float32),
        "onesc": np.ones((128, 1), np.float32),
        "cst": np.tile(np.array([[EPS_IN, EPS_OUT, -0.5, 0.5]], np.float32), (128, 1)),
    }
    return {k: np.ascontiguousarray(np.asarray(v, dtype=np.float32)) for k, v in d.items()}


def kernel(**inputs):
    inputs = {k: np.asarray(v) for k, v in inputs.items()}
    cores = list(range(8))
    nc = build_fused()
    maps = []
    for c in cores:
        m = {}
        for pfx, fn in [("a0", _l0_attn_inputs), ("f0", _l0_ffn_inputs),
                        ("a1", _l1_attn_inputs), ("f1", _l1_ffn_inputs)]:
            for k, v in fn(inputs, c).items():
                m[pfx + k] = v
        m["xin"] = m.pop("a0xin")
        maps.append(m)
    res = run_bass_kernel_spmd(nc, maps, core_ids=cores)
    out = np.zeros((4, 8192, D), np.float32)
    for c in cores:
        b, h = c // 2, c % 2
        out[b, h * NT:(h + 1) * NT, :] = np.asarray(res.results[c]["xout"]).T
    return out


def kernel_unfused(**inputs):
    inputs = {k: np.asarray(v) for k, v in inputs.items()}
    cores = list(range(8))

    nc1 = build_l0_attn()
    maps = [_l0_attn_inputs(inputs, c) for c in cores]
    res = run_bass_kernel_spmd(nc1, maps, core_ids=cores)
    x = [np.asarray(res.results[c]["xout"]) for c in cores]

    nc2 = build_l0_ffn()
    maps = [_l0_ffn_inputs(inputs, c) for c in cores]
    for c in cores:
        maps[c]["xin"] = np.ascontiguousarray(x[c])
    res = run_bass_kernel_spmd(nc2, maps, core_ids=cores)
    x = [np.asarray(res.results[c]["xout"]) for c in cores]

    nc3 = build_l1_attn()
    maps = [_l1_attn_inputs(inputs, c) for c in cores]
    for c in cores:
        maps[c]["xin"] = np.ascontiguousarray(x[c])
    res = run_bass_kernel_spmd(nc3, maps, core_ids=cores)
    x = [np.asarray(res.results[c]["xout"]) for c in cores]

    nc4 = build_l1_ffn()
    maps = [_l1_ffn_inputs(inputs, c) for c in cores]
    for c in cores:
        maps[c]["xin"] = np.ascontiguousarray(x[c])
    res = run_bass_kernel_spmd(nc4, maps, core_ids=cores)
    x = [np.asarray(res.results[c]["xout"]) for c in cores]

    out = np.zeros((4, 8192, D), np.float32)
    for c in cores:
        b, h = c // 2, c % 2
        out[b, h * NT:(h + 1) * NT, :] = x[c].T
    return out




def emit_l0_ffn(nc, es_s, pfx, xin, xout, ne_used=6, debug=False, wbf=None, s_pc=None, pc_need=0, xbin=None, s_bin=None):
    NE, NG, NJ = ne_used, 4, 32
    TT_ = 512
    NTT = NT // TT_
    LTB = 256
    NLT = NT // LTB
    NROUT = max(NE - 2, 0)

    if wbf is None:
        w1_l = nc.dram_tensor(pfx + "w1_l", [6, 128, NJ * NCH * 128], F32, kind="ExternalInput").ap()
        w2_l = nc.dram_tensor(pfx + "w2_l", [6, 128, NJ * NCH * 128], F32, kind="ExternalInput").ap()
    b1_c = nc.dram_tensor(pfx + "b1_c", [6, 128, NJ], F32, kind="ExternalInput").ap()
    b2_c = nc.dram_tensor(pfx + "b2_c", [6, 128, NCH], F32, kind="ExternalInput").ap()
    gw_r = nc.dram_tensor(pfx + "gw_r", [NCH, 128, NG], F32, kind="ExternalInput").ap()
    gb_row = nc.dram_tensor(pfx + "gb_row", [128, NG], F32, kind="ExternalInput").ap()
    ident = nc.dram_tensor(pfx + "ident", [128, 128], F32, kind="ExternalInput").ap()
    ones1 = nc.dram_tensor(pfx + "ones1", [1, 128], F32, kind="ExternalInput").ap()
    onesc = nc.dram_tensor(pfx + "onesc", [128, 1], F32, kind="ExternalInput").ap()
    cst = nc.dram_tensor(pfx + "cst", [128, 4], F32, kind="ExternalInput").ap()
    acc = nc.dram_tensor(pfx + "acc", [D, NT], F32)
    if xbin is None:
        xbf_d = nc.dram_tensor(pfx + "xbf_d", [D, NT], BF16)
        xsrc = xbf_d.ap()
    else:
        xsrc = xbin
    coef_d = nc.dram_tensor(pfx + "coef_d", [128, 32 * NG], F32)
    if debug:
        d_acc = nc.dram_tensor("d_acc", [D, NT], F32, kind="ExternalOutput").ap()
        d_coef = nc.dram_tensor("d_coef", [128, 128], F32, kind="ExternalOutput").ap()
        d_gts = nc.dram_tensor("d_gts", [128, 128], F32, kind="ExternalOutput").ap()
        d_ctm = nc.dram_tensor("d_ctm", [128, 128], F32, kind="ExternalOutput").ap()

    es = ExitStack()
    sb = lambda n, s, d=F32: es.enter_context(nc.sbuf_tensor(pfx + n, s, d))
    ps = lambda n, s, d=F32: es.enter_context(nc.psum_tensor(pfx + n, s, d))
    sem = lambda n: es_s.enter_context(nc.semaphore(pfx + n))

    W1 = sb("W1", [128, NJ * NCH * 128], BF16)
    W2 = sb("W2", [128, NJ * NCH * 128], BF16)
    xbt = sb("xbt", [128, 2 * NCH * TT_], BF16)
    h1 = sb("h1", [128, NJ * TT_], BF16)
    stage = sb("stage", [128, 2048])
    yln = sb("yln", [128, NCH * LTB])
    gx = yln[:, 0:1024]
    y2 = sb("y2", [128, 2 * LTB])
    rows = sb("rows", [1, 4 * LTB])
    gts = sb("gts", [128, 32 * NG])
    gsc = sb("gsc", [128, 4 * 32])
    ctm = sb("ctm", [128, 32 * NG])
    coef_sb = sb("coef_sb", [128, TT_])
    crow = sb("crow", [1, TT_])
    b1a = sb("b1a", [128, 6 * NJ])
    b2a = sb("b2a", [128, 6 * NCH])
    gwr = sb("gwr", [128, NCH * NG], BF16)
    gbr = sb("gbr", [128, NG])
    id_sb = sb("id_sb", [128, 128])
    on1 = sb("on1", [1, 128]); onc = sb("onc", [128, 1]); cstc = sb("cstc", [128, 4])

    ps_h = [ps(f"ps_h{i}", [128, 512]) for i in range(2)]
    ps_o = [ps(f"ps_o{i}", [128, 512]) for i in range(4)]
    ps_c = ps("ps_c", [128, 512])
    ps_s = ps("ps_s", [128, 512])

    s_mi = sem("s_mi"); s_xb = sem("s_xb"); s_a0 = sem("s_a0"); s_gw = sem("s_gw")
    s_gxa = sem("s_gxa"); s_w2 = sem("s_w2")
    s_gx = sem("s_gx"); s_gmm = sem("s_gmm"); s_gev = sem("s_gev")
    s_gsub = sem("s_gsub"); s_gexp = sem("s_gexp"); s_gcoef = sem("s_gcoef")
    s_gtr = sem("s_gtr"); s_gtev = sem("s_gtev"); s_gd = sem("s_gd")
    s_w = sem("s_w"); s_xt = sem("s_xt"); s_hmm = sem("s_hmm"); s_hev = sem("s_hev")
    s_omm = sem("s_omm"); s_cd = sem("s_cd"); s_cmm = sem("s_cmm"); s_cev = sem("s_cev")
    s_sev = sem("s_sev"); s_ad = sem("s_ad")
    s_lnl = sem("s_lnl"); s_sq = sem("s_sq"); s_stm = sem("s_stm"); s_rw = sem("s_rw")
    s_var = sem("s_var"); s_ar = sem("s_ar"); s_ab = sem("s_ab"); s_bc = sem("s_bc")
    s_ap = sem("s_ap"); s_out = sem("s_out")

    N_ACC = NE * NTT * 2 * 4 * 16

    with es, nc.Block() as block:
        @block.sync
        def _(sync):
            sync.dma_start(out=b1a[:].rearrange("p (e n) -> p e n", n=NJ),
                           in_=b1_c.rearrange("e p n -> p e n")).then_inc(s_mi, 16)
            sync.dma_start(out=b2a[:].rearrange("p (e n) -> p e n", n=NCH),
                           in_=b2_c.rearrange("e p n -> p e n")).then_inc(s_mi, 16)
            sync.dma_start(out=gbr[:], in_=gb_row).then_inc(s_mi, 16)
            sync.dma_start(out=id_sb[:], in_=ident).then_inc(s_mi, 16)
            sync.dma_start(out=on1[:], in_=ones1).then_inc(s_mi, 16)
            sync.dma_start(out=onc[:], in_=onesc).then_inc(s_mi, 16)
            sync.dma_start(out=cstc[:], in_=cst).then_inc(s_mi, 16)
            sync.dma_start(out=acc.rearrange("(c p) n -> p c n", p=128),
                           in_=xin.rearrange("(c p) n -> p c n", p=128)).then_inc(s_a0, 16)
            sync.wait_ge(s_gtev, 1)
            sync.dma_start(out=coef_d.ap(), in_=ctm[:]).then_inc(s_gd, 16)

            def coef_rows(r):
                for t in range(NTT):
                    kc = r * NTT + t
                    sync.wait_ge(s_gd, 16)
                    if kc >= 1:
                        sync.wait_ge(s_cmm, kc)
                    sync.dma_start(
                        out=crow[:],
                        in_=coef_d.ap().rearrange("a b -> (a b)")[None, :]
                            .rearrange("o (g e q) -> o g e q", e=NG, q=128)[:, 4 * t:4 * (t + 1), r]
                    ).then_inc(s_cd, 16)

            if wbf is None:
                for r in range(NROUT):
                    coef_rows(r)
            else:
                w1b, w2b = wbf
                for e in range(NE):
                    if e == 0:
                        sync.wait_ge(s_pc, pc_need)
                    else:
                        sync.wait_ge(s_hmm, NJ * NTT * e)
                    sync.dma_start(out=W1[:], in_=w1b[e]).then_inc(s_w, 16)
                    if e >= 1:
                        sync.wait_ge(s_omm, 2 * NTT * e)
                    sync.dma_start(out=W2[:], in_=w2b[e]).then_inc(s_w, 16)
                    if e >= 2:
                        coef_rows(e - 2)
            for tt in range(NLT):
                sync.wait_ge(s_ad, N_ACC)
                if tt >= 1:
                    sync.wait_ge(s_ap, NCH * tt)
                    sync.wait_ge(s_out, 16 * tt)
                sync.dma_start(
                    out=yln[:].rearrange("p (c n) -> p c n", n=LTB),
                    in_=acc.rearrange("(c p) n -> p c n", p=128)[:, :, tt * LTB:(tt + 1) * LTB]
                ).then_inc(s_lnl, 16)
                sync.wait_ge(s_ap, NCH * (tt + 1))
                sync.dma_start(
                    out=xout.rearrange("(c p) n -> p c n", p=128)[:, :, tt * LTB:(tt + 1) * LTB],
                    in_=yln[:].rearrange("p (c n) -> p c n", n=LTB)
                ).then_inc(s_out, 16)
            sync.wait_ge(s_out, 16 * NLT)
            if debug:
                sync.dma_start(out=d_acc, in_=acc.ap()).then_inc(s_out, 16)
                sync.dma_start(out=d_coef, in_=coef_d.ap()[:, 0:128]).then_inc(s_out, 16)
                sync.wait_ge(s_gcoef, 1)
                sync.dma_start(out=d_gts, in_=gts[:]).then_inc(s_out, 16)
                sync.dma_start(out=d_ctm, in_=ctm[:]).then_inc(s_out, 16)
                sync.wait_ge(s_out, 16 * NLT + 64)

        @block.gpsimd
        def _(gpsimd):
            gpsimd.dma_start(out=gwr[:].rearrange("p (c n) -> p c n", n=NG),
                             in_=gw_r.rearrange("c p n -> p c n")).then_inc(s_gw, 16)
            gpsimd.dma_start(out=W1[:], in_=w1_l[0]).then_inc(s_w, 16)
            if xbin is not None:
                gpsimd.wait_ge(s_bin, 16)
                gpsimd.dma_start(out=W2[:].rearrange("p (c n) -> p c n", n=NT),
                                 in_=xsrc.rearrange("(c p) n -> p c n", p=128)).then_inc(s_gxa, 16 * NCH)
            else:
                for cc in range(NCH):
                    gpsimd.dma_start(
                        out=W2[:].rearrange("p (c n) -> p c n", n=NT)[:, cc:cc + 1, :],
                        in_=xin.rearrange("(c p) n -> p c n", p=128)[:, cc:cc + 1, :]
                    ).then_inc(s_gxa, 16)
                for sc in range(NTT):
                    gpsimd.dma_start(out=xbf_d.ap()[:, sc * TT_:(sc + 1) * TT_],
                                     in_=xin[:, sc * TT_:(sc + 1) * TT_]).then_inc(s_xb, 16)

            def xload(k2):
                if k2 >= NE * NTT:
                    return
                t2 = k2 % NTT
                if xbin is None:
                    gpsimd.wait_ge(s_xb, 16 * (t2 + 1))
                gpsimd.dma_start(
                    out=xbt[:, (t2 % 2) * NCH * TT_:((t2 % 2) + 1) * NCH * TT_]
                        .rearrange("p (c n) -> p c n", n=TT_),
                    in_=xsrc.rearrange("(c p) n -> p c n", p=128)[:, :, t2 * TT_:(t2 + 1) * TT_]
                ).then_inc(s_xt, 16)

            xload(0)
            xload(1)
            for e in range(NE):
                if e >= 1:
                    gpsimd.wait_ge(s_hmm, NJ * NTT * e)
                    gpsimd.dma_start(out=W1[:], in_=w1_l[e]).then_inc(s_w, 16)
                    gpsimd.wait_ge(s_omm, 2 * NTT * e)
                    gpsimd.dma_start(out=W2[:], in_=w2_l[e]).then_inc(s_w2, 16)
                for t in range(NTT):
                    k = e * NTT + t
                    if k == 0:
                        gpsimd.wait_ge(s_gmm, 1)
                        gpsimd.dma_start(out=W2[:], in_=w2_l[0]).then_inc(s_w2, 16)
                        gpsimd.wait_ge(s_a0, 16)
                    gpsimd.wait_ge(s_hmm, NJ * (k + 1))
                    xload(k + 2)
                    for og in range(2):
                        kk = k * 2 + og
                        for oo in range(4):
                            gpsimd.wait_ge(s_sev, 4 * kk + oo + 1)
                            gpsimd.dma_start(
                                out=acc.rearrange("(c p) n -> p c n", p=128)[:, og * 4 + oo:og * 4 + oo + 1, t * TT_:(t + 1) * TT_],
                                in_=stage[:, oo * 512:(oo + 1) * 512].rearrange("p (c n) -> p c n", n=TT_),
                                accum_op=ALU.add).then_inc(s_ad, 16)

        @block.tensor
        def _(tensor):
            tensor.wait_ge(s_mi, 7 * 16)
            tensor.wait_ge(s_gw, 16)
            tensor.wait_ge(s_gxa, 16 * NCH)
            for g in range(32):
                for cd in range(NCH):
                    ins = tensor.matmul(ps_s[0:128, g * NG:(g + 1) * NG],
                                        W2[:, cd * NT + g * 128:cd * NT + (g + 1) * 128],
                                        gwr[:, cd * NG:(cd + 1) * NG],
                                        start=(cd == 0), stop=(cd == NCH - 1))
            ins.then_inc(s_gmm, 1)
            tensor.wait_ge(s_gcoef, 1)
            tensor.transpose(ps_c[:, 0:128], ctm[:, 0:128], id_sb[:]).then_inc(s_gtr, 1)
            for e in range(NE):
                for t in range(NTT):
                    k = e * NTT + t
                    par = t % 2
                    xof = par * NCH * TT_
                    tensor.wait_ge(s_w, 16 * (e + 1))
                    tensor.wait_ge(s_xt, 16 * (k + 1))
                    if e >= 2:
                        kc = (e - 2) * NTT + t
                        tensor.wait_ge(s_cd, 16 * (kc + 1))
                        if kc >= 1:
                            tensor.wait_ge(s_cev, kc)
                        else:
                            tensor.wait_ge(s_gtev, 1)
                        tensor.matmul(ps_c[:, 0:TT_], on1[0:1, :], crow[0:1, :],
                                      start=True, stop=True).then_inc(s_cmm, 1)
                    for j in range(NJ):
                        kj = k * NJ + j
                        if kj >= 2:
                            tensor.wait_ge(s_hev, kj - 1)
                        for cd in range(NCH):
                            ins = tensor.matmul(
                                ps_h[j % 2][:],
                                W1[:, (j * NCH + cd) * 128:(j * NCH + cd + 1) * 128],
                                xbt[:, xof + cd * TT_: xof + (cd + 1) * TT_],
                                start=(cd == 0), stop=(cd == NCH - 1))
                        ins.then_inc(s_hmm, 1)
                    if t == 0:
                        tensor.wait_ge(s_w2, 16 * (e + 1))
                    for og in range(2):
                        kk = k * 2 + og
                        if kk >= 1:
                            tensor.wait_ge(s_sev, 4 * kk)
                        for j in range(NJ):
                            tensor.wait_ge(s_hev, k * NJ + j + 1)
                            for oo in range(4):
                                o = og * 4 + oo
                                ins = tensor.matmul(
                                    ps_o[oo][:],
                                    W2[:, (j * NCH + o) * 128:(j * NCH + o + 1) * 128],
                                    h1[:, j * TT_:(j + 1) * TT_],
                                    start=(j == 0), stop=(j == NJ - 1))
                        ins.then_inc(s_omm, 1)
            for tt in range(NLT):
                tensor.wait_ge(s_lnl, 16 * (tt + 1))
                if tt >= 1:
                    tensor.wait_ge(s_rw, tt)
                for c in range(NCH):
                    tensor.matmul(ps_s[0:1, 0:LTB], onc[:, 0:1],
                                  yln[:, c * LTB:(c + 1) * LTB],
                                  start=(c == 0), stop=(c == NCH - 1))
                for c in range(NCH):
                    tensor.wait_ge(s_sq, NCH * tt + c + 1)
                    tensor.matmul(ps_s[0:1, LTB:2 * LTB], onc[:, 0:1],
                                  y2[:, (c % 2) * LTB:(c % 2 + 1) * LTB],
                                  start=(c == 0), stop=(c == NCH - 1)).then_inc(s_stm, 1)
                tensor.wait_ge(s_ab, tt + 1)
                if tt == 0:
                    tensor.wait_ge(s_cev, NROUT * NTT)
                else:
                    tensor.wait_ge(s_ap, NCH * tt)
                tensor.matmul(ps_c[:, 0:LTB], on1[0:1, :], rows[0:1, 2 * LTB:3 * LTB],
                              start=True, stop=True)
                tensor.matmul(ps_c[:, LTB:2 * LTB], on1[0:1, :], rows[0:1, 3 * LTB:4 * LTB],
                              start=True, stop=True).then_inc(s_bc, 1)

        @block.vector
        def _(vector):
            vector.wait_ge(s_gmm, 1)
            vector.wait_ge(s_mi, 7 * 16)
            vector.tensor_copy(out=gts[:], in_=ps_s[0:128, 0:32 * NG]).then_inc(s_gev, 1)
            vector.drain()
            g3 = gts[:].rearrange("p (g e) -> p g e", e=NG)
            a_, b_ = bass.broadcast_tensor_aps(g3, gbr[:].rearrange("p (o e) -> p o e", o=1))
            vector.tensor_tensor(out=g3, in0=a_, in1=b_, op=ALU.add)
            vector.drain()
            rmax = gsc[:, 0:32]; rsum = gsc[:, 32:64]
            cnt = gsc[:, 64:96]; gtmp = gsc[:, 96:128]
            vector.reduce_max(out=rmax, in_=g3, axis=mybir.AxisListType.X)
            vector.drain()
            a_, b_ = bass.broadcast_tensor_aps(g3, rmax.rearrange("p (g o) -> p g o", o=1))
            vector.tensor_tensor(out=g3, in0=a_, in1=b_, op=ALU.subtract).then_inc(s_gsub, 1)
            vector.wait_ge(s_gexp, 1)
            vector.reduce_sum(out=rsum, in_=g3, axis=mybir.AxisListType.X)
            vector.drain()
            vector.reciprocal(out=rsum, in_=rsum)
            vector.drain()
            a_, b_ = bass.broadcast_tensor_aps(g3, rsum.rearrange("p (g o) -> p g o", o=1))
            vector.tensor_tensor(out=g3, in0=a_, in1=b_, op=ALU.mult)
            vector.drain()
            c3 = ctm[:].rearrange("p (g e) -> p g e", e=NG)
            for ei in range(NG):
                first = True
                for j in range(NG):
                    if j == ei:
                        continue
                    wj = g3[:, :, j:j + 1]
                    we = g3[:, :, ei:ei + 1]
                    if first:
                        vector.tensor_tensor(out=cnt.rearrange("p (g o) -> p g o", o=1),
                                             in0=wj, in1=we, op=ALU.is_gt)
                        first = False
                    else:
                        vector.tensor_tensor(out=gtmp.rearrange("p (g o) -> p g o", o=1),
                                             in0=wj, in1=we, op=ALU.is_gt)
                        vector.drain()
                        vector.tensor_tensor(out=cnt, in0=cnt, in1=gtmp, op=ALU.add)
                    vector.drain()
                vector.tensor_scalar(out=gtmp, in0=cnt, scalar1=1.5, scalar2=None,
                                     op0=ALU.is_le)
                vector.drain()
                ins = vector.tensor_tensor(out=c3[:, :, ei:ei + 1],
                                           in0=g3[:, :, ei:ei + 1],
                                           in1=gtmp.rearrange("p (g o) -> p g o", o=1),
                                           op=ALU.mult)
                vector.drain()
            ins.then_inc(s_gcoef, 1)
            vector.wait_ge(s_gtr, 1)
            vector.tensor_copy(out=ctm[:], in_=ps_c[:, 0:128]).then_inc(s_gtev, 1)
            for e in range(NE):
                for t in range(NTT):
                    k = e * NTT + t
                    if e >= 2:
                        kc = (e - 2) * NTT + t
                        vector.wait_ge(s_cmm, kc + 1)
                        vector.tensor_copy(out=coef_sb[:], in_=ps_c[:, 0:TT_]).then_inc(s_cev, 1)
                    for og in range(2):
                        kk = k * 2 + og
                        vector.wait_ge(s_omm, kk + 1)
                        for oo in range(4):
                            o = og * 4 + oo
                            if kk >= 1:
                                vector.wait_ge(s_ad, 16 * (4 * (kk - 1) + oo + 1))
                            if e < 2:
                                vector.tensor_scalar(
                                    out=stage[:, oo * 512:(oo + 1) * 512],
                                    in0=ps_o[oo][:], scalar1=b2a[:, e * NCH + o:e * NCH + o + 1],
                                    scalar2=cstc[:, 3:4], op0=ALU.add,
                                    op1=ALU.mult).then_inc(s_sev, 1)
                            else:
                                vector.scalar_tensor_tensor(
                                    out=stage[:, oo * 512:(oo + 1) * 512], in0=ps_o[oo][:],
                                    scalar=b2a[:, e * NCH + o:e * NCH + o + 1], in1=coef_sb[:],
                                    op0=ALU.add, op1=ALU.mult).then_inc(s_sev, 1)
            for tt in range(NLT):
                vector.wait_ge(s_stm, NCH * (tt + 1))
                vector.tensor_scalar(out=rows[0:1, 0:LTB], in0=ps_s[0:1, 0:LTB],
                                     scalar1=1.0 / D, scalar2=None, op0=ALU.mult)
                vector.tensor_scalar(out=rows[0:1, LTB:2 * LTB], in0=ps_s[0:1, LTB:2 * LTB],
                                     scalar1=1.0 / D, scalar2=None, op0=ALU.mult).then_inc(s_rw, 1)
                vector.tensor_tensor(out=rows[0:1, 2 * LTB:3 * LTB], in0=rows[0:1, 0:LTB],
                                     in1=rows[0:1, 0:LTB], op=ALU.mult)
                vector.tensor_tensor(out=rows[0:1, 2 * LTB:3 * LTB], in0=rows[0:1, LTB:2 * LTB],
                                     in1=rows[0:1, 2 * LTB:3 * LTB],
                                     op=ALU.subtract).then_inc(s_var, 1)
                vector.wait_ge(s_ar, tt + 1)
                vector.tensor_tensor(out=rows[0:1, 3 * LTB:4 * LTB], in0=rows[0:1, 0:LTB],
                                     in1=rows[0:1, 2 * LTB:3 * LTB], op=ALU.mult)
                vector.tensor_scalar(out=rows[0:1, 3 * LTB:4 * LTB],
                                     in0=rows[0:1, 3 * LTB:4 * LTB], scalar1=-1.0,
                                     scalar2=None, op0=ALU.mult).then_inc(s_ab, 1)
                vector.wait_ge(s_bc, tt + 1)
                for c in range(NCH):
                    vector.tensor_tensor(out=stage[:, 0:LTB],
                                         in0=yln[:, c * LTB:(c + 1) * LTB],
                                         in1=ps_c[:, 0:LTB], op=ALU.mult)
                    vector.tensor_tensor(out=yln[:, c * LTB:(c + 1) * LTB],
                                         in0=stage[:, 0:LTB],
                                         in1=ps_c[:, LTB:2 * LTB], op=ALU.add).then_inc(s_ap, 1)

        @block.scalar
        def _(scalar):
            scalar.wait_ge(s_gsub, 1)
            scalar.activation(out=gts[:], in_=gts[:], func=AF.Exp).then_inc(s_gexp, 1)
            for e in range(NE):
                for t in range(NTT):
                    k = e * NTT + t
                    for j in range(NJ):
                        kj = k * NJ + j
                        scalar.wait_ge(s_hmm, kj + 1)
                        if k >= 1 and j == 0:
                            scalar.wait_ge(s_omm, 2 * k)
                        scalar.activation(out=h1[:, j * TT_:(j + 1) * TT_],
                                          in_=ps_h[j % 2][:], func=AF.Relu,
                                          bias=b1a[:, e * NJ + j:e * NJ + j + 1]).then_inc(s_hev, 1)
            for tt in range(NLT):
                for c in range(NCH):
                    scalar.wait_ge(s_lnl, 16 * (tt + 1))
                    kk = NCH * tt + c
                    if kk >= 2:
                        scalar.wait_ge(s_stm, kk - 1)
                    scalar.activation(out=y2[:, (c % 2) * LTB:(c % 2 + 1) * LTB],
                                      in_=yln[:, c * LTB:(c + 1) * LTB],
                                      func=AF.Square).then_inc(s_sq, 1)
                scalar.wait_ge(s_var, tt + 1)
                scalar.activation(out=rows[0:1, 2 * LTB:3 * LTB], in_=rows[0:1, 2 * LTB:3 * LTB],
                                  func=AF.Ln, bias=cstc[0:1, 1:2])
                scalar.activation(out=rows[0:1, 2 * LTB:3 * LTB], in_=rows[0:1, 2 * LTB:3 * LTB],
                                  func=AF.Exp, scale=cstc[0:1, 2:3]).then_inc(s_ar, 1)


def build_l0_ffn(ne_used=6, debug=False):
    nc = bass.Bass("TRN2", target_bir_lowering=False, debug=False, detect_race_conditions=DETECT_RACES)
    xin = nc.dram_tensor("xin", [D, NT], F32, kind="ExternalInput").ap()
    xout = nc.dram_tensor("xout", [D, NT], F32, kind="ExternalOutput").ap()
    es_s = ExitStack()
    with es_s:
        emit_l0_ffn(nc, es_s, "", xin, xout, ne_used=ne_used, debug=debug)
    return nc


def _l0_ffn_inputs(inputs, core):
    b, h = core // 2, core % 2
    sw1, sw2 = inputs["L0_ffn_sw1"], inputs["L0_ffn_sw2"]
    rw1, rw2 = inputs["L0_ffn_rw1"], inputs["L0_ffn_rw2"]
    sb1, sb2 = inputs["L0_ffn_sb1"], inputs["L0_ffn_sb2"]
    rb1, rb2 = inputs["L0_ffn_rb1"], inputs["L0_ffn_rb2"]
    w1 = np.concatenate([sw1, rw1], 0)
    w2 = np.concatenate([sw2, rw2], 0)
    b1 = np.concatenate([sb1, rb1], 0)
    b2 = np.concatenate([sb2, rb2], 0)
    NJ = 32
    d = {
        "w1_l": w1.reshape(6, NCH, 128, NJ, 128).transpose(0, 2, 3, 1, 4).reshape(6, 128, NJ * NCH * 128),
        "w2_l": w2.reshape(6, NJ, 128, NCH, 128).transpose(0, 2, 1, 3, 4).reshape(6, 128, NJ * NCH * 128),
        "b1_c": b1.reshape(6, NJ, 128).transpose(0, 2, 1),
        "b2_c": b2.reshape(6, NCH, 128).transpose(0, 2, 1),
        "gw_r": inputs["L0_ffn_gate_w"].reshape(NCH, 128, 4),
        "gb_row": np.tile(inputs["L0_ffn_gate_b"].reshape(1, 4), (128, 1)),
        "ident": np.eye(128, dtype=np.float32),
        "ones1": np.ones((1, 128), np.float32),
        "onesc": np.ones((128, 1), np.float32),
        "cst": np.tile(np.array([[EPS_IN, EPS_OUT, -0.5, 0.5]], np.float32), (128, 1)),
    }
    return {k: np.ascontiguousarray(np.asarray(v, dtype=np.float32)) for k, v in d.items()}


def emit_l1_attn(nc, es_s, pfx, xin, xout, debug=False, bout=None, s_bout=None):
    TT_ = 512
    NTT = NT // TT_
    LTB = 256
    NLT = NT // LTB

    wf_l = nc.dram_tensor(pfx + "wf_l", [128, 16 * NCH * 128], F32, kind="ExternalInput").ap()
    emask = nc.dram_tensor(pfx + "emask", [128, 1], F32, kind="ExternalInput").ap()
    omask = nc.dram_tensor(pfx + "omask", [128, 1], F32, kind="ExternalInput").ap()
    sbias = nc.dram_tensor(pfx + "sbias", [128, 1], F32, kind="ExternalInput").ap()
    ones1 = nc.dram_tensor(pfx + "ones1", [1, 128], F32, kind="ExternalInput").ap()
    onesc = nc.dram_tensor(pfx + "onesc", [128, 1], F32, kind="ExternalInput").ap()
    cst = nc.dram_tensor(pfx + "cst", [128, 4], F32, kind="ExternalInput").ap()
    xbf_d = nc.dram_tensor(pfx + "xbf_d", [D, NT], BF16)
    cc_i = [nc.dram_tensor(f"{pfx}cc_i{c}", [128, 1], F32) for c in range(NCH)]
    cc_o = [nc.dram_tensor(f"{pfx}cc_o{c}", [128, 1], F32) for c in range(NCH)]
    if debug:
        d_f = nc.dram_tensor("d_f", [128, NT], F32, kind="ExternalOutput").ap()
        d_u = nc.dram_tensor("d_u", [128, NT], F32, kind="ExternalOutput").ap()
        d_fr = nc.dram_tensor("d_fr", [128, NT], F32, kind="ExternalOutput").ap()
        d_ur = nc.dram_tensor("d_ur", [128, NT], F32, kind="ExternalOutput").ap()

    es = ExitStack()
    sb = lambda n, s, d=F32: es.enter_context(nc.sbuf_tensor(pfx + n, s, d))
    ps = lambda n, s, d=F32: es.enter_context(nc.psum_tensor(pfx + n, s, d))
    sem = lambda n: es_s.enter_context(nc.semaphore(pfx + n))

    X = sb("X", [128, NCH * NT])
    f_sl = sb("f_sl", [128, NT])
    u_sl = sb("u_sl", [128, NT])
    xbf = sb("xbf", [128, 2 * NCH * TT_], BF16)
    wf = sb("wf", [128, 4 * NCH * 128], BF16)
    t1 = sb("t1", [128, TT_])
    t2 = sb("t2", [128, TT_])
    y2 = sb("y2", [128, 2 * LTB])
    rows = sb("rows", [1, 4 * LTB])
    carry = sb("carry", [128, 1])
    sin = sb("sin", [128, 1])
    emk = sb("emk", [128, 1]); omk = sb("omk", [128, 1]); sbi = sb("sbi", [128, 1])
    on1 = sb("on1", [1, 128]); onc = sb("onc", [128, 1]); cstc = sb("cstc", [128, 4])

    ps_k = [ps(f"ps_k{i}", [128, 512]) for i in range(2)]
    ps_h = [ps(f"ps_h{i}", [128, 512]) for i in range(2)]
    ps_st = ps("ps_st", [128, 512])
    ps_bc = ps("ps_bc", [128, 512])

    s_x = sem("s_x"); s_mi = sem("s_mi"); s_wf = sem("s_wf"); s_xb = sem("s_xb")
    s_xt = sem("s_xt"); s_kmm = sem("s_kmm")
    s_sig = sem("s_sig"); s_kfree = sem("s_kfree"); s_hxfree = sem("s_hxfree")
    s_uev = sem("s_uev"); s_scan = sem("s_scan")
    s_cci = sem("s_cci"); s_cc = sem("s_cc"); s_cco = sem("s_cco"); s_fix = sem("s_fix")
    s_lsq = sem("s_lsq"); s_stm = sem("s_stm"); s_rw = sem("s_rw"); s_var = sem("s_var")
    s_ar = sem("s_ar"); s_ab = sem("s_ab"); s_bc = sem("s_bc"); s_ap = sem("s_ap")
    s_out = sem("s_out"); s_dbg = sem("s_dbg")

    with es, nc.Block() as block:
        @block.sync
        def _(sync):
            sync.dma_start(out=X[:].rearrange("p (c n) -> p c n", n=NT),
                           in_=xin.rearrange("(c p) n -> p c n", p=128)).then_inc(s_x, 16)
            for apd, t in [(emask, emk), (omask, omk), (sbias, sbi),
                           (ones1, on1), (onesc, onc), (cst, cstc)]:
                sync.dma_start(out=t[:], in_=apd).then_inc(s_mi, 16)
            for c in range(NCH):
                if debug and c == NCH - 1:
                    sync.wait_ge(s_uev, NCH * NTT)
                    sync.dma_start(out=d_fr, in_=f_sl[:]).then_inc(s_dbg, 16)
                    sync.dma_start(out=d_ur, in_=u_sl[:]).then_inc(s_dbg, 16)
                sync.wait_ge(s_scan, c + 1)
                sync.dma_start(out=cc_i[c].ap(), in_=carry[:]).then_inc(s_cci, 16)
                sync.wait_ge(s_cc, c + 1)
                if c >= 1:
                    sync.wait_ge(s_fix, c)
                sync.dma_start(out=sin[:], in_=cc_o[c].ap()).then_inc(s_cco, 16)
            sync.wait_ge(s_ap, NCH * NLT)
            sync.dma_start(out=xout.rearrange("(c p) n -> p c n", p=128),
                           in_=X[:].rearrange("p (c n) -> p c n", n=NT)).then_inc(s_out, 16)
            nout = 16
            if debug:
                sync.dma_start(out=d_f, in_=f_sl[:]).then_inc(s_out, 16)
                sync.dma_start(out=d_u, in_=u_sl[:]).then_inc(s_out, 16)
                nout = 48
            sync.wait_ge(s_out, nout)

        @block.gpsimd
        def _(gpsimd):
            gpsimd.dma_start(out=xbf_d.ap(), in_=xin).then_inc(s_xb, 16)
            def load_wf(c):
                par = c % 2
                wof = par * 2 * NCH * 128
                if c >= 1:
                    gpsimd.wait_ge(s_wf, 16 * c)
                if c >= 2:
                    gpsimd.wait_ge(s_kmm, 16 * (c - 1))
                gpsimd.dma_start(out=wf[:, wof:wof + 2 * NCH * 128],
                                 in_=wf_l[:, c * 2048:(c + 1) * 2048]).then_inc(s_wf, 16)
            load_wf(0)
            load_wf(1)
            for c in range(NCH):
                gpsimd.wait_ge(s_cci, 16 * (c + 1))
                gpsimd.collective_compute(
                    "AllReduce", ALU.add,
                    replica_groups=[[0, 1], [2, 3], [4, 5], [6, 7]],
                    ins=[cc_i[c].ap().opt()], outs=[cc_o[c].ap().opt()],
                ).then_inc(s_cc)
                if c + 2 < NCH:
                    load_wf(c + 2)
            if bout is not None:
                gpsimd.wait_ge(s_ap, NCH * NLT)
                gpsimd.dma_start(out=bout.rearrange("(c p) n -> p c n", p=128),
                                 in_=X[:].rearrange("p (c n) -> p c n", n=NT)).then_inc(s_bout, 16)

        @block.tensor
        def _(tensor):
            tensor.wait_ge(s_x, 16)
            for c in range(NCH):
                par = c % 2
                wof = par * 2 * NCH * 128
                tensor.wait_ge(s_wf, 16 * (c + 1))
                for t in range(NTT):
                    kt = c * NTT + t
                    xof = (kt % 2) * NCH * TT_
                    tensor.wait_ge(s_xt, 16 * (kt + 1))
                    if kt >= 2:
                        tensor.wait_ge(s_kfree, kt - 1)
                    for cd in range(NCH):
                        ins = tensor.matmul(
                            ps_k[t % 2][:],
                            wf[:, wof + cd * 128:wof + (cd + 1) * 128],
                            xbf[:, xof + cd * TT_:xof + (cd + 1) * TT_],
                            start=(cd == 0), stop=(cd == NCH - 1))
                    ins.then_inc(s_kmm, 1)
                    if kt >= 2:
                        tensor.wait_ge(s_hxfree, kt - 1)
                    for cd in range(NCH):
                        ins = tensor.matmul(
                            ps_h[t % 2][:],
                            wf[:, wof + (NCH + cd) * 128:wof + (NCH + cd + 1) * 128],
                            xbf[:, xof + cd * TT_:xof + (cd + 1) * TT_],
                            start=(cd == 0), stop=(cd == NCH - 1))
                    ins.then_inc(s_kmm, 1)
            for tt in range(NLT):
                tensor.wait_ge(s_fix, NCH)
                if tt >= 1:
                    tensor.wait_ge(s_rw, tt)
                for c in range(NCH):
                    tensor.matmul(ps_st[0:1, 0:LTB], onc[:, 0:1],
                                  Xtile_g(X, c, tt * LTB, LTB),
                                  start=(c == 0), stop=(c == NCH - 1))
                for c in range(NCH):
                    tensor.wait_ge(s_lsq, NCH * tt + c + 1)
                    tensor.matmul(ps_st[0:1, LTB:2 * LTB], onc[:, 0:1],
                                  y2[:, (c % 2) * LTB:(c % 2 + 1) * LTB],
                                  start=(c == 0), stop=(c == NCH - 1)).then_inc(s_stm, 1)
                tensor.wait_ge(s_ab, tt + 1)
                if tt >= 1:
                    tensor.wait_ge(s_ap, NCH * tt)
                tensor.matmul(ps_bc[:, 0:LTB], on1[0:1, :], rows[0:1, 2 * LTB:3 * LTB],
                              start=True, stop=True)
                tensor.matmul(ps_bc[:, LTB:2 * LTB], on1[0:1, :], rows[0:1, 3 * LTB:4 * LTB],
                              start=True, stop=True).then_inc(s_bc, 1)

        @block.vector
        def _(vector):
            vector.wait_ge(s_x, 16)
            vector.wait_ge(s_mi, 96)
            for c in range(NCH):
                for t in range(NTT):
                    kt = c * NTT + t
                    T0 = t * TT_
                    vector.wait_ge(s_sig, 3 * kt + 3)
                    vector.tensor_scalar(out=t2[:], in0=t2[:], scalar1=0.5, scalar2=None,
                                         op0=ALU.min)
                    vector.scalar_tensor_tensor(out=t2[:], in0=ps_h[t % 2][:], scalar=0.0,
                                                in1=t2[:], op0=ALU.max,
                                                op1=ALU.add).then_inc(s_hxfree, 1)
                    vector.tensor_tensor(out=u_sl[:, T0:T0 + TT_], in0=t1[:], in1=t2[:],
                                         op=ALU.mult).then_inc(s_uev, 1)
                if debug and c == NCH - 1:
                    vector.wait_ge(s_dbg, 32)
                vector.tensor_tensor_scan(out=u_sl[:], data0=f_sl[:], data1=u_sl[:],
                                          initial=sbi[:, 0:1], op0=ALU.mult, op1=ALU.add)
                vector.drain()
                if c >= 1:
                    vector.wait_ge(s_cci, 16 * c)
                vector.tensor_scalar(out=carry[:], in0=u_sl[:, NT - 1:NT],
                                     scalar1=emk[:, 0:1], scalar2=None, op0=ALU.mult)
                vector.drain()
                ins = vector.tensor_tensor_scan(out=f_sl[:], data0=f_sl[:], data1=f_sl[:],
                                                initial=1.0, op0=ALU.mult, op1=ALU.bypass)
                ins.then_inc(s_scan, 1)
                vector.wait_ge(s_cco, 16 * (c + 1))
                vector.tensor_scalar(out=sin[:], in0=sin[:], scalar1=omk[:, 0:1],
                                     scalar2=None, op0=ALU.mult)
                vector.drain()
                vector.scalar_tensor_tensor(out=u_sl[:], in0=f_sl[:], scalar=sin[:, 0:1],
                                            in1=u_sl[:], op0=ALU.mult, op1=ALU.add)
                vector.tensor_tensor(out=X[:, c * NT:(c + 1) * NT],
                                     in0=X[:, c * NT:(c + 1) * NT], in1=u_sl[:],
                                     op=ALU.add).then_inc(s_fix, 1)
            for tt in range(NLT):
                vector.wait_ge(s_stm, NCH * (tt + 1))
                vector.tensor_scalar(out=rows[0:1, 0:LTB], in0=ps_st[0:1, 0:LTB],
                                     scalar1=1.0 / D, scalar2=None, op0=ALU.mult)
                vector.tensor_scalar(out=rows[0:1, LTB:2 * LTB], in0=ps_st[0:1, LTB:2 * LTB],
                                     scalar1=1.0 / D, scalar2=None, op0=ALU.mult).then_inc(s_rw, 1)
                vector.tensor_tensor(out=rows[0:1, 2 * LTB:3 * LTB], in0=rows[0:1, 0:LTB],
                                     in1=rows[0:1, 0:LTB], op=ALU.mult)
                vector.tensor_tensor(out=rows[0:1, 2 * LTB:3 * LTB], in0=rows[0:1, LTB:2 * LTB],
                                     in1=rows[0:1, 2 * LTB:3 * LTB],
                                     op=ALU.subtract).then_inc(s_var, 1)
                vector.wait_ge(s_ar, tt + 1)
                vector.tensor_tensor(out=rows[0:1, 3 * LTB:4 * LTB], in0=rows[0:1, 0:LTB],
                                     in1=rows[0:1, 2 * LTB:3 * LTB], op=ALU.mult)
                vector.tensor_scalar(out=rows[0:1, 3 * LTB:4 * LTB],
                                     in0=rows[0:1, 3 * LTB:4 * LTB], scalar1=-1.0,
                                     scalar2=None, op0=ALU.mult).then_inc(s_ab, 1)
                vector.wait_ge(s_bc, tt + 1)
                for c in range(NCH):
                    vector.tensor_tensor(out=t1[:, 0:LTB],
                                         in0=Xtile_g(X, c, tt * LTB, LTB),
                                         in1=ps_bc[:, 0:LTB], op=ALU.mult)
                    vector.tensor_tensor(out=Xtile_g(X, c, tt * LTB, LTB),
                                         in0=t1[:, 0:LTB],
                                         in1=ps_bc[:, LTB:2 * LTB], op=ALU.add).then_inc(s_ap, 1)

        @block.scalar
        def _(scalar):
            scalar.wait_ge(s_mi, 96)

            def xdma(kt):
                if kt >= 64:
                    return
                c2, t2 = kt // NTT, kt % NTT
                if kt >= 2:
                    scalar.wait_ge(s_kmm, 2 * (kt - 1))
                scalar.dma_start(
                    out=xbf[:, (kt % 2) * NCH * TT_:((kt % 2) + 1) * NCH * TT_]
                        .rearrange("p (c n) -> p c n", n=TT_),
                    in_=xbf_d.ap().rearrange("(c p) n -> p c n", p=128)[:, :, t2 * TT_:(t2 + 1) * TT_]
                ).then_inc(s_xt, 16)

            scalar.wait_ge(s_xb, 16)
            xdma(0)
            xdma(1)
            for c in range(NCH):
                for t in range(NTT):
                    kt = c * NTT + t
                    T0 = t * TT_
                    xdma(kt + 2)
                    if t == 0 and c >= 1:
                        scalar.wait_ge(s_fix, c)
                    scalar.wait_ge(s_kmm, 2 * kt + 1)
                    scalar.activation(out=f_sl[:, T0:T0 + TT_], in_=ps_k[t % 2][:],
                                      func=AF.Sigmoid, scale=-1.0)
                    if kt >= 1:
                        scalar.wait_ge(s_uev, kt)
                    scalar.activation(out=t1[:], in_=ps_k[t % 2][:],
                                      func=AF.Sigmoid).then_inc(s_kfree, 1)
                    scalar.wait_ge(s_kmm, 2 * kt + 2)
                    scalar.activation(out=t2[:], in_=ps_h[t % 2][:],
                                      func=AF.Sigmoid).then_inc(s_sig, 3)
            for tt in range(NLT):
                scalar.wait_ge(s_fix, NCH)
                for c in range(NCH):
                    kk = NCH * tt + c
                    if kk >= 2:
                        scalar.wait_ge(s_stm, kk - 1)
                    scalar.activation(out=y2[:, (c % 2) * LTB:(c % 2 + 1) * LTB],
                                      in_=Xtile_g(X, c, tt * LTB, LTB),
                                      func=AF.Square).then_inc(s_lsq, 1)
                scalar.wait_ge(s_var, tt + 1)
                scalar.activation(out=rows[0:1, 2 * LTB:3 * LTB], in_=rows[0:1, 2 * LTB:3 * LTB],
                                  func=AF.Ln, bias=cstc[0:1, 1:2])
                scalar.activation(out=rows[0:1, 2 * LTB:3 * LTB], in_=rows[0:1, 2 * LTB:3 * LTB],
                                  func=AF.Exp, scale=cstc[0:1, 2:3]).then_inc(s_ar, 1)


def build_l1_attn(debug=False):
    nc = bass.Bass("TRN2", target_bir_lowering=False, debug=False, detect_race_conditions=DETECT_RACES)
    xin = nc.dram_tensor("xin", [D, NT], F32, kind="ExternalInput").ap()
    xout = nc.dram_tensor("xout", [D, NT], F32, kind="ExternalOutput").ap()
    es_s = ExitStack()
    with es_s:
        emit_l1_attn(nc, es_s, "", xin, xout, debug=debug)
    return nc


def Xtile_g(X, c, col0, w):
    return X[:, c * NT + col0: c * NT + col0 + w]


def _l1_attn_inputs(inputs, core):
    Wf = inputs["L1_attn_W_f"]
    odd = core % 2
    d = {
        "wf_l": Wf.reshape(16, 128, 8, 128).transpose(3, 0, 2, 1)[
            :, [c // 2 + 8 * (c % 2) for c in range(16)]].reshape(128, 16384),
        "emask": np.full((128, 1), 0.0 if odd else 1.0, np.float32),
        "omask": np.full((128, 1), 1.0 if odd else 0.0, np.float32),
        "sbias": np.full((128, 1), 0.0 if odd else 0.5, np.float32),
        "ones1": np.ones((1, 128), np.float32),
        "onesc": np.ones((128, 1), np.float32),
        "cst": np.tile(np.array([[EPS_IN, EPS_OUT, -0.5, 0.5]], np.float32), (128, 1)),
    }
    return {k: np.ascontiguousarray(np.asarray(v, dtype=np.float32)) for k, v in d.items()}


def emit_l1_ffn(nc, es_s, pfx, xin, xout, ne_used=10, debug=False, wbf=None, s_pc=None, pc_need=0, xbin=None, s_bin=None):
    NE, NG, NJ = ne_used, 8, 16
    TT_ = 512
    NTT = NT // TT_
    LTB = 256
    NLT = NT // LTB
    NROUT = max(NE - 2, 0)

    if wbf is None:
        w1_l = nc.dram_tensor(pfx + "w1_l", [10, 128, NJ * NCH * 128], F32, kind="ExternalInput").ap()
        w2_l = nc.dram_tensor(pfx + "w2_l", [10, 128, NJ * NCH * 128], F32, kind="ExternalInput").ap()
        w3_l = nc.dram_tensor(pfx + "w3_l", [10, 128, NJ * NCH * 128], F32, kind="ExternalInput").ap()
    b1_c = nc.dram_tensor(pfx + "b1_c", [10, 128, NJ], F32, kind="ExternalInput").ap()
    b2_c = nc.dram_tensor(pfx + "b2_c", [10, 128, NJ], F32, kind="ExternalInput").ap()
    b3_c = nc.dram_tensor(pfx + "b3_c", [10, 128, NCH], F32, kind="ExternalInput").ap()
    gw_r = nc.dram_tensor(pfx + "gw_r", [NCH, 128, NG], F32, kind="ExternalInput").ap()
    gb_row = nc.dram_tensor(pfx + "gb_row", [128, NG], F32, kind="ExternalInput").ap()
    ident = nc.dram_tensor(pfx + "ident", [128, 128], F32, kind="ExternalInput").ap()
    ones1 = nc.dram_tensor(pfx + "ones1", [1, 128], F32, kind="ExternalInput").ap()
    onesc = nc.dram_tensor(pfx + "onesc", [128, 1], F32, kind="ExternalInput").ap()
    cst = nc.dram_tensor(pfx + "cst", [128, 4], F32, kind="ExternalInput").ap()
    acc = nc.dram_tensor(pfx + "acc", [D, NT], F32)
    if xbin is None:
        xbf_d = nc.dram_tensor(pfx + "xbf_d", [D, NT], BF16)
        xsrc = xbf_d.ap()
    else:
        xsrc = xbin
    coef_d = nc.dram_tensor(pfx + "coef_d", [128, 256], F32)
    if debug:
        d_acc = nc.dram_tensor("d_acc", [D, NT], F32, kind="ExternalOutput").ap()
        d_coef = nc.dram_tensor("d_coef", [128, 256], F32, kind="ExternalOutput").ap()

    es = ExitStack()
    sb = lambda n, s, d=F32: es.enter_context(nc.sbuf_tensor(pfx + n, s, d))
    ps = lambda n, s, d=F32: es.enter_context(nc.psum_tensor(pfx + n, s, d))
    sem = lambda n: es_s.enter_context(nc.semaphore(pfx + n))

    W1 = sb("W1", [128, NJ * NCH * 128], BF16)
    W2 = sb("W2", [128, NJ * NCH * 128], BF16)
    W3 = sb("W3", [128, NJ * NCH * 128], BF16)
    xbt = sb("xbt", [128, 2 * NCH * TT_], BF16)
    h1 = sb("h1", [128, NJ * TT_], BF16)
    tsil = sb("tsil", [128, TT_])
    stage = sb("stage", [128, 4096])
    yln = sb("yln", [128, NCH * LTB])
    gx = yln[:, 0:1024]
    y2 = sb("y2", [128, 2 * LTB])
    rows = sb("rows", [1, 4 * LTB])
    gts = sb("gts", [128, 32 * NG])
    gsc = sb("gsc", [128, 4 * 32])
    ctm = sb("ctm", [128, 32 * NG])
    coef_sb = sb("coef_sb", [128, TT_])
    crow = sb("crow", [1, TT_])
    b1a = sb("b1a", [128, 10 * NJ])
    b2a = sb("b2a", [128, 10 * NJ])
    b3a = sb("b3a", [128, 10 * NCH])
    gwr = sb("gwr", [128, NCH * NG], BF16)
    gbr = sb("gbr", [128, NG])
    id_sb = sb("id_sb", [128, 128])
    on1 = sb("on1", [1, 128]); onc = sb("onc", [128, 1]); cstc = sb("cstc", [128, 4])

    ps_p1 = [ps(f"ps_p1{i}", [128, 512]) for i in range(2)]
    ps_p2 = [ps(f"ps_p2{i}", [128, 512]) for i in range(2)]
    ps_o = [ps(f"ps_o{i}", [128, 512]) for i in range(4)]

    s_mi = sem("s_mi"); s_xb = sem("s_xb"); s_a0 = sem("s_a0"); s_gw = sem("s_gw")
    s_gxa = sem("s_gxa"); s_w2 = sem("s_w2"); s_w3 = sem("s_w3")
    s_gx = sem("s_gx"); s_gmm = sem("s_gmm"); s_gev = sem("s_gev")
    s_gsub = sem("s_gsub"); s_gexp = sem("s_gexp"); s_gcoef = sem("s_gcoef")
    s_gtr = sem("s_gtr"); s_gtev = sem("s_gtev"); s_gd = sem("s_gd")
    s_w = sem("s_w"); s_xt = sem("s_xt")
    s_pmm = sem("s_pmm"); s_sil = sem("s_sil"); s_glu = sem("s_glu")
    s_omm = sem("s_omm"); s_cd = sem("s_cd"); s_cmm = sem("s_cmm"); s_cev = sem("s_cev")
    s_sev = sem("s_sev"); s_ad = sem("s_ad")
    s_lnl = sem("s_lnl"); s_sq = sem("s_sq"); s_stm = sem("s_stm"); s_rw = sem("s_rw")
    s_var = sem("s_var"); s_ar = sem("s_ar"); s_ab = sem("s_ab"); s_bc = sem("s_bc")
    s_ap = sem("s_ap"); s_out = sem("s_out")

    N_ACC = NE * NTT * 2 * 4 * 16

    with es, nc.Block() as block:
        @block.sync
        def _(sync):
            sync.dma_start(out=b1a[:].rearrange("p (e n) -> p e n", n=NJ),
                           in_=b1_c.rearrange("e p n -> p e n")).then_inc(s_mi, 16)
            sync.dma_start(out=b2a[:].rearrange("p (e n) -> p e n", n=NJ),
                           in_=b2_c.rearrange("e p n -> p e n")).then_inc(s_mi, 16)
            sync.dma_start(out=b3a[:].rearrange("p (e n) -> p e n", n=NCH),
                           in_=b3_c.rearrange("e p n -> p e n")).then_inc(s_mi, 16)
            sync.dma_start(out=gbr[:], in_=gb_row).then_inc(s_mi, 16)
            sync.dma_start(out=id_sb[:], in_=ident).then_inc(s_mi, 16)
            sync.dma_start(out=on1[:], in_=ones1).then_inc(s_mi, 16)
            sync.dma_start(out=onc[:], in_=onesc).then_inc(s_mi, 16)
            sync.dma_start(out=cstc[:], in_=cst).then_inc(s_mi, 16)
            sync.dma_start(out=acc.rearrange("(c p) n -> p c n", p=128),
                           in_=xin.rearrange("(c p) n -> p c n", p=128)).then_inc(s_a0, 16)
            sync.wait_ge(s_gtev, 2)
            sync.dma_start(out=coef_d.ap(), in_=ctm[:, 0:256]).then_inc(s_gd, 16)

            def coef_rows(r):
                for t in range(NTT):
                    kc = r * NTT + t
                    sync.wait_ge(s_gd, 16)
                    if kc >= 1:
                        sync.wait_ge(s_cmm, kc)
                    tl = t % 4
                    off = 128 if t >= 4 else 0
                    sync.dma_start(
                        out=crow[:],
                        in_=coef_d.ap().rearrange("a b -> (a b)")[None, :]
                            .rearrange("o (g e q) -> o g e q", e=NG, q=256)
                            [:, 4 * tl:4 * (tl + 1), r, off:off + 128]
                    ).then_inc(s_cd, 16)

            if wbf is None:
                for r in range(NROUT):
                    coef_rows(r)
            else:
                w1b, w2b, w3b = wbf
                for e in range(NE):
                    if e == 0:
                        sync.wait_ge(s_pc, pc_need)
                    else:
                        sync.wait_ge(s_pmm, 2 * NJ * NTT * e)
                    sync.dma_start(out=W1[:], in_=w1b[e]).then_inc(s_w, 16)
                    sync.dma_start(out=W2[:], in_=w2b[e]).then_inc(s_w, 16)
                    if e >= 1:
                        sync.wait_ge(s_omm, 2 * NTT * e)
                    sync.dma_start(out=W3[:], in_=w3b[e]).then_inc(s_w, 16)
                    if e >= 2:
                        coef_rows(e - 2)
            for tt in range(NLT):
                sync.wait_ge(s_ad, N_ACC)
                if tt >= 1:
                    sync.wait_ge(s_ap, NCH * tt)
                    sync.wait_ge(s_out, 16 * tt)
                sync.dma_start(
                    out=yln[:].rearrange("p (c n) -> p c n", n=LTB),
                    in_=acc.rearrange("(c p) n -> p c n", p=128)[:, :, tt * LTB:(tt + 1) * LTB]
                ).then_inc(s_lnl, 16)
                sync.wait_ge(s_ap, NCH * (tt + 1))
                sync.dma_start(
                    out=xout.rearrange("(c p) n -> p c n", p=128)[:, :, tt * LTB:(tt + 1) * LTB],
                    in_=yln[:].rearrange("p (c n) -> p c n", n=LTB)
                ).then_inc(s_out, 16)
            sync.wait_ge(s_out, 16 * NLT)
            if debug:
                sync.dma_start(out=d_acc, in_=acc.ap()).then_inc(s_out, 16)
                sync.dma_start(out=d_coef, in_=coef_d.ap()).then_inc(s_out, 16)
                sync.wait_ge(s_out, 16 * NLT + 32)

        @block.gpsimd
        def _(gpsimd):
            gpsimd.dma_start(out=gwr[:].rearrange("p (c n) -> p c n", n=NG),
                             in_=gw_r.rearrange("c p n -> p c n")).then_inc(s_gw, 16)
            gpsimd.dma_start(out=W1[:], in_=w1_l[0]).then_inc(s_w, 16)
            if xbin is not None:
                gpsimd.wait_ge(s_bin, 16)
                gpsimd.dma_start(out=W2[:].rearrange("p (c n) -> p c n", n=NT),
                                 in_=xsrc.rearrange("(c p) n -> p c n", p=128)[:, 0:4, :]
                                 ).then_inc(s_gxa, 16 * 4)
                gpsimd.dma_start(out=W3[:].rearrange("p (c n) -> p c n", n=NT),
                                 in_=xsrc.rearrange("(c p) n -> p c n", p=128)[:, 4:8, :]
                                 ).then_inc(s_gxa, 16 * 4)
            else:
                for cc in range(NCH):
                    slab = W2 if cc < 4 else W3
                    gpsimd.dma_start(
                        out=slab[:].rearrange("p (c n) -> p c n", n=NT)[:, cc % 4:cc % 4 + 1, :],
                        in_=xin.rearrange("(c p) n -> p c n", p=128)[:, cc:cc + 1, :]
                    ).then_inc(s_gxa, 16)
                for sc in range(NTT):
                    gpsimd.dma_start(out=xbf_d.ap()[:, sc * TT_:(sc + 1) * TT_],
                                     in_=xin[:, sc * TT_:(sc + 1) * TT_]).then_inc(s_xb, 16)

            def xload(k2):
                if k2 >= NE * NTT:
                    return
                t2 = k2 % NTT
                if xbin is None:
                    gpsimd.wait_ge(s_xb, 16 * (t2 + 1))
                gpsimd.dma_start(
                    out=xbt[:, (t2 % 2) * NCH * TT_:((t2 % 2) + 1) * NCH * TT_]
                        .rearrange("p (c n) -> p c n", n=TT_),
                    in_=xsrc.rearrange("(c p) n -> p c n", p=128)[:, :, t2 * TT_:(t2 + 1) * TT_]
                ).then_inc(s_xt, 16)

            xload(0)
            xload(1)
            for e in range(NE):
                if e >= 1:
                    gpsimd.wait_ge(s_pmm, 2 * NJ * NTT * e)
                    gpsimd.dma_start(out=W1[:], in_=w1_l[e]).then_inc(s_w, 16)
                    gpsimd.dma_start(out=W2[:], in_=w2_l[e]).then_inc(s_w2, 16)
                    gpsimd.wait_ge(s_omm, 2 * NTT * e)
                    gpsimd.dma_start(out=W3[:], in_=w3_l[e]).then_inc(s_w3, 16)
                for t in range(NTT):
                    k = e * NTT + t
                    if k == 0:
                        gpsimd.wait_ge(s_gmm, 1)
                        gpsimd.dma_start(out=W2[:], in_=w2_l[0]).then_inc(s_w2, 16)
                        gpsimd.dma_start(out=W3[:], in_=w3_l[0]).then_inc(s_w3, 16)
                        gpsimd.wait_ge(s_a0, 16)
                    gpsimd.wait_ge(s_pmm, 2 * NJ * (k + 1))
                    xload(k + 2)
                    for og in range(2):
                        kk = k * 2 + og
                        for oo in range(4):
                            gpsimd.wait_ge(s_sev, 4 * kk + oo + 1)
                            gpsimd.dma_start(
                                out=acc.rearrange("(c p) n -> p c n", p=128)[:, og * 4 + oo:og * 4 + oo + 1, t * TT_:(t + 1) * TT_],
                                in_=stage[:, (kk % 2) * 2048 + oo * 512:(kk % 2) * 2048 + (oo + 1) * 512]
                                    .rearrange("p (c n) -> p c n", n=TT_),
                                accum_op=ALU.add).then_inc(s_ad, 16)

        @block.tensor
        def _(tensor):
            tensor.wait_ge(s_mi, 8 * 16)
            tensor.wait_ge(s_gw, 16)
            tensor.wait_ge(s_gxa, 16 * NCH)
            for g in range(32):
                for cd in range(NCH):
                    slab = W2 if cd < 4 else W3
                    ins = tensor.matmul(ps_p1[0][0:128, g * NG:(g + 1) * NG],
                                        slab[:, (cd % 4) * NT + g * 128:(cd % 4) * NT + (g + 1) * 128],
                                        gwr[:, cd * NG:(cd + 1) * NG],
                                        start=(cd == 0), stop=(cd == NCH - 1))
            ins.then_inc(s_gmm, 1)
            tensor.wait_ge(s_gcoef, 1)
            tensor.transpose(ps_p2[0][:, 0:128], ctm[:, 0:128], id_sb[:]).then_inc(s_gtr, 1)
            tensor.wait_ge(s_gtev, 1)
            tensor.transpose(ps_p2[0][:, 0:128], ctm[:, 128:256], id_sb[:]).then_inc(s_gtr, 1)
            for e in range(NE):
                for t in range(NTT):
                    k = e * NTT + t
                    par = t % 2
                    xof = par * NCH * TT_
                    tensor.wait_ge(s_w, 16 * (e + 1))
                    tensor.wait_ge(s_w2, 16 * (e + 1))
                    tensor.wait_ge(s_xt, 16 * (k + 1))
                    if e >= 2:
                        kc = (e - 2) * NTT + t
                        tensor.wait_ge(s_cd, 16 * (kc + 1))
                        if kc >= 1:
                            tensor.wait_ge(s_cev, kc)
                        else:
                            tensor.wait_ge(s_gtev, 2)
                        tensor.matmul(ps_p1[0][:, 0:TT_], on1[0:1, :], crow[0:1, :],
                                      start=True, stop=True).then_inc(s_cmm, 1)
                    for j in range(NJ):
                        kj = k * NJ + j
                        if kj == 0:
                            tensor.wait_ge(s_gev, 1)
                        if kj >= 2:
                            tensor.wait_ge(s_sil, kj - 1)
                        if e >= 2 and kj == k * NJ:
                            tensor.wait_ge(s_cev, (e - 2) * NTT + t + 1)
                        for cd in range(NCH):
                            ins = tensor.matmul(
                                ps_p1[j % 2][:],
                                W1[:, (j * NCH + cd) * 128:(j * NCH + cd + 1) * 128],
                                xbt[:, xof + cd * TT_: xof + (cd + 1) * TT_],
                                start=(cd == 0), stop=(cd == NCH - 1))
                        ins.then_inc(s_pmm, 1)
                        if kj == 0:
                            tensor.wait_ge(s_gtev, 2)
                        if kj >= 2:
                            tensor.wait_ge(s_glu, kj - 1)
                        for cd in range(NCH):
                            ins = tensor.matmul(
                                ps_p2[j % 2][:],
                                W2[:, (j * NCH + cd) * 128:(j * NCH + cd + 1) * 128],
                                xbt[:, xof + cd * TT_: xof + (cd + 1) * TT_],
                                start=(cd == 0), stop=(cd == NCH - 1))
                        ins.then_inc(s_pmm, 1)
                    if t == 0:
                        tensor.wait_ge(s_w3, 16 * (e + 1))
                    for og in range(2):
                        kk = k * 2 + og
                        if kk >= 1:
                            tensor.wait_ge(s_sev, 4 * kk)
                        for j in range(NJ):
                            tensor.wait_ge(s_glu, k * NJ + j + 1)
                            for oo in range(4):
                                o = og * 4 + oo
                                ins = tensor.matmul(
                                    ps_o[oo][:],
                                    W3[:, (j * NCH + o) * 128:(j * NCH + o + 1) * 128],
                                    h1[:, j * TT_:(j + 1) * TT_],
                                    start=(j == 0), stop=(j == NJ - 1))
                        ins.then_inc(s_omm, 1)
            for tt in range(NLT):
                tensor.wait_ge(s_lnl, 16 * (tt + 1))
                if tt >= 1:
                    tensor.wait_ge(s_rw, tt)
                for c in range(NCH):
                    tensor.matmul(ps_p1[0][0:1, 0:LTB], onc[:, 0:1],
                                  yln[:, c * LTB:(c + 1) * LTB],
                                  start=(c == 0), stop=(c == NCH - 1))
                for c in range(NCH):
                    tensor.wait_ge(s_sq, NCH * tt + c + 1)
                    tensor.matmul(ps_p1[0][0:1, LTB:2 * LTB], onc[:, 0:1],
                                  y2[:, (c % 2) * LTB:(c % 2 + 1) * LTB],
                                  start=(c == 0), stop=(c == NCH - 1)).then_inc(s_stm, 1)
                tensor.wait_ge(s_ab, tt + 1)
                if tt >= 1:
                    tensor.wait_ge(s_ap, NCH * tt)
                tensor.matmul(ps_p2[0][:, 0:LTB], on1[0:1, :], rows[0:1, 2 * LTB:3 * LTB],
                              start=True, stop=True)
                tensor.matmul(ps_p2[0][:, LTB:2 * LTB], on1[0:1, :], rows[0:1, 3 * LTB:4 * LTB],
                              start=True, stop=True).then_inc(s_bc, 1)

        @block.vector
        def _(vector):
            vector.wait_ge(s_gmm, 1)
            vector.wait_ge(s_mi, 8 * 16)
            vector.tensor_copy(out=gts[:], in_=ps_p1[0][0:128, 0:32 * NG]).then_inc(s_gev, 1)
            vector.drain()
            g3 = gts[:].rearrange("p (g e) -> p g e", e=NG)
            a_, b_ = bass.broadcast_tensor_aps(g3, gbr[:].rearrange("p (o e) -> p o e", o=1))
            vector.tensor_tensor(out=g3, in0=a_, in1=b_, op=ALU.add)
            vector.drain()
            rmax = gsc[:, 0:32]; rsum = gsc[:, 32:64]
            cnt = gsc[:, 64:96]; gtmp = gsc[:, 96:128]
            vector.reduce_max(out=rmax, in_=g3, axis=mybir.AxisListType.X)
            vector.drain()
            a_, b_ = bass.broadcast_tensor_aps(g3, rmax.rearrange("p (g o) -> p g o", o=1))
            vector.tensor_tensor(out=g3, in0=a_, in1=b_, op=ALU.subtract).then_inc(s_gsub, 1)
            vector.wait_ge(s_gexp, 1)
            vector.reduce_sum(out=rsum, in_=g3, axis=mybir.AxisListType.X)
            vector.drain()
            vector.reciprocal(out=rsum, in_=rsum)
            vector.drain()
            a_, b_ = bass.broadcast_tensor_aps(g3, rsum.rearrange("p (g o) -> p g o", o=1))
            vector.tensor_tensor(out=g3, in0=a_, in1=b_, op=ALU.mult)
            vector.drain()
            c3 = ctm[:].rearrange("p (g e) -> p g e", e=NG)
            for ei in range(NG):
                first = True
                for j in range(NG):
                    if j == ei:
                        continue
                    wj = g3[:, :, j:j + 1]
                    we = g3[:, :, ei:ei + 1]
                    if first:
                        vector.tensor_tensor(out=cnt.rearrange("p (g o) -> p g o", o=1),
                                             in0=wj, in1=we, op=ALU.is_gt)
                        first = False
                    else:
                        vector.tensor_tensor(out=gtmp.rearrange("p (g o) -> p g o", o=1),
                                             in0=wj, in1=we, op=ALU.is_gt)
                        vector.drain()
                        vector.tensor_tensor(out=cnt, in0=cnt, in1=gtmp, op=ALU.add)
                    vector.drain()
                vector.tensor_scalar(out=gtmp, in0=cnt, scalar1=1.5, scalar2=None,
                                     op0=ALU.is_le)
                vector.drain()
                ins = vector.tensor_tensor(out=c3[:, :, ei:ei + 1],
                                           in0=g3[:, :, ei:ei + 1],
                                           in1=gtmp.rearrange("p (g o) -> p g o", o=1),
                                           op=ALU.mult)
                vector.drain()
            ins.then_inc(s_gcoef, 1)
            vector.wait_ge(s_gtr, 1)
            vector.tensor_copy(out=ctm[:, 0:128], in_=ps_p2[0][:, 0:128]).then_inc(s_gtev, 1)
            vector.wait_ge(s_gtr, 2)
            vector.tensor_copy(out=ctm[:, 128:256], in_=ps_p2[0][:, 0:128]).then_inc(s_gtev, 1)
            for e in range(NE):
                for t in range(NTT):
                    k = e * NTT + t
                    if e >= 2:
                        kc = (e - 2) * NTT + t
                        vector.wait_ge(s_cmm, kc + 1)
                        vector.tensor_copy(out=coef_sb[:], in_=ps_p1[0][:, 0:TT_]).then_inc(s_cev, 1)
                    for j in range(NJ):
                        kj = k * NJ + j
                        vector.wait_ge(s_pmm, 2 * kj + 2)
                        vector.wait_ge(s_sil, kj + 1)
                        vector.scalar_tensor_tensor(
                            out=h1[:, j * TT_:(j + 1) * TT_], in0=ps_p2[j % 2][:],
                            scalar=b2a[:, e * NJ + j:e * NJ + j + 1], in1=tsil[:],
                            op0=ALU.add, op1=ALU.mult).then_inc(s_glu, 1)
                    for og in range(2):
                        kk = k * 2 + og
                        vector.wait_ge(s_omm, kk + 1)
                        sof = (kk % 2) * 2048
                        for oo in range(4):
                            o = og * 4 + oo
                            if kk >= 2:
                                vector.wait_ge(s_ad, 16 * (4 * (kk - 2) + oo + 1))
                            if e < 2:
                                vector.tensor_scalar(
                                    out=stage[:, sof + oo * 512:sof + (oo + 1) * 512],
                                    in0=ps_o[oo][:], scalar1=b3a[:, e * NCH + o:e * NCH + o + 1],
                                    scalar2=cstc[:, 3:4], op0=ALU.add,
                                    op1=ALU.mult).then_inc(s_sev, 1)
                            else:
                                vector.scalar_tensor_tensor(
                                    out=stage[:, sof + oo * 512:sof + (oo + 1) * 512], in0=ps_o[oo][:],
                                    scalar=b3a[:, e * NCH + o:e * NCH + o + 1], in1=coef_sb[:],
                                    op0=ALU.add, op1=ALU.mult).then_inc(s_sev, 1)
            for tt in range(NLT):
                vector.wait_ge(s_stm, NCH * (tt + 1))
                vector.tensor_scalar(out=rows[0:1, 0:LTB], in0=ps_p1[0][0:1, 0:LTB],
                                     scalar1=1.0 / D, scalar2=None, op0=ALU.mult)
                vector.tensor_scalar(out=rows[0:1, LTB:2 * LTB], in0=ps_p1[0][0:1, LTB:2 * LTB],
                                     scalar1=1.0 / D, scalar2=None, op0=ALU.mult).then_inc(s_rw, 1)
                vector.tensor_tensor(out=rows[0:1, 2 * LTB:3 * LTB], in0=rows[0:1, 0:LTB],
                                     in1=rows[0:1, 0:LTB], op=ALU.mult)
                vector.tensor_tensor(out=rows[0:1, 2 * LTB:3 * LTB], in0=rows[0:1, LTB:2 * LTB],
                                     in1=rows[0:1, 2 * LTB:3 * LTB],
                                     op=ALU.subtract).then_inc(s_var, 1)
                vector.wait_ge(s_ar, tt + 1)
                vector.tensor_tensor(out=rows[0:1, 3 * LTB:4 * LTB], in0=rows[0:1, 0:LTB],
                                     in1=rows[0:1, 2 * LTB:3 * LTB], op=ALU.mult)
                vector.tensor_scalar(out=rows[0:1, 3 * LTB:4 * LTB],
                                     in0=rows[0:1, 3 * LTB:4 * LTB], scalar1=-1.0,
                                     scalar2=None, op0=ALU.mult).then_inc(s_ab, 1)
                vector.wait_ge(s_bc, tt + 1)
                for c in range(NCH):
                    vector.tensor_tensor(out=stage[:, 0:LTB],
                                         in0=yln[:, c * LTB:(c + 1) * LTB],
                                         in1=ps_p2[0][:, 0:LTB], op=ALU.mult)
                    vector.tensor_tensor(out=yln[:, c * LTB:(c + 1) * LTB],
                                         in0=stage[:, 0:LTB],
                                         in1=ps_p2[0][:, LTB:2 * LTB], op=ALU.add).then_inc(s_ap, 1)

        @block.scalar
        def _(scalar):
            scalar.wait_ge(s_gsub, 1)
            scalar.activation(out=gts[:], in_=gts[:], func=AF.Exp).then_inc(s_gexp, 1)
            for e in range(NE):
                for t in range(NTT):
                    k = e * NTT + t
                    for j in range(NJ):
                        kj = k * NJ + j
                        scalar.wait_ge(s_pmm, 2 * kj + 1)
                        if kj >= 1:
                            scalar.wait_ge(s_glu, kj)
                        if k >= 1 and j == 0:
                            scalar.wait_ge(s_omm, 2 * k)
                        scalar.activation(out=tsil[:], in_=ps_p1[j % 2][:], func=AF.Silu,
                                          bias=b1a[:, e * NJ + j:e * NJ + j + 1]).then_inc(s_sil, 1)
            for tt in range(NLT):
                for c in range(NCH):
                    scalar.wait_ge(s_lnl, 16 * (tt + 1))
                    kk = NCH * tt + c
                    if kk >= 2:
                        scalar.wait_ge(s_stm, kk - 1)
                    scalar.activation(out=y2[:, (c % 2) * LTB:(c % 2 + 1) * LTB],
                                      in_=yln[:, c * LTB:(c + 1) * LTB],
                                      func=AF.Square).then_inc(s_sq, 1)
                scalar.wait_ge(s_var, tt + 1)
                scalar.activation(out=rows[0:1, 2 * LTB:3 * LTB], in_=rows[0:1, 2 * LTB:3 * LTB],
                                  func=AF.Ln, bias=cstc[0:1, 1:2])
                scalar.activation(out=rows[0:1, 2 * LTB:3 * LTB], in_=rows[0:1, 2 * LTB:3 * LTB],
                                  func=AF.Exp, scale=cstc[0:1, 2:3]).then_inc(s_ar, 1)


def build_l1_ffn(ne_used=10, debug=False):
    nc = bass.Bass("TRN2", target_bir_lowering=False, debug=False, detect_race_conditions=DETECT_RACES)
    xin = nc.dram_tensor("xin", [D, NT], F32, kind="ExternalInput").ap()
    xout = nc.dram_tensor("xout", [D, NT], F32, kind="ExternalOutput").ap()
    es_s = ExitStack()
    with es_s:
        emit_l1_ffn(nc, es_s, "", xin, xout, ne_used=ne_used, debug=debug)
    return nc


def build_fused(taps=False):
    nc = bass.Bass("TRN2", target_bir_lowering=False, debug=False, detect_race_conditions=DETECT_RACES)
    xin = nc.dram_tensor("xin", [D, NT], F32, kind="ExternalInput").ap()
    xout = nc.dram_tensor("xout", [D, NT], F32, kind="ExternalOutput").ap()
    kind = "ExternalOutput" if taps else "Internal"
    xb1 = nc.dram_tensor("xb1", [D, NT], F32, kind=kind)
    xb2 = nc.dram_tensor("xb2", [D, NT], F32, kind=kind)
    xb3 = nc.dram_tensor("xb3", [D, NT], F32, kind=kind)
    xb1b = nc.dram_tensor("xb1b", [D, NT], BF16)
    xb3b = nc.dram_tensor("xb3b", [D, NT], BF16)
    s_b1 = nc.alloc_semaphore(name="s_b1")
    s_b3 = nc.alloc_semaphore(name="s_b3")
    emits = [
        (emit_l0_attn, "a0", xin, xb1.ap(), {"bout": xb1b.ap(), "s_bout": s_b1}),
        (emit_l0_ffn, "f0", xb1.ap(), xb2.ap(), {"xbin": xb1b.ap(), "s_bin": s_b1}),
        (emit_l1_attn, "a1", xb2.ap(), xb3.ap(), {"bout": xb3b.ap(), "s_bout": s_b3}),
        (emit_l1_ffn, "f1", xb3.ap(), xout, {"xbin": xb3b.ap(), "s_bin": s_b3}),
    ]
    for fn, pfx, i_ap, o_ap, kw in emits:
        with nc.cleanup_on_exit():
            fn(nc, ExitStack(), pfx, i_ap, o_ap, **kw)
            nc.all_engine_barrier()
    return nc


def _l1_ffn_inputs(inputs, core):
    NJ = 16
    sw1, sw2, sw3 = inputs["L1_ffn_sw1"], inputs["L1_ffn_sw2"], inputs["L1_ffn_sw3"]
    rw1, rw2, rw3 = inputs["L1_ffn_rw1"], inputs["L1_ffn_rw2"], inputs["L1_ffn_rw3"]
    sb1, sb2, sb3 = inputs["L1_ffn_sb1"], inputs["L1_ffn_sb2"], inputs["L1_ffn_sb3"]
    rb1, rb2, rb3 = inputs["L1_ffn_rb1"], inputs["L1_ffn_rb2"], inputs["L1_ffn_rb3"]
    w1 = np.concatenate([sw1, rw1], 0)
    w2 = np.concatenate([sw2, rw2], 0)
    w3 = np.concatenate([sw3, rw3], 0)
    b1 = np.concatenate([sb1, rb1], 0)
    b2 = np.concatenate([sb2, rb2], 0)
    b3 = np.concatenate([sb3, rb3], 0)
    d = {
        "w1_l": w1.reshape(10, NCH, 128, NJ, 128).transpose(0, 2, 3, 1, 4).reshape(10, 128, NJ * NCH * 128),
        "w2_l": w2.reshape(10, NCH, 128, NJ, 128).transpose(0, 2, 3, 1, 4).reshape(10, 128, NJ * NCH * 128),
        "w3_l": w3.reshape(10, NJ, 128, NCH, 128).transpose(0, 2, 1, 3, 4).reshape(10, 128, NJ * NCH * 128),
        "b1_c": b1.reshape(10, NJ, 128).transpose(0, 2, 1),
        "b2_c": b2.reshape(10, NJ, 128).transpose(0, 2, 1),
        "b3_c": b3.reshape(10, NCH, 128).transpose(0, 2, 1),
        "gw_r": inputs["L1_ffn_gate_w"].reshape(NCH, 128, 8),
        "gb_row": np.tile(inputs["L1_ffn_gate_b"].reshape(1, 8), (128, 1)),
        "ident": np.eye(128, dtype=np.float32),
        "ones1": np.ones((1, 128), np.float32),
        "onesc": np.ones((128, 1), np.float32),
        "cst": np.tile(np.array([[EPS_IN, EPS_OUT, -0.5, 0.5]], np.float32), (128, 1)),
    }
    return {k: np.ascontiguousarray(np.asarray(v, dtype=np.float32)) for k, v in d.items()}


# version 12 (final)
# speedup vs baseline: 1.0734x; 1.0734x over previous
import numpy as np
import concourse.bass as bass
import concourse.mybir as mybir
from concourse.bass_utils import run_bass_kernel_spmd
from contextlib import ExitStack

F32 = mybir.dt.float32
BF16 = mybir.dt.bfloat16
AF = mybir.ActivationFunctionType
ALU = mybir.AluOpType

D = 1024
NCH = 8
NT = 4096
EPS_IN = 1e-6
EPS_OUT = 1e-5
DETECT_RACES = False


def emit_l0_attn(nc, es_s, pfx, xin, xout, debug=False, nsub=32, phase_b=True, precast=None, s_pc=None):
    wk_r = nc.dram_tensor(pfx + "wk_r", [NCH, 128, D], F32, kind="ExternalInput").ap()
    wv_r = nc.dram_tensor(pfx + "wv_r", [NCH, 128, D], F32, kind="ExternalInput").ap()
    wq_l = nc.dram_tensor(pfx + "wq_l", [128, 8192], F32, kind="ExternalInput").ap()
    wo_l = nc.dram_tensor(pfx + "wo_l", [128, 8192], F32, kind="ExternalInput").ap()
    bk_row = nc.dram_tensor(pfx + "bk_row", [1, D], F32, kind="ExternalInput").ap()
    bv_row = nc.dram_tensor(pfx + "bv_row", [1, D], F32, kind="ExternalInput").ap()
    bq_col = nc.dram_tensor(pfx + "bq_col", [128, NCH], F32, kind="ExternalInput").ap()
    bo_col = nc.dram_tensor(pfx + "bo_col", [128, NCH], F32, kind="ExternalInput").ap()
    gam_col = nc.dram_tensor(pfx + "gam_col", [128, NCH], F32, kind="ExternalInput").ap()
    bet_col = nc.dram_tensor(pfx + "bet_col", [128, NCH], F32, kind="ExternalInput").ap()
    ones1 = nc.dram_tensor(pfx + "ones1", [1, 128], F32, kind="ExternalInput").ap()
    onesc = nc.dram_tensor(pfx + "onesc", [128, 1], F32, kind="ExternalInput").ap()
    cst = nc.dram_tensor(pfx + "cst", [128, 4], F32, kind="ExternalInput").ap()
    cc_in = nc.dram_tensor(pfx + "cc_in", [64, 1024], F32)
    cc_out = nc.dram_tensor(pfx + "cc_out", [64, 1024], F32)
    if debug:
        d_S = nc.dram_tensor("d_S", [64, 1024], F32, kind="ExternalOutput").ap()
        d_sblk = nc.dram_tensor("d_sblk", [128, 1024], F32, kind="ExternalOutput").ap()
        d_kbf = nc.dram_tensor("d_kbf", [128, 1024], F32, kind="ExternalOutput").ap()
        d_vbf = nc.dram_tensor("d_vbf", [128, 1024], F32, kind="ExternalOutput").ap()
        d_qa = nc.dram_tensor("d_qa", [128, 2048], F32, kind="ExternalOutput").ap()
        d_y = nc.dram_tensor("d_y", [128, 2048], F32, kind="ExternalOutput").ap()
        d_rst = nc.dram_tensor("d_rst", [128, 16], F32, kind="ExternalOutput").ap()
        d_kq = nc.dram_tensor("d_kq", [128, 1024], F32, kind="ExternalOutput").ap()

    NSUB = nsub
    TB = 256
    NTB = NT // TB

    es = ExitStack()
    sb = lambda n, s, d=F32: es.enter_context(nc.sbuf_tensor(pfx + n, s, d))
    ps = lambda n, s, d=F32: es.enter_context(nc.psum_tensor(pfx + n, s, d))
    sem = lambda n: es_s.enter_context(nc.semaphore(pfx + n))

    X = sb("X", [128, NCH * NT])
    W1 = sb("W1", [128, 8192], BF16)
    W2 = sb("W2", [128, 8192], BF16)
    xbf = sb("xbf", [128, 2048], BF16)
    arena = sb("arena", [128, 2048])
    kbf = sb("kbf", [128, 1024], BF16)
    vbf = sb("vbf", [128, 1024], BF16)
    S_acc = sb("S_acc", [64, 1024])
    Sblk = sb("Sblk", [128, 1024], BF16)
    qa = sb("qa", [128, NCH * TB], BF16)
    y_sb = sb("y_sb", [128, NCH * TB])
    y2 = sb("y2", [128, 2 * TB])
    rowsA = sb("rowsA", [1, 2048])
    rows = rowsA[0:1, 0:4 * TB]
    rst = sb("rst", [128, 16])
    bkr = rowsA[0:1, 0:D]; bvr = rowsA[0:1, D:2 * D]
    bqc = sb("bqc", [128, NCH]); boc = sb("boc", [128, NCH])
    gmc = sb("gmc", [128, NCH]); btc = sb("btc", [128, NCH])
    on1 = sb("on1", [1, 128]); onc = sb("onc", [128, 1]); cstc = sb("cstc", [128, 4])

    ps_k = [ps(f"ps_k{i}", [128, 512]) for i in range(2)]
    ps_v = [ps(f"ps_v{i}", [128, 512]) for i in range(2)]
    ps_S = [ps(f"ps_S{i}", [128, 512]) for i in range(2)]
    ps_stat = ps("ps_stat", [1, 512])
    ps_bc = ps("ps_bc", [128, 512])

    s_x = sem("s_x"); s_wA = sem("s_wA"); s_wB = sem("s_wB"); s_mi = sem("s_mi")
    s_cast = sem("s_cast"); s_kv = sem("s_kv"); s_kfree = sem("s_kfree")
    s_vfree = sem("s_vfree"); s_kbf = sem("s_kbf"); s_dmin = sem("s_dmin")
    s_exp = sem("s_exp"); s_eps = sem("s_eps"); s_lnr = sem("s_lnr")
    s_half = sem("s_half"); s_exp2 = sem("s_exp2"); s_S = sem("s_S")
    s_sev = sem("s_sev"); s_dma = sem("s_dma"); s_cc = sem("s_cc")
    s_qmm = sem("s_qmm"); s_q1 = sem("s_q1"); s_exq = sem("s_exq")
    s_qfree = sem("s_qfree"); s_qa = sem("s_qa"); s_atmm = sem("s_atmm")
    s_atev = sem("s_atev"); s_omm = sem("s_omm"); s_yev = sem("s_yev")
    s_sq = sem("s_sq"); s_statm = sem("s_statm")
    s_rows = sem("s_rows"); s_eps2 = sem("s_eps2"); s_lnr2 = sem("s_lnr2")
    s_half2 = sem("s_half2"); s_arow = sem("s_arow"); s_bc = sem("s_bc")
    s_ap1 = sem("s_ap1"); s_ap2 = sem("s_ap2"); s_out = sem("s_out")
    s_sblk = sem("s_sblk"); s_sacc = sem("s_sacc")

    def Xtile(c, col0, w):
        return X[:, c * NT + col0: c * NT + col0 + w]

    def X3(col0, w):
        return X[:].rearrange("p (c n) -> p c n", n=NT)[:, :, col0:col0 + w]

    with es, nc.Block() as block:
        @block.sync
        def _(sync):
            sync.dma_start(out=X[:].rearrange("p (c n) -> p c n", n=NT),
                           in_=xin.rearrange("(c p) n -> p c n", p=128)).then_inc(s_x, 16)
            sync.dma_start(out=bkr, in_=bk_row).then_inc(s_mi, 16)
            sync.dma_start(out=bvr, in_=bv_row).then_inc(s_mi, 16)
            for apd, t in [(bq_col, bqc), (bo_col, boc),
                           (gam_col, gmc), (bet_col, btc), (ones1, on1), (onesc, onc), (cst, cstc)]:
                sync.dma_start(out=t[:], in_=apd).then_inc(s_mi, 16)
            sync.wait_ge(s_sacc, 2 * NSUB)
            sync.dma_start(out=cc_in.ap(), in_=S_acc[:]).then_inc(s_dma, 16)
            sync.wait_ge(s_cc, 1)
            sync.dma_start(out=S_acc[:], in_=cc_out.ap()).then_inc(s_dma, 32)
            sync.wait_ge(s_ap2, NCH * NTB if phase_b else 0)
            if not phase_b:
                sync.wait_ge(s_sblk, 256)
            sync.dma_start(out=xout.rearrange("(c p) n -> p c n", p=128),
                           in_=X[:].rearrange("p (c n) -> p c n", n=NT)).then_inc(s_out, 16)
            nout = 16
            if debug:
                sync.dma_start(out=d_S, in_=S_acc[:]).then_inc(s_out, 16)
                sync.dma_start(out=d_y, in_=y_sb[:]).then_inc(s_out, 16)
                nout += 32

            sync.wait_ge(s_out, nout)

        @block.gpsimd
        def _(gpsimd):
            gpsimd.dma_start(out=W1[:].rearrange("p (c n) -> p c n", n=1024), in_=wk_r.rearrange("c p n -> p c n")).then_inc(s_wA, 16)
            gpsimd.dma_start(out=W2[:].rearrange("p (c n) -> p c n", n=1024), in_=wv_r.rearrange("c p n -> p c n")).then_inc(s_wA, 16)
            gpsimd.wait_ge(s_kv, 4 * NSUB)
            gpsimd.dma_start(out=W1[:], in_=wq_l).then_inc(s_wB, 16)
            gpsimd.dma_start(out=W2[:], in_=wo_l).then_inc(s_wB, 16)
            gpsimd.wait_ge(s_dma, 16)
            gpsimd.collective_compute(
                "AllReduce", ALU.add,
                replica_groups=[[0, 1], [2, 3], [4, 5], [6, 7]],
                ins=[cc_in.ap().opt()], outs=[cc_out.ap().opt()],
            ).then_inc(s_cc)
            if precast:
                for dst, src_ in precast:
                    gpsimd.dma_start(out=dst, in_=src_).then_inc(s_pc, 16)
            gpsimd.wait_ge(s_sev, 1)
            for p in range(NCH):
                for i in range(2):
                    h = 2 * p + i
                    gpsimd.dma_start(
                        out=Sblk[i * 64:(i + 1) * 64, p * 128 + i * 64: p * 128 + (i + 1) * 64],
                        in_=S_acc[0:64, h * 64:(h + 1) * 64]).then_inc(s_sblk, 16)
            if debug:
                gpsimd.wait_ge(s_sblk, 256)
                gpsimd.dma_start(out=d_sblk, in_=Sblk[:]).then_inc(s_sblk, 16)
                gpsimd.wait_ge(s_S, NSUB)
                gpsimd.dma_start(out=d_kbf, in_=kbf[:]).then_inc(s_sblk, 16)
                gpsimd.dma_start(out=d_rst, in_=rst[:]).then_inc(s_sblk, 16)
                gpsimd.dma_start(out=d_kq, in_=arena[:, 0:1024]).then_inc(s_sblk, 16)
                gpsimd.dma_start(out=d_vbf, in_=vbf[:]).then_inc(s_sblk, 16)
                gpsimd.wait_ge(s_ap2 if phase_b else s_sblk, NCH * NTB if phase_b else 272)
                gpsimd.dma_start(out=d_qa, in_=qa[:]).then_inc(s_sblk, 16)

        @block.tensor
        def _(tensor):
            tensor.wait_ge(s_x, 16)
            tensor.wait_ge(s_wA, 32)
            tensor.wait_ge(s_mi, 144)
            for t in range(NSUB):
                par = t % 2
                xof = par * 1024
                tensor.wait_ge(s_cast, t + 1)
                for half in range(2):
                    if t >= 1:
                        tensor.wait_ge(s_kfree, 2 * (t - 1) + half + 1)
                    tensor.matmul(ps_k[half][:], on1[0:1, :],
                                  bkr[0:1, half * 512:(half + 1) * 512], start=True, stop=False)
                    for cd in range(NCH):
                        ins = tensor.matmul(
                            ps_k[half][:],
                            xbf[:, xof + cd * 128: xof + (cd + 1) * 128],
                            W1[:, cd * 1024 + half * 512: cd * 1024 + (half + 1) * 512],
                            start=False, stop=(cd == NCH - 1))
                    ins.then_inc(s_kv, 1)
                if t >= 1:
                    tensor.wait_ge(s_kbf, 2 * t)
                    tensor.wait_ge(s_vfree, 2 * t)
                    if t >= 2:
                        tensor.wait_ge(s_sacc, 2 * (t - 1))
                    for h in range(16):
                        bank = h // 8
                        col = (h % 8) * 64
                        ins = tensor.matmul(ps_S[bank][0:64, col:col + 64],
                                            kbf[:, h * 64:(h + 1) * 64],
                                            vbf[:, h * 64:(h + 1) * 64],
                                            start=True, stop=True)
                    ins.then_inc(s_S, 1)
                for half in range(2):
                    if t >= 1:
                        tensor.wait_ge(s_vfree, 2 * (t - 1) + half + 1)
                    tensor.matmul(ps_v[half][:], on1[0:1, :],
                                  bvr[0:1, half * 512:(half + 1) * 512], start=True, stop=False)
                    for cd in range(NCH):
                        ins = tensor.matmul(
                            ps_v[half][:],
                            xbf[:, xof + cd * 128: xof + (cd + 1) * 128],
                            W2[:, cd * 1024 + half * 512: cd * 1024 + (half + 1) * 512],
                            start=False, stop=(cd == NCH - 1))
                    ins.then_inc(s_kv, 1)
            tensor.wait_ge(s_kbf, 2 * NSUB)
            tensor.wait_ge(s_vfree, 2 * NSUB)
            tensor.wait_ge(s_sacc, 2 * (NSUB - 1))
            for h in range(16):
                bank = h // 8
                col = (h % 8) * 64
                ins = tensor.matmul(ps_S[bank][0:64, col:col + 64],
                                    kbf[:, h * 64:(h + 1) * 64], vbf[:, h * 64:(h + 1) * 64],
                                    start=True, stop=True)
            ins.then_inc(s_S, 1)

            if not phase_b:
                return
            tensor.wait_ge(s_wB, 32)
            tensor.wait_ge(s_sblk, 256)
            tensor.wait_ge(s_sacc, 2 * NSUB)
            tensor.wait_ge(s_kfree, 2 * NSUB)
            tensor.wait_ge(s_vfree, 2 * NSUB)
            for tt in range(NTB):
                xof = 0
                tensor.wait_ge(s_cast, NSUB + tt + 1)
                for j in range(NCH):
                    k = NCH * tt + j
                    if k >= 2:
                        tensor.wait_ge(s_qfree, k - 1)
                    for cd in range(NCH):
                        ins = tensor.matmul(
                            ps_k[j % 2][:, 0:TB],
                            W1[:, (j * 8 + cd) * 128:(j * 8 + cd + 1) * 128],
                            xbf[:, xof + cd * TB: xof + (cd + 1) * TB],
                            start=(cd == 0), stop=(cd == NCH - 1))
                    ins.then_inc(s_qmm, 1)
                for p in range(NCH):
                    k = NCH * tt + p
                    tensor.wait_ge(s_qa, k + 1)
                    if k >= 2:
                        tensor.wait_ge(s_atev, k - 1)
                    tensor.matmul(ps_v[p % 2][:, 0:TB], Sblk[:, p * 128:(p + 1) * 128],
                                  qa[:, p * TB:(p + 1) * TB],
                                  start=True, stop=True).then_inc(s_atmm, 1)
                tensor.wait_ge(s_atev, NCH * (tt + 1))
                for o in range(NCH):
                    k = NCH * tt + o
                    if k >= 2:
                        tensor.wait_ge(s_yev, k - 1)
                    for cd in range(NCH):
                        ins = tensor.matmul(
                            ps_S[o % 2][:, 0:TB],
                            W2[:, (o * 8 + cd) * 128:(o * 8 + cd + 1) * 128],
                            qa[:, cd * TB:(cd + 1) * TB],
                            start=(cd == 0), stop=(cd == NCH - 1))
                    ins.then_inc(s_omm, 1)
                for rnd in range(2):
                    tensor.wait_ge(s_rows, 2 * tt + rnd)
                    ready = s_yev if rnd == 0 else s_ap1
                    for c in range(NCH):
                        tensor.wait_ge(ready, NCH * tt + c + 1)
                        tensor.matmul(ps_stat[0:1, 0:TB], onc[:, 0:1],
                                      y_sb[:, c * TB:(c + 1) * TB],
                                      start=(c == 0), stop=(c == NCH - 1))
                    for c in range(NCH):
                        tensor.wait_ge(s_sq, 16 * tt + 8 * rnd + c + 1)
                        tensor.matmul(ps_stat[0:1, TB:2 * TB], onc[:, 0:1],
                                      y2[:, (c % 2) * TB:(c % 2 + 1) * TB],
                                      start=(c == 0), stop=(c == NCH - 1)).then_inc(s_statm, 1)
                    tensor.wait_ge(s_arow, 2 * tt + rnd + 1)
                    if rnd == 0:
                        tensor.wait_ge(s_ap2, NCH * tt)
                    else:
                        tensor.wait_ge(s_ap1, NCH * (tt + 1))
                    tensor.matmul(ps_bc[:, 0:TB], on1[0:1, :], rows[0:1, 2 * TB:3 * TB],
                                  start=True, stop=True)
                    tensor.matmul(ps_bc[:, TB:2 * TB], on1[0:1, :], rows[0:1, 3 * TB:4 * TB],
                                  start=True, stop=True).then_inc(s_bc, 1)

        @block.vector
        def _(vector):
            vector.memset(S_acc[:], 0.0)
            vector.wait_ge(s_x, 16)
            for t in range(NSUB):
                par = t % 2
                xof = par * 1024
                if t >= 2:
                    vector.wait_ge(s_kv, 4 * (t - 1))
                vector.tensor_copy(
                    out=xbf[:, xof:xof + 1024].rearrange("p (c n) -> p c n", n=128),
                    in_=X3(t * 128, 128)).then_inc(s_cast, 1)
                for half in range(2):
                    vector.wait_ge(s_kv, 4 * t + half + 1)
                    vector.tensor_scalar(out=arena[:, half * 512:(half + 1) * 512],
                                         in0=ps_k[half][:], scalar1=0.0, scalar2=None,
                                         op0=ALU.min).then_inc(s_dmin, 1)
                for half in range(2):
                    hof = half * 512
                    vector.wait_ge(s_exp, 2 * t + half + 1)
                    vector.scalar_tensor_tensor(
                        out=arena[:, hof:hof + 512], in0=ps_k[half][:], scalar=0.0,
                        in1=arena[:, 1024 + hof:1024 + hof + 512],
                        op0=ALU.max, op1=ALU.add).then_inc(s_kfree, 1)
                    vector.tensor_tensor(out=arena[:, 1024 + hof:1024 + hof + 512],
                                         in0=arena[:, hof:hof + 512],
                                         in1=arena[:, hof:hof + 512], op=ALU.mult)
                    vector.reduce_sum(
                        out=rst[:, half * 8:(half + 1) * 8],
                        in_=arena[:, 1024 + hof:1024 + hof + 512].rearrange("p (h w) -> p h w", w=64),
                        axis=mybir.AxisListType.X).then_inc(s_eps, 1)
                if t >= 1:
                    vector.wait_ge(s_S, t)
                    vector.tensor_tensor(out=S_acc[0:64, 0:512], in0=S_acc[0:64, 0:512],
                                         in1=ps_S[0][0:64, :], op=ALU.add).then_inc(s_sacc, 1)
                    vector.tensor_tensor(out=S_acc[0:64, 512:1024], in0=S_acc[0:64, 512:1024],
                                         in1=ps_S[1][0:64, :], op=ALU.add).then_inc(s_sacc, 1)
                for half in range(2):
                    hof = half * 512
                    vector.wait_ge(s_exp2, 2 * t + half + 1)
                    for h in range(8):
                        gh = half * 8 + h
                        ins = vector.tensor_scalar(
                            out=kbf[:, gh * 64:(gh + 1) * 64],
                            in0=arena[:, hof + h * 64:hof + (h + 1) * 64],
                            scalar1=rst[:, gh:gh + 1], scalar2=None, op0=ALU.mult)
                    ins.then_inc(s_kbf, 1)
            vector.wait_ge(s_S, NSUB)
            vector.tensor_tensor(out=S_acc[0:64, 0:512], in0=S_acc[0:64, 0:512],
                                 in1=ps_S[0][0:64, :], op=ALU.add).then_inc(s_sacc, 1)
            vector.tensor_tensor(out=S_acc[0:64, 512:1024], in0=S_acc[0:64, 512:1024],
                                 in1=ps_S[1][0:64, :], op=ALU.add).then_inc(s_sacc, 1)
            vector.wait_ge(s_dma, 32)
            vector.memset(Sblk[:], 0.0).then_inc(s_sev, 1)

            if not phase_b:
                return
            for tt in range(NTB):
                xof = 0
                if tt >= 1:
                    vector.wait_ge(s_qmm, NCH * tt)
                vector.tensor_copy(
                    out=xbf[:, 0:2048].rearrange("p (c n) -> p c n", n=TB),
                    in_=X3(tt * TB, TB)).then_inc(s_cast, 1)
                for j in range(NCH):
                    k = NCH * tt + j
                    pp = j % 2
                    vector.wait_ge(s_qmm, k + 1)
                    if k >= 2:
                        vector.wait_ge(s_exq, k - 1)
                    vector.tensor_scalar(out=arena[:, pp * TB:(pp + 1) * TB],
                                         in0=ps_k[pp][:, 0:TB], scalar1=bqc[:, j:j + 1],
                                         scalar2=0.0, op0=ALU.add,
                                         op1=ALU.min).then_inc(s_q1, 1)
                    vector.tensor_scalar(out=arena[:, 1024:1024 + TB],
                                         in0=ps_k[pp][:, 0:TB], scalar1=bqc[:, j:j + 1],
                                         scalar2=0.0, op0=ALU.add,
                                         op1=ALU.max).then_inc(s_qfree, 1)
                    vector.wait_ge(s_exq, k + 1)
                    if tt >= 1:
                        vector.wait_ge(s_omm, NCH * tt)
                    vector.tensor_tensor(out=qa[:, j * TB:(j + 1) * TB],
                                         in0=arena[:, 1024:1024 + TB],
                                         in1=arena[:, 512 + pp * TB:512 + (pp + 1) * TB],
                                         op=ALU.add).then_inc(s_qa, 1)
                if tt >= 1:
                    vector.wait_ge(s_sq, 16 * tt)
                    vector.wait_ge(s_statm, 16 * tt)
                for o in range(NCH):
                    k = NCH * tt + o
                    vector.wait_ge(s_omm, k + 1)
                    vector.scalar_tensor_tensor(
                        out=y_sb[:, o * TB:(o + 1) * TB], in0=ps_S[o % 2][:, 0:TB],
                        scalar=boc[:, o:o + 1], in1=Xtile(o, tt * TB, TB),
                        op0=ALU.add, op1=ALU.add).then_inc(s_yev, 1)
                for rnd in range(2):
                    vector.wait_ge(s_statm, 16 * tt + 8 * rnd + 8)
                    vector.tensor_scalar(out=rows[0:1, 0:TB], in0=ps_stat[0:1, 0:TB],
                                         scalar1=1.0 / D, scalar2=None, op0=ALU.mult)
                    vector.tensor_scalar(out=rows[0:1, TB:2 * TB], in0=ps_stat[0:1, TB:2 * TB],
                                         scalar1=1.0 / D, scalar2=None,
                                         op0=ALU.mult).then_inc(s_rows, 1)
                    vector.tensor_tensor(out=rows[0:1, 2 * TB:3 * TB], in0=rows[0:1, 0:TB],
                                         in1=rows[0:1, 0:TB], op=ALU.mult)
                    vector.tensor_tensor(out=rows[0:1, 2 * TB:3 * TB], in0=rows[0:1, TB:2 * TB],
                                         in1=rows[0:1, 2 * TB:3 * TB],
                                         op=ALU.subtract).then_inc(s_eps2, 1)
                    vector.wait_ge(s_exp2, 2 * NSUB + 2 * tt + rnd + 1)
                    vector.tensor_tensor(out=rows[0:1, 3 * TB:4 * TB], in0=rows[0:1, 0:TB],
                                         in1=rows[0:1, 2 * TB:3 * TB], op=ALU.mult)
                    vector.tensor_scalar(out=rows[0:1, 3 * TB:4 * TB],
                                         in0=rows[0:1, 3 * TB:4 * TB], scalar1=-1.0,
                                         scalar2=None, op0=ALU.mult).then_inc(s_arow, 1)
                    vector.wait_ge(s_bc, 2 * tt + rnd + 1)
                    for c in range(NCH):
                        if rnd == 0:
                            vector.tensor_tensor(out=arena[:, 1280:1280 + TB],
                                                 in0=y_sb[:, c * TB:(c + 1) * TB],
                                                 in1=ps_bc[:, 0:TB], op=ALU.mult)
                            vector.tensor_scalar(out=arena[:, 1536:1536 + TB],
                                                 in0=ps_bc[:, TB:2 * TB],
                                                 scalar1=gmc[:, c:c + 1], scalar2=btc[:, c:c + 1],
                                                 op0=ALU.mult, op1=ALU.add)
                            vector.scalar_tensor_tensor(
                                out=arena[:, 1280:1280 + TB], in0=arena[:, 1280:1280 + TB],
                                scalar=gmc[:, c:c + 1], in1=arena[:, 1536:1536 + TB],
                                op0=ALU.mult, op1=ALU.add)
                            ins = vector.tensor_tensor(
                                out=y_sb[:, c * TB:(c + 1) * TB],
                                in0=arena[:, 1280:1280 + TB],
                                in1=Xtile(c, tt * TB, TB), op=ALU.add)
                            ins.then_inc(s_ap1, 1)
                        else:
                            vector.tensor_tensor(out=arena[:, 1280:1280 + TB],
                                                 in0=y_sb[:, c * TB:(c + 1) * TB],
                                                 in1=ps_bc[:, 0:TB], op=ALU.mult)
                            ins = vector.tensor_tensor(
                                out=Xtile(c, tt * TB, TB),
                                in0=arena[:, 1280:1280 + TB],
                                in1=ps_bc[:, TB:2 * TB], op=ALU.add)
                            ins.then_inc(s_ap2, 1)

        @block.scalar
        def _(scalar):
            for t in range(NSUB):
                for half in range(2):
                    hof = half * 512
                    scalar.wait_ge(s_dmin, 2 * t + half + 1)
                    scalar.activation(out=arena[:, 1024 + hof:1024 + hof + 512],
                                      in_=arena[:, hof:hof + 512],
                                      func=AF.Exp).then_inc(s_exp, 1)
                for half in range(2):
                    scalar.wait_ge(s_kv, 4 * t + 2 + half + 1)
                    if t >= 1:
                        scalar.wait_ge(s_S, t)
                    scalar.copy(out=vbf[:, half * 512:(half + 1) * 512],
                                in_=ps_v[half][:]).then_inc(s_vfree, 1)
                for half in range(2):
                    scalar.wait_ge(s_eps, 2 * t + half + 1)
                    scalar.activation(out=rst[:, half * 8:(half + 1) * 8],
                                      in_=rst[:, half * 8:(half + 1) * 8],
                                      func=AF.Ln, bias=cstc[:, 0:1])
                    scalar.activation(out=rst[:, half * 8:(half + 1) * 8],
                                      in_=rst[:, half * 8:(half + 1) * 8],
                                      func=AF.Exp, scale=cstc[:, 2:3]).then_inc(s_exp2, 1)
            if not phase_b:
                return
            for tt in range(NTB):
                for j in range(NCH):
                    k = NCH * tt + j
                    scalar.wait_ge(s_q1, k + 1)
                    if k >= 1:
                        scalar.wait_ge(s_qa, k)
                    scalar.activation(out=arena[:, 512 + (j % 2) * TB:512 + (j % 2 + 1) * TB],
                                      in_=arena[:, (j % 2) * TB:(j % 2 + 1) * TB],
                                      func=AF.Exp).then_inc(s_exq, 1)
                for p in range(NCH):
                    k = NCH * tt + p
                    scalar.wait_ge(s_atmm, k + 1)
                    scalar.copy(out=qa[:, p * TB:(p + 1) * TB],
                                in_=ps_v[p % 2][:, 0:TB]).then_inc(s_atev, 1)
                for rnd in range(2):
                    ready = s_yev if rnd == 0 else s_ap1
                    for c in range(NCH):
                        scalar.wait_ge(ready, NCH * tt + c + 1)
                        kk = 16 * tt + 8 * rnd + c
                        if kk >= 2:
                            scalar.wait_ge(s_statm, kk - 1)
                        scalar.activation(out=y2[:, (c % 2) * TB:(c % 2 + 1) * TB],
                                          in_=y_sb[:, c * TB:(c + 1) * TB],
                                          func=AF.Square).then_inc(s_sq, 1)
                    scalar.wait_ge(s_eps2, 2 * tt + rnd + 1)
                    scalar.activation(out=rows[0:1, 2 * TB:3 * TB], in_=rows[0:1, 2 * TB:3 * TB],
                                      func=AF.Ln, bias=(cstc[0:1, 0:1] if rnd == 0 else cstc[0:1, 1:2]))
                    scalar.activation(out=rows[0:1, 2 * TB:3 * TB], in_=rows[0:1, 2 * TB:3 * TB],
                                      func=AF.Exp, scale=cstc[0:1, 2:3]).then_inc(s_exp2, 1)


def build_l0_attn(debug=False, nsub=32, phase_b=True):
    nc = bass.Bass("TRN2", target_bir_lowering=False, debug=False, detect_race_conditions=DETECT_RACES)
    xin = nc.dram_tensor("xin", [D, NT], F32, kind="ExternalInput").ap()
    xout = nc.dram_tensor("xout", [D, NT], F32, kind="ExternalOutput").ap()
    es_s = ExitStack()
    with es_s:
        emit_l0_attn(nc, es_s, "", xin, xout, debug=debug, nsub=nsub, phase_b=phase_b)
    return nc


def _l0_attn_inputs(inputs, core):
    b, h = core // 2, core % 2
    x = inputs["L0_attn_x"]
    xc = np.ascontiguousarray(x[b, h * NT:(h + 1) * NT, :].T).astype(np.float32)
    Wq, Wk, Wv, Wo = (inputs["L0_attn_Wq"], inputs["L0_attn_Wk"],
                      inputs["L0_attn_Wv"], inputs["L0_attn_Wo"])
    d = {
        "xin": xc,
        "wk_r": np.ascontiguousarray(Wk.reshape(NCH, 128, D)),
        "wv_r": np.ascontiguousarray(Wv.reshape(NCH, 128, D)),
        "wq_l": np.ascontiguousarray(
            Wq.reshape(NCH, 128, NCH, 128).transpose(1, 2, 0, 3).reshape(128, 8192)),
        "wo_l": np.ascontiguousarray(
            Wo.reshape(NCH, 128, NCH, 128).transpose(1, 2, 0, 3).reshape(128, 8192)),
        "bk_row": inputs["L0_attn_bk"].reshape(1, D),
        "bv_row": inputs["L0_attn_bv"].reshape(1, D),
        "bq_col": np.ascontiguousarray(inputs["L0_attn_bq"].reshape(NCH, 128).T),
        "bo_col": np.ascontiguousarray(inputs["L0_attn_bo"].reshape(NCH, 128).T),
        "gam_col": np.ascontiguousarray(inputs["L0_attn_gamma"].reshape(NCH, 128).T),
        "bet_col": np.ascontiguousarray(inputs["L0_attn_beta"].reshape(NCH, 128).T),
        "ones1": np.ones((1, 128), np.float32),
        "onesc": np.ones((128, 1), np.float32),
        "cst": np.tile(np.array([[EPS_IN, EPS_OUT, -0.5, 0.5]], np.float32), (128, 1)),
    }
    return {k: np.ascontiguousarray(np.asarray(v, dtype=np.float32)) for k, v in d.items()}


def kernel(**inputs):
    inputs = {k: np.asarray(v) for k, v in inputs.items()}
    cores = list(range(8))
    nc = build_fused()
    maps = []
    for c in cores:
        m = {}
        for pfx, fn in [("a0", _l0_attn_inputs), ("f0", _l0_ffn_inputs),
                        ("a1", _l1_attn_inputs), ("f1", _l1_ffn_inputs)]:
            for k, v in fn(inputs, c).items():
                m[pfx + k] = v
        m["xin"] = m.pop("a0xin")
        maps.append(m)
    res = run_bass_kernel_spmd(nc, maps, core_ids=cores)
    out = np.zeros((4, 8192, D), np.float32)
    for c in cores:
        b, h = c // 2, c % 2
        out[b, h * NT:(h + 1) * NT, :] = np.asarray(res.results[c]["xout"]).T
    return out


def kernel_unfused(**inputs):
    inputs = {k: np.asarray(v) for k, v in inputs.items()}
    cores = list(range(8))

    nc1 = build_l0_attn()
    maps = [_l0_attn_inputs(inputs, c) for c in cores]
    res = run_bass_kernel_spmd(nc1, maps, core_ids=cores)
    x = [np.asarray(res.results[c]["xout"]) for c in cores]

    nc2 = build_l0_ffn()
    maps = [_l0_ffn_inputs(inputs, c) for c in cores]
    for c in cores:
        maps[c]["xin"] = np.ascontiguousarray(x[c])
    res = run_bass_kernel_spmd(nc2, maps, core_ids=cores)
    x = [np.asarray(res.results[c]["xout"]) for c in cores]

    nc3 = build_l1_attn()
    maps = [_l1_attn_inputs(inputs, c) for c in cores]
    for c in cores:
        maps[c]["xin"] = np.ascontiguousarray(x[c])
    res = run_bass_kernel_spmd(nc3, maps, core_ids=cores)
    x = [np.asarray(res.results[c]["xout"]) for c in cores]

    nc4 = build_l1_ffn()
    maps = [_l1_ffn_inputs(inputs, c) for c in cores]
    for c in cores:
        maps[c]["xin"] = np.ascontiguousarray(x[c])
    res = run_bass_kernel_spmd(nc4, maps, core_ids=cores)
    x = [np.asarray(res.results[c]["xout"]) for c in cores]

    out = np.zeros((4, 8192, D), np.float32)
    for c in cores:
        b, h = c // 2, c % 2
        out[b, h * NT:(h + 1) * NT, :] = x[c].T
    return out




def emit_l0_ffn(nc, es_s, pfx, xin, xout, ne_used=6, debug=False, wbf=None, s_pc=None, pc_need=0):
    NE, NG, NJ = ne_used, 4, 32
    TT_ = 512
    NTT = NT // TT_
    LTB = 256
    NLT = NT // LTB
    NROUT = max(NE - 2, 0)

    if wbf is None:
        w1_l = nc.dram_tensor(pfx + "w1_l", [6, 128, NJ * NCH * 128], F32, kind="ExternalInput").ap()
        w2_l = nc.dram_tensor(pfx + "w2_l", [6, 128, NJ * NCH * 128], F32, kind="ExternalInput").ap()
    b1_c = nc.dram_tensor(pfx + "b1_c", [6, 128, NJ], F32, kind="ExternalInput").ap()
    b2_c = nc.dram_tensor(pfx + "b2_c", [6, 128, NCH], F32, kind="ExternalInput").ap()
    gw_r = nc.dram_tensor(pfx + "gw_r", [NCH, 128, NG], F32, kind="ExternalInput").ap()
    gb_row = nc.dram_tensor(pfx + "gb_row", [128, NG], F32, kind="ExternalInput").ap()
    ident = nc.dram_tensor(pfx + "ident", [128, 128], F32, kind="ExternalInput").ap()
    ones1 = nc.dram_tensor(pfx + "ones1", [1, 128], F32, kind="ExternalInput").ap()
    onesc = nc.dram_tensor(pfx + "onesc", [128, 1], F32, kind="ExternalInput").ap()
    cst = nc.dram_tensor(pfx + "cst", [128, 4], F32, kind="ExternalInput").ap()
    acc = nc.dram_tensor(pfx + "acc", [D, NT], F32)
    xbf_d = nc.dram_tensor(pfx + "xbf_d", [D, NT], BF16)
    coef_d = nc.dram_tensor(pfx + "coef_d", [128, 32 * NG], F32)
    if debug:
        d_acc = nc.dram_tensor("d_acc", [D, NT], F32, kind="ExternalOutput").ap()
        d_coef = nc.dram_tensor("d_coef", [128, 128], F32, kind="ExternalOutput").ap()
        d_gts = nc.dram_tensor("d_gts", [128, 128], F32, kind="ExternalOutput").ap()
        d_ctm = nc.dram_tensor("d_ctm", [128, 128], F32, kind="ExternalOutput").ap()

    es = ExitStack()
    sb = lambda n, s, d=F32: es.enter_context(nc.sbuf_tensor(pfx + n, s, d))
    ps = lambda n, s, d=F32: es.enter_context(nc.psum_tensor(pfx + n, s, d))
    sem = lambda n: es_s.enter_context(nc.semaphore(pfx + n))

    W1 = sb("W1", [128, NJ * NCH * 128], BF16)
    W2 = sb("W2", [128, NJ * NCH * 128], BF16)
    xbt = sb("xbt", [128, 2 * NCH * TT_], BF16)
    h1 = sb("h1", [128, NJ * TT_], BF16)
    stage = sb("stage", [128, 2048])
    yln = sb("yln", [128, NCH * LTB])
    gx = yln[:, 0:1024]
    y2 = sb("y2", [128, 2 * LTB])
    rows = sb("rows", [1, 4 * LTB])
    gts = sb("gts", [128, 32 * NG])
    gsc = sb("gsc", [128, 4 * 32])
    ctm = sb("ctm", [128, 32 * NG])
    coef_sb = sb("coef_sb", [128, TT_])
    crow = sb("crow", [1, TT_])
    b1a = sb("b1a", [128, 6 * NJ])
    b2a = sb("b2a", [128, 6 * NCH])
    gwr = sb("gwr", [128, NCH * NG], BF16)
    gbr = sb("gbr", [128, NG])
    id_sb = sb("id_sb", [128, 128])
    on1 = sb("on1", [1, 128]); onc = sb("onc", [128, 1]); cstc = sb("cstc", [128, 4])

    ps_h = [ps(f"ps_h{i}", [128, 512]) for i in range(2)]
    ps_o = [ps(f"ps_o{i}", [128, 512]) for i in range(4)]
    ps_c = ps("ps_c", [128, 512])
    ps_s = ps("ps_s", [128, 512])

    s_mi = sem("s_mi"); s_xb = sem("s_xb"); s_a0 = sem("s_a0"); s_gw = sem("s_gw")
    s_gxa = sem("s_gxa"); s_w2 = sem("s_w2")
    s_gx = sem("s_gx"); s_gmm = sem("s_gmm"); s_gev = sem("s_gev")
    s_gsub = sem("s_gsub"); s_gexp = sem("s_gexp"); s_gcoef = sem("s_gcoef")
    s_gtr = sem("s_gtr"); s_gtev = sem("s_gtev"); s_gd = sem("s_gd")
    s_w = sem("s_w"); s_xt = sem("s_xt"); s_hmm = sem("s_hmm"); s_hev = sem("s_hev")
    s_omm = sem("s_omm"); s_cd = sem("s_cd"); s_cmm = sem("s_cmm"); s_cev = sem("s_cev")
    s_sev = sem("s_sev"); s_ad = sem("s_ad")
    s_lnl = sem("s_lnl"); s_sq = sem("s_sq"); s_stm = sem("s_stm"); s_rw = sem("s_rw")
    s_var = sem("s_var"); s_ar = sem("s_ar"); s_ab = sem("s_ab"); s_bc = sem("s_bc")
    s_ap = sem("s_ap"); s_out = sem("s_out")

    N_ACC = NE * NTT * 2 * 4 * 16

    with es, nc.Block() as block:
        @block.sync
        def _(sync):
            sync.dma_start(out=b1a[:].rearrange("p (e n) -> p e n", n=NJ),
                           in_=b1_c.rearrange("e p n -> p e n")).then_inc(s_mi, 16)
            sync.dma_start(out=b2a[:].rearrange("p (e n) -> p e n", n=NCH),
                           in_=b2_c.rearrange("e p n -> p e n")).then_inc(s_mi, 16)
            sync.dma_start(out=gbr[:], in_=gb_row).then_inc(s_mi, 16)
            sync.dma_start(out=id_sb[:], in_=ident).then_inc(s_mi, 16)
            sync.dma_start(out=on1[:], in_=ones1).then_inc(s_mi, 16)
            sync.dma_start(out=onc[:], in_=onesc).then_inc(s_mi, 16)
            sync.dma_start(out=cstc[:], in_=cst).then_inc(s_mi, 16)
            sync.dma_start(out=acc.rearrange("(c p) n -> p c n", p=128),
                           in_=xin.rearrange("(c p) n -> p c n", p=128)).then_inc(s_a0, 16)
            sync.wait_ge(s_gtev, 1)
            sync.dma_start(out=coef_d.ap(), in_=ctm[:]).then_inc(s_gd, 16)

            def coef_rows(r):
                for t in range(NTT):
                    kc = r * NTT + t
                    sync.wait_ge(s_gd, 16)
                    if kc >= 1:
                        sync.wait_ge(s_cmm, kc)
                    sync.dma_start(
                        out=crow[:],
                        in_=coef_d.ap().rearrange("a b -> (a b)")[None, :]
                            .rearrange("o (g e q) -> o g e q", e=NG, q=128)[:, 4 * t:4 * (t + 1), r]
                    ).then_inc(s_cd, 16)

            if wbf is None:
                for r in range(NROUT):
                    coef_rows(r)
            else:
                w1b, w2b = wbf
                for e in range(NE):
                    if e == 0:
                        sync.wait_ge(s_pc, pc_need)
                    else:
                        sync.wait_ge(s_hmm, NJ * NTT * e)
                    sync.dma_start(out=W1[:], in_=w1b[e]).then_inc(s_w, 16)
                    if e >= 1:
                        sync.wait_ge(s_omm, 2 * NTT * e)
                    sync.dma_start(out=W2[:], in_=w2b[e]).then_inc(s_w, 16)
                    if e >= 2:
                        coef_rows(e - 2)
            for tt in range(NLT):
                sync.wait_ge(s_ad, N_ACC)
                if tt >= 1:
                    sync.wait_ge(s_ap, NCH * tt)
                    sync.wait_ge(s_out, 16 * tt)
                sync.dma_start(
                    out=yln[:].rearrange("p (c n) -> p c n", n=LTB),
                    in_=acc.rearrange("(c p) n -> p c n", p=128)[:, :, tt * LTB:(tt + 1) * LTB]
                ).then_inc(s_lnl, 16)
                sync.wait_ge(s_ap, NCH * (tt + 1))
                sync.dma_start(
                    out=xout.rearrange("(c p) n -> p c n", p=128)[:, :, tt * LTB:(tt + 1) * LTB],
                    in_=yln[:].rearrange("p (c n) -> p c n", n=LTB)
                ).then_inc(s_out, 16)
            sync.wait_ge(s_out, 16 * NLT)
            if debug:
                sync.dma_start(out=d_acc, in_=acc.ap()).then_inc(s_out, 16)
                sync.dma_start(out=d_coef, in_=coef_d.ap()[:, 0:128]).then_inc(s_out, 16)
                sync.wait_ge(s_gcoef, 1)
                sync.dma_start(out=d_gts, in_=gts[:]).then_inc(s_out, 16)
                sync.dma_start(out=d_ctm, in_=ctm[:]).then_inc(s_out, 16)
                sync.wait_ge(s_out, 16 * NLT + 64)

        @block.gpsimd
        def _(gpsimd):
            for cc in range(NCH):
                gpsimd.dma_start(
                    out=W2[:].rearrange("p (c n) -> p c n", n=NT)[:, cc:cc + 1, :],
                    in_=xin.rearrange("(c p) n -> p c n", p=128)[:, cc:cc + 1, :]
                ).then_inc(s_gxa, 16)
            gpsimd.dma_start(out=gwr[:].rearrange("p (c n) -> p c n", n=NG),
                             in_=gw_r.rearrange("c p n -> p c n")).then_inc(s_gw, 16)
            for sc in range(NTT):
                gpsimd.dma_start(out=xbf_d.ap()[:, sc * TT_:(sc + 1) * TT_],
                                 in_=xin[:, sc * TT_:(sc + 1) * TT_]).then_inc(s_xb, 16)
            gpsimd.dma_start(out=W1[:], in_=w1_l[0]).then_inc(s_w, 16)

            def xload(k2):
                if k2 >= NE * NTT:
                    return
                t2 = k2 % NTT
                gpsimd.wait_ge(s_xb, 16 * (t2 + 1))
                gpsimd.dma_start(
                    out=xbt[:, (t2 % 2) * NCH * TT_:((t2 % 2) + 1) * NCH * TT_]
                        .rearrange("p (c n) -> p c n", n=TT_),
                    in_=xbf_d.ap().rearrange("(c p) n -> p c n", p=128)[:, :, t2 * TT_:(t2 + 1) * TT_]
                ).then_inc(s_xt, 16)

            xload(0)
            xload(1)
            for e in range(NE):
                if e >= 1:
                    gpsimd.wait_ge(s_hmm, NJ * NTT * e)
                    gpsimd.dma_start(out=W1[:], in_=w1_l[e]).then_inc(s_w, 16)
                    gpsimd.wait_ge(s_omm, 2 * NTT * e)
                    gpsimd.dma_start(out=W2[:], in_=w2_l[e]).then_inc(s_w2, 16)
                for t in range(NTT):
                    k = e * NTT + t
                    if k == 0:
                        gpsimd.wait_ge(s_gmm, 1)
                        gpsimd.dma_start(out=W2[:], in_=w2_l[0]).then_inc(s_w2, 16)
                        gpsimd.wait_ge(s_a0, 16)
                    gpsimd.wait_ge(s_hmm, NJ * (k + 1))
                    xload(k + 2)
                    for og in range(2):
                        kk = k * 2 + og
                        for oo in range(4):
                            gpsimd.wait_ge(s_sev, 4 * kk + oo + 1)
                            gpsimd.dma_start(
                                out=acc.rearrange("(c p) n -> p c n", p=128)[:, og * 4 + oo:og * 4 + oo + 1, t * TT_:(t + 1) * TT_],
                                in_=stage[:, oo * 512:(oo + 1) * 512].rearrange("p (c n) -> p c n", n=TT_),
                                accum_op=ALU.add).then_inc(s_ad, 16)

        @block.tensor
        def _(tensor):
            tensor.wait_ge(s_mi, 7 * 16)
            tensor.wait_ge(s_gw, 16)
            tensor.wait_ge(s_gxa, 16 * NCH)
            for g in range(32):
                for cd in range(NCH):
                    ins = tensor.matmul(ps_s[0:128, g * NG:(g + 1) * NG],
                                        W2[:, cd * NT + g * 128:cd * NT + (g + 1) * 128],
                                        gwr[:, cd * NG:(cd + 1) * NG],
                                        start=(cd == 0), stop=(cd == NCH - 1))
            ins.then_inc(s_gmm, 1)
            tensor.wait_ge(s_gcoef, 1)
            tensor.transpose(ps_c[:, 0:128], ctm[:, 0:128], id_sb[:]).then_inc(s_gtr, 1)
            for e in range(NE):
                for t in range(NTT):
                    k = e * NTT + t
                    par = t % 2
                    xof = par * NCH * TT_
                    tensor.wait_ge(s_w, 16 * (e + 1))
                    tensor.wait_ge(s_xt, 16 * (k + 1))
                    if e >= 2:
                        kc = (e - 2) * NTT + t
                        tensor.wait_ge(s_cd, 16 * (kc + 1))
                        if kc >= 1:
                            tensor.wait_ge(s_cev, kc)
                        else:
                            tensor.wait_ge(s_gtev, 1)
                        tensor.matmul(ps_c[:, 0:TT_], on1[0:1, :], crow[0:1, :],
                                      start=True, stop=True).then_inc(s_cmm, 1)
                    for j in range(NJ):
                        kj = k * NJ + j
                        if kj >= 2:
                            tensor.wait_ge(s_hev, kj - 1)
                        for cd in range(NCH):
                            ins = tensor.matmul(
                                ps_h[j % 2][:],
                                W1[:, (j * NCH + cd) * 128:(j * NCH + cd + 1) * 128],
                                xbt[:, xof + cd * TT_: xof + (cd + 1) * TT_],
                                start=(cd == 0), stop=(cd == NCH - 1))
                        ins.then_inc(s_hmm, 1)
                    if t == 0:
                        tensor.wait_ge(s_w2, 16 * (e + 1))
                    for og in range(2):
                        kk = k * 2 + og
                        if kk >= 1:
                            tensor.wait_ge(s_sev, 4 * kk)
                        for j in range(NJ):
                            tensor.wait_ge(s_hev, k * NJ + j + 1)
                            for oo in range(4):
                                o = og * 4 + oo
                                ins = tensor.matmul(
                                    ps_o[oo][:],
                                    W2[:, (j * NCH + o) * 128:(j * NCH + o + 1) * 128],
                                    h1[:, j * TT_:(j + 1) * TT_],
                                    start=(j == 0), stop=(j == NJ - 1))
                        ins.then_inc(s_omm, 1)
            for tt in range(NLT):
                tensor.wait_ge(s_lnl, 16 * (tt + 1))
                if tt >= 1:
                    tensor.wait_ge(s_rw, tt)
                for c in range(NCH):
                    tensor.matmul(ps_s[0:1, 0:LTB], onc[:, 0:1],
                                  yln[:, c * LTB:(c + 1) * LTB],
                                  start=(c == 0), stop=(c == NCH - 1))
                for c in range(NCH):
                    tensor.wait_ge(s_sq, NCH * tt + c + 1)
                    tensor.matmul(ps_s[0:1, LTB:2 * LTB], onc[:, 0:1],
                                  y2[:, (c % 2) * LTB:(c % 2 + 1) * LTB],
                                  start=(c == 0), stop=(c == NCH - 1)).then_inc(s_stm, 1)
                tensor.wait_ge(s_ab, tt + 1)
                if tt == 0:
                    tensor.wait_ge(s_cev, NROUT * NTT)
                else:
                    tensor.wait_ge(s_ap, NCH * tt)
                tensor.matmul(ps_c[:, 0:LTB], on1[0:1, :], rows[0:1, 2 * LTB:3 * LTB],
                              start=True, stop=True)
                tensor.matmul(ps_c[:, LTB:2 * LTB], on1[0:1, :], rows[0:1, 3 * LTB:4 * LTB],
                              start=True, stop=True).then_inc(s_bc, 1)

        @block.vector
        def _(vector):
            vector.wait_ge(s_gmm, 1)
            vector.wait_ge(s_mi, 7 * 16)
            vector.tensor_copy(out=gts[:], in_=ps_s[0:128, 0:32 * NG]).then_inc(s_gev, 1)
            vector.drain()
            g3 = gts[:].rearrange("p (g e) -> p g e", e=NG)
            a_, b_ = bass.broadcast_tensor_aps(g3, gbr[:].rearrange("p (o e) -> p o e", o=1))
            vector.tensor_tensor(out=g3, in0=a_, in1=b_, op=ALU.add)
            vector.drain()
            rmax = gsc[:, 0:32]; rsum = gsc[:, 32:64]
            cnt = gsc[:, 64:96]; gtmp = gsc[:, 96:128]
            vector.reduce_max(out=rmax, in_=g3, axis=mybir.AxisListType.X)
            vector.drain()
            a_, b_ = bass.broadcast_tensor_aps(g3, rmax.rearrange("p (g o) -> p g o", o=1))
            vector.tensor_tensor(out=g3, in0=a_, in1=b_, op=ALU.subtract).then_inc(s_gsub, 1)
            vector.wait_ge(s_gexp, 1)
            vector.reduce_sum(out=rsum, in_=g3, axis=mybir.AxisListType.X)
            vector.drain()
            vector.reciprocal(out=rsum, in_=rsum)
            vector.drain()
            a_, b_ = bass.broadcast_tensor_aps(g3, rsum.rearrange("p (g o) -> p g o", o=1))
            vector.tensor_tensor(out=g3, in0=a_, in1=b_, op=ALU.mult)
            vector.drain()
            c3 = ctm[:].rearrange("p (g e) -> p g e", e=NG)
            for ei in range(NG):
                first = True
                for j in range(NG):
                    if j == ei:
                        continue
                    wj = g3[:, :, j:j + 1]
                    we = g3[:, :, ei:ei + 1]
                    if first:
                        vector.tensor_tensor(out=cnt.rearrange("p (g o) -> p g o", o=1),
                                             in0=wj, in1=we, op=ALU.is_gt)
                        first = False
                    else:
                        vector.tensor_tensor(out=gtmp.rearrange("p (g o) -> p g o", o=1),
                                             in0=wj, in1=we, op=ALU.is_gt)
                        vector.drain()
                        vector.tensor_tensor(out=cnt, in0=cnt, in1=gtmp, op=ALU.add)
                    vector.drain()
                vector.tensor_scalar(out=gtmp, in0=cnt, scalar1=1.5, scalar2=None,
                                     op0=ALU.is_le)
                vector.drain()
                ins = vector.tensor_tensor(out=c3[:, :, ei:ei + 1],
                                           in0=g3[:, :, ei:ei + 1],
                                           in1=gtmp.rearrange("p (g o) -> p g o", o=1),
                                           op=ALU.mult)
                vector.drain()
            ins.then_inc(s_gcoef, 1)
            vector.wait_ge(s_gtr, 1)
            vector.tensor_copy(out=ctm[:], in_=ps_c[:, 0:128]).then_inc(s_gtev, 1)
            for e in range(NE):
                for t in range(NTT):
                    k = e * NTT + t
                    if e >= 2:
                        kc = (e - 2) * NTT + t
                        vector.wait_ge(s_cmm, kc + 1)
                        vector.tensor_copy(out=coef_sb[:], in_=ps_c[:, 0:TT_]).then_inc(s_cev, 1)
                    for og in range(2):
                        kk = k * 2 + og
                        vector.wait_ge(s_omm, kk + 1)
                        for oo in range(4):
                            o = og * 4 + oo
                            if kk >= 1:
                                vector.wait_ge(s_ad, 16 * (4 * (kk - 1) + oo + 1))
                            if e < 2:
                                vector.tensor_scalar(
                                    out=stage[:, oo * 512:(oo + 1) * 512],
                                    in0=ps_o[oo][:], scalar1=b2a[:, e * NCH + o:e * NCH + o + 1],
                                    scalar2=cstc[:, 3:4], op0=ALU.add,
                                    op1=ALU.mult).then_inc(s_sev, 1)
                            else:
                                vector.scalar_tensor_tensor(
                                    out=stage[:, oo * 512:(oo + 1) * 512], in0=ps_o[oo][:],
                                    scalar=b2a[:, e * NCH + o:e * NCH + o + 1], in1=coef_sb[:],
                                    op0=ALU.add, op1=ALU.mult).then_inc(s_sev, 1)
            for tt in range(NLT):
                vector.wait_ge(s_stm, NCH * (tt + 1))
                vector.tensor_scalar(out=rows[0:1, 0:LTB], in0=ps_s[0:1, 0:LTB],
                                     scalar1=1.0 / D, scalar2=None, op0=ALU.mult)
                vector.tensor_scalar(out=rows[0:1, LTB:2 * LTB], in0=ps_s[0:1, LTB:2 * LTB],
                                     scalar1=1.0 / D, scalar2=None, op0=ALU.mult).then_inc(s_rw, 1)
                vector.tensor_tensor(out=rows[0:1, 2 * LTB:3 * LTB], in0=rows[0:1, 0:LTB],
                                     in1=rows[0:1, 0:LTB], op=ALU.mult)
                vector.tensor_tensor(out=rows[0:1, 2 * LTB:3 * LTB], in0=rows[0:1, LTB:2 * LTB],
                                     in1=rows[0:1, 2 * LTB:3 * LTB],
                                     op=ALU.subtract).then_inc(s_var, 1)
                vector.wait_ge(s_ar, tt + 1)
                vector.tensor_tensor(out=rows[0:1, 3 * LTB:4 * LTB], in0=rows[0:1, 0:LTB],
                                     in1=rows[0:1, 2 * LTB:3 * LTB], op=ALU.mult)
                vector.tensor_scalar(out=rows[0:1, 3 * LTB:4 * LTB],
                                     in0=rows[0:1, 3 * LTB:4 * LTB], scalar1=-1.0,
                                     scalar2=None, op0=ALU.mult).then_inc(s_ab, 1)
                vector.wait_ge(s_bc, tt + 1)
                for c in range(NCH):
                    vector.tensor_tensor(out=stage[:, 0:LTB],
                                         in0=yln[:, c * LTB:(c + 1) * LTB],
                                         in1=ps_c[:, 0:LTB], op=ALU.mult)
                    vector.tensor_tensor(out=yln[:, c * LTB:(c + 1) * LTB],
                                         in0=stage[:, 0:LTB],
                                         in1=ps_c[:, LTB:2 * LTB], op=ALU.add).then_inc(s_ap, 1)

        @block.scalar
        def _(scalar):
            scalar.wait_ge(s_gsub, 1)
            scalar.activation(out=gts[:], in_=gts[:], func=AF.Exp).then_inc(s_gexp, 1)
            for e in range(NE):
                for t in range(NTT):
                    k = e * NTT + t
                    for j in range(NJ):
                        kj = k * NJ + j
                        scalar.wait_ge(s_hmm, kj + 1)
                        if k >= 1 and j == 0:
                            scalar.wait_ge(s_omm, 2 * k)
                        scalar.activation(out=h1[:, j * TT_:(j + 1) * TT_],
                                          in_=ps_h[j % 2][:], func=AF.Relu,
                                          bias=b1a[:, e * NJ + j:e * NJ + j + 1]).then_inc(s_hev, 1)
            for tt in range(NLT):
                for c in range(NCH):
                    scalar.wait_ge(s_lnl, 16 * (tt + 1))
                    kk = NCH * tt + c
                    if kk >= 2:
                        scalar.wait_ge(s_stm, kk - 1)
                    scalar.activation(out=y2[:, (c % 2) * LTB:(c % 2 + 1) * LTB],
                                      in_=yln[:, c * LTB:(c + 1) * LTB],
                                      func=AF.Square).then_inc(s_sq, 1)
                scalar.wait_ge(s_var, tt + 1)
                scalar.activation(out=rows[0:1, 2 * LTB:3 * LTB], in_=rows[0:1, 2 * LTB:3 * LTB],
                                  func=AF.Ln, bias=cstc[0:1, 1:2])
                scalar.activation(out=rows[0:1, 2 * LTB:3 * LTB], in_=rows[0:1, 2 * LTB:3 * LTB],
                                  func=AF.Exp, scale=cstc[0:1, 2:3]).then_inc(s_ar, 1)


def build_l0_ffn(ne_used=6, debug=False):
    nc = bass.Bass("TRN2", target_bir_lowering=False, debug=False, detect_race_conditions=DETECT_RACES)
    xin = nc.dram_tensor("xin", [D, NT], F32, kind="ExternalInput").ap()
    xout = nc.dram_tensor("xout", [D, NT], F32, kind="ExternalOutput").ap()
    es_s = ExitStack()
    with es_s:
        emit_l0_ffn(nc, es_s, "", xin, xout, ne_used=ne_used, debug=debug)
    return nc


def _l0_ffn_inputs(inputs, core):
    b, h = core // 2, core % 2
    sw1, sw2 = inputs["L0_ffn_sw1"], inputs["L0_ffn_sw2"]
    rw1, rw2 = inputs["L0_ffn_rw1"], inputs["L0_ffn_rw2"]
    sb1, sb2 = inputs["L0_ffn_sb1"], inputs["L0_ffn_sb2"]
    rb1, rb2 = inputs["L0_ffn_rb1"], inputs["L0_ffn_rb2"]
    w1 = np.concatenate([sw1, rw1], 0)
    w2 = np.concatenate([sw2, rw2], 0)
    b1 = np.concatenate([sb1, rb1], 0)
    b2 = np.concatenate([sb2, rb2], 0)
    NJ = 32
    d = {
        "w1_l": w1.reshape(6, NCH, 128, NJ, 128).transpose(0, 2, 3, 1, 4).reshape(6, 128, NJ * NCH * 128),
        "w2_l": w2.reshape(6, NJ, 128, NCH, 128).transpose(0, 2, 1, 3, 4).reshape(6, 128, NJ * NCH * 128),
        "b1_c": b1.reshape(6, NJ, 128).transpose(0, 2, 1),
        "b2_c": b2.reshape(6, NCH, 128).transpose(0, 2, 1),
        "gw_r": inputs["L0_ffn_gate_w"].reshape(NCH, 128, 4),
        "gb_row": np.tile(inputs["L0_ffn_gate_b"].reshape(1, 4), (128, 1)),
        "ident": np.eye(128, dtype=np.float32),
        "ones1": np.ones((1, 128), np.float32),
        "onesc": np.ones((128, 1), np.float32),
        "cst": np.tile(np.array([[EPS_IN, EPS_OUT, -0.5, 0.5]], np.float32), (128, 1)),
    }
    return {k: np.ascontiguousarray(np.asarray(v, dtype=np.float32)) for k, v in d.items()}


def emit_l1_attn(nc, es_s, pfx, xin, xout, debug=False):
    TT_ = 512
    NTT = NT // TT_
    LTB = 256
    NLT = NT // LTB

    wf_l = nc.dram_tensor(pfx + "wf_l", [128, 16 * NCH * 128], F32, kind="ExternalInput").ap()
    emask = nc.dram_tensor(pfx + "emask", [128, 1], F32, kind="ExternalInput").ap()
    omask = nc.dram_tensor(pfx + "omask", [128, 1], F32, kind="ExternalInput").ap()
    sbias = nc.dram_tensor(pfx + "sbias", [128, 1], F32, kind="ExternalInput").ap()
    ones1 = nc.dram_tensor(pfx + "ones1", [1, 128], F32, kind="ExternalInput").ap()
    onesc = nc.dram_tensor(pfx + "onesc", [128, 1], F32, kind="ExternalInput").ap()
    cst = nc.dram_tensor(pfx + "cst", [128, 4], F32, kind="ExternalInput").ap()
    xbf_d = nc.dram_tensor(pfx + "xbf_d", [D, NT], BF16)
    cc_i = [nc.dram_tensor(f"{pfx}cc_i{c}", [128, 1], F32) for c in range(NCH)]
    cc_o = [nc.dram_tensor(f"{pfx}cc_o{c}", [128, 1], F32) for c in range(NCH)]
    if debug:
        d_f = nc.dram_tensor("d_f", [128, NT], F32, kind="ExternalOutput").ap()
        d_u = nc.dram_tensor("d_u", [128, NT], F32, kind="ExternalOutput").ap()
        d_fr = nc.dram_tensor("d_fr", [128, NT], F32, kind="ExternalOutput").ap()
        d_ur = nc.dram_tensor("d_ur", [128, NT], F32, kind="ExternalOutput").ap()

    es = ExitStack()
    sb = lambda n, s, d=F32: es.enter_context(nc.sbuf_tensor(pfx + n, s, d))
    ps = lambda n, s, d=F32: es.enter_context(nc.psum_tensor(pfx + n, s, d))
    sem = lambda n: es_s.enter_context(nc.semaphore(pfx + n))

    X = sb("X", [128, NCH * NT])
    f_sl = sb("f_sl", [128, NT])
    u_sl = sb("u_sl", [128, NT])
    xbf = sb("xbf", [128, 2 * NCH * TT_], BF16)
    wf = sb("wf", [128, 4 * NCH * 128], BF16)
    t1 = sb("t1", [128, TT_])
    t2 = sb("t2", [128, TT_])
    y2 = sb("y2", [128, 2 * LTB])
    rows = sb("rows", [1, 4 * LTB])
    carry = sb("carry", [128, 1])
    sin = sb("sin", [128, 1])
    emk = sb("emk", [128, 1]); omk = sb("omk", [128, 1]); sbi = sb("sbi", [128, 1])
    on1 = sb("on1", [1, 128]); onc = sb("onc", [128, 1]); cstc = sb("cstc", [128, 4])

    ps_k = [ps(f"ps_k{i}", [128, 512]) for i in range(2)]
    ps_h = [ps(f"ps_h{i}", [128, 512]) for i in range(2)]
    ps_st = ps("ps_st", [128, 512])
    ps_bc = ps("ps_bc", [128, 512])

    s_x = sem("s_x"); s_mi = sem("s_mi"); s_wf = sem("s_wf"); s_xb = sem("s_xb")
    s_xt = sem("s_xt"); s_kmm = sem("s_kmm")
    s_sig = sem("s_sig"); s_kfree = sem("s_kfree"); s_hxfree = sem("s_hxfree")
    s_uev = sem("s_uev"); s_scan = sem("s_scan")
    s_cci = sem("s_cci"); s_cc = sem("s_cc"); s_cco = sem("s_cco"); s_fix = sem("s_fix")
    s_lsq = sem("s_lsq"); s_stm = sem("s_stm"); s_rw = sem("s_rw"); s_var = sem("s_var")
    s_ar = sem("s_ar"); s_ab = sem("s_ab"); s_bc = sem("s_bc"); s_ap = sem("s_ap")
    s_out = sem("s_out"); s_dbg = sem("s_dbg")

    with es, nc.Block() as block:
        @block.sync
        def _(sync):
            sync.dma_start(out=X[:].rearrange("p (c n) -> p c n", n=NT),
                           in_=xin.rearrange("(c p) n -> p c n", p=128)).then_inc(s_x, 16)
            for apd, t in [(emask, emk), (omask, omk), (sbias, sbi),
                           (ones1, on1), (onesc, onc), (cst, cstc)]:
                sync.dma_start(out=t[:], in_=apd).then_inc(s_mi, 16)
            for c in range(NCH):
                if debug and c == NCH - 1:
                    sync.wait_ge(s_uev, NCH * NTT)
                    sync.dma_start(out=d_fr, in_=f_sl[:]).then_inc(s_dbg, 16)
                    sync.dma_start(out=d_ur, in_=u_sl[:]).then_inc(s_dbg, 16)
                sync.wait_ge(s_scan, c + 1)
                sync.dma_start(out=cc_i[c].ap(), in_=carry[:]).then_inc(s_cci, 16)
                sync.wait_ge(s_cc, c + 1)
                if c >= 1:
                    sync.wait_ge(s_fix, c)
                sync.dma_start(out=sin[:], in_=cc_o[c].ap()).then_inc(s_cco, 16)
            sync.wait_ge(s_ap, NCH * NLT)
            sync.dma_start(out=xout.rearrange("(c p) n -> p c n", p=128),
                           in_=X[:].rearrange("p (c n) -> p c n", n=NT)).then_inc(s_out, 16)
            nout = 16
            if debug:
                sync.dma_start(out=d_f, in_=f_sl[:]).then_inc(s_out, 16)
                sync.dma_start(out=d_u, in_=u_sl[:]).then_inc(s_out, 16)
                nout = 48
            sync.wait_ge(s_out, nout)

        @block.gpsimd
        def _(gpsimd):
            gpsimd.dma_start(out=xbf_d.ap(), in_=xin).then_inc(s_xb, 16)
            def load_wf(c):
                par = c % 2
                wof = par * 2 * NCH * 128
                if c >= 1:
                    gpsimd.wait_ge(s_wf, 16 * c)
                if c >= 2:
                    gpsimd.wait_ge(s_kmm, 16 * (c - 1))
                gpsimd.dma_start(out=wf[:, wof:wof + 2 * NCH * 128],
                                 in_=wf_l[:, c * 2048:(c + 1) * 2048]).then_inc(s_wf, 16)
            load_wf(0)
            load_wf(1)
            for c in range(NCH):
                gpsimd.wait_ge(s_cci, 16 * (c + 1))
                gpsimd.collective_compute(
                    "AllReduce", ALU.add,
                    replica_groups=[[0, 1], [2, 3], [4, 5], [6, 7]],
                    ins=[cc_i[c].ap().opt()], outs=[cc_o[c].ap().opt()],
                ).then_inc(s_cc)
                if c + 2 < NCH:
                    load_wf(c + 2)

        @block.tensor
        def _(tensor):
            tensor.wait_ge(s_x, 16)
            for c in range(NCH):
                par = c % 2
                wof = par * 2 * NCH * 128
                tensor.wait_ge(s_wf, 16 * (c + 1))
                for t in range(NTT):
                    kt = c * NTT + t
                    xof = (kt % 2) * NCH * TT_
                    tensor.wait_ge(s_xt, 16 * (kt + 1))
                    if kt >= 2:
                        tensor.wait_ge(s_kfree, kt - 1)
                    for cd in range(NCH):
                        ins = tensor.matmul(
                            ps_k[t % 2][:],
                            wf[:, wof + cd * 128:wof + (cd + 1) * 128],
                            xbf[:, xof + cd * TT_:xof + (cd + 1) * TT_],
                            start=(cd == 0), stop=(cd == NCH - 1))
                    ins.then_inc(s_kmm, 1)
                    if kt >= 2:
                        tensor.wait_ge(s_hxfree, kt - 1)
                    for cd in range(NCH):
                        ins = tensor.matmul(
                            ps_h[t % 2][:],
                            wf[:, wof + (NCH + cd) * 128:wof + (NCH + cd + 1) * 128],
                            xbf[:, xof + cd * TT_:xof + (cd + 1) * TT_],
                            start=(cd == 0), stop=(cd == NCH - 1))
                    ins.then_inc(s_kmm, 1)
            for tt in range(NLT):
                tensor.wait_ge(s_fix, NCH)
                if tt >= 1:
                    tensor.wait_ge(s_rw, tt)
                for c in range(NCH):
                    tensor.matmul(ps_st[0:1, 0:LTB], onc[:, 0:1],
                                  Xtile_g(X, c, tt * LTB, LTB),
                                  start=(c == 0), stop=(c == NCH - 1))
                for c in range(NCH):
                    tensor.wait_ge(s_lsq, NCH * tt + c + 1)
                    tensor.matmul(ps_st[0:1, LTB:2 * LTB], onc[:, 0:1],
                                  y2[:, (c % 2) * LTB:(c % 2 + 1) * LTB],
                                  start=(c == 0), stop=(c == NCH - 1)).then_inc(s_stm, 1)
                tensor.wait_ge(s_ab, tt + 1)
                if tt >= 1:
                    tensor.wait_ge(s_ap, NCH * tt)
                tensor.matmul(ps_bc[:, 0:LTB], on1[0:1, :], rows[0:1, 2 * LTB:3 * LTB],
                              start=True, stop=True)
                tensor.matmul(ps_bc[:, LTB:2 * LTB], on1[0:1, :], rows[0:1, 3 * LTB:4 * LTB],
                              start=True, stop=True).then_inc(s_bc, 1)

        @block.vector
        def _(vector):
            vector.wait_ge(s_x, 16)
            vector.wait_ge(s_mi, 96)
            for c in range(NCH):
                for t in range(NTT):
                    kt = c * NTT + t
                    T0 = t * TT_
                    vector.wait_ge(s_sig, 3 * kt + 3)
                    vector.tensor_scalar(out=t2[:], in0=t2[:], scalar1=0.5, scalar2=None,
                                         op0=ALU.min)
                    vector.scalar_tensor_tensor(out=t2[:], in0=ps_h[t % 2][:], scalar=0.0,
                                                in1=t2[:], op0=ALU.max,
                                                op1=ALU.add).then_inc(s_hxfree, 1)
                    vector.tensor_tensor(out=u_sl[:, T0:T0 + TT_], in0=t1[:], in1=t2[:],
                                         op=ALU.mult).then_inc(s_uev, 1)
                if debug and c == NCH - 1:
                    vector.wait_ge(s_dbg, 32)
                vector.tensor_tensor_scan(out=u_sl[:], data0=f_sl[:], data1=u_sl[:],
                                          initial=sbi[:, 0:1], op0=ALU.mult, op1=ALU.add)
                vector.drain()
                if c >= 1:
                    vector.wait_ge(s_cci, 16 * c)
                vector.tensor_scalar(out=carry[:], in0=u_sl[:, NT - 1:NT],
                                     scalar1=emk[:, 0:1], scalar2=None, op0=ALU.mult)
                vector.drain()
                ins = vector.tensor_tensor_scan(out=f_sl[:], data0=f_sl[:], data1=f_sl[:],
                                                initial=1.0, op0=ALU.mult, op1=ALU.bypass)
                ins.then_inc(s_scan, 1)
                vector.wait_ge(s_cco, 16 * (c + 1))
                vector.tensor_scalar(out=sin[:], in0=sin[:], scalar1=omk[:, 0:1],
                                     scalar2=None, op0=ALU.mult)
                vector.drain()
                vector.scalar_tensor_tensor(out=u_sl[:], in0=f_sl[:], scalar=sin[:, 0:1],
                                            in1=u_sl[:], op0=ALU.mult, op1=ALU.add)
                vector.tensor_tensor(out=X[:, c * NT:(c + 1) * NT],
                                     in0=X[:, c * NT:(c + 1) * NT], in1=u_sl[:],
                                     op=ALU.add).then_inc(s_fix, 1)
            for tt in range(NLT):
                vector.wait_ge(s_stm, NCH * (tt + 1))
                vector.tensor_scalar(out=rows[0:1, 0:LTB], in0=ps_st[0:1, 0:LTB],
                                     scalar1=1.0 / D, scalar2=None, op0=ALU.mult)
                vector.tensor_scalar(out=rows[0:1, LTB:2 * LTB], in0=ps_st[0:1, LTB:2 * LTB],
                                     scalar1=1.0 / D, scalar2=None, op0=ALU.mult).then_inc(s_rw, 1)
                vector.tensor_tensor(out=rows[0:1, 2 * LTB:3 * LTB], in0=rows[0:1, 0:LTB],
                                     in1=rows[0:1, 0:LTB], op=ALU.mult)
                vector.tensor_tensor(out=rows[0:1, 2 * LTB:3 * LTB], in0=rows[0:1, LTB:2 * LTB],
                                     in1=rows[0:1, 2 * LTB:3 * LTB],
                                     op=ALU.subtract).then_inc(s_var, 1)
                vector.wait_ge(s_ar, tt + 1)
                vector.tensor_tensor(out=rows[0:1, 3 * LTB:4 * LTB], in0=rows[0:1, 0:LTB],
                                     in1=rows[0:1, 2 * LTB:3 * LTB], op=ALU.mult)
                vector.tensor_scalar(out=rows[0:1, 3 * LTB:4 * LTB],
                                     in0=rows[0:1, 3 * LTB:4 * LTB], scalar1=-1.0,
                                     scalar2=None, op0=ALU.mult).then_inc(s_ab, 1)
                vector.wait_ge(s_bc, tt + 1)
                for c in range(NCH):
                    vector.tensor_tensor(out=t1[:, 0:LTB],
                                         in0=Xtile_g(X, c, tt * LTB, LTB),
                                         in1=ps_bc[:, 0:LTB], op=ALU.mult)
                    vector.tensor_tensor(out=Xtile_g(X, c, tt * LTB, LTB),
                                         in0=t1[:, 0:LTB],
                                         in1=ps_bc[:, LTB:2 * LTB], op=ALU.add).then_inc(s_ap, 1)

        @block.scalar
        def _(scalar):
            scalar.wait_ge(s_mi, 96)

            def xdma(kt):
                if kt >= 64:
                    return
                c2, t2 = kt // NTT, kt % NTT
                if kt >= 2:
                    scalar.wait_ge(s_kmm, 2 * (kt - 1))
                scalar.dma_start(
                    out=xbf[:, (kt % 2) * NCH * TT_:((kt % 2) + 1) * NCH * TT_]
                        .rearrange("p (c n) -> p c n", n=TT_),
                    in_=xbf_d.ap().rearrange("(c p) n -> p c n", p=128)[:, :, t2 * TT_:(t2 + 1) * TT_]
                ).then_inc(s_xt, 16)

            scalar.wait_ge(s_xb, 16)
            xdma(0)
            xdma(1)
            for c in range(NCH):
                for t in range(NTT):
                    kt = c * NTT + t
                    T0 = t * TT_
                    xdma(kt + 2)
                    if t == 0 and c >= 1:
                        scalar.wait_ge(s_fix, c)
                    scalar.wait_ge(s_kmm, 2 * kt + 1)
                    scalar.activation(out=f_sl[:, T0:T0 + TT_], in_=ps_k[t % 2][:],
                                      func=AF.Sigmoid, scale=-1.0)
                    if kt >= 1:
                        scalar.wait_ge(s_uev, kt)
                    scalar.activation(out=t1[:], in_=ps_k[t % 2][:],
                                      func=AF.Sigmoid).then_inc(s_kfree, 1)
                    scalar.wait_ge(s_kmm, 2 * kt + 2)
                    scalar.activation(out=t2[:], in_=ps_h[t % 2][:],
                                      func=AF.Sigmoid).then_inc(s_sig, 3)
            for tt in range(NLT):
                scalar.wait_ge(s_fix, NCH)
                for c in range(NCH):
                    kk = NCH * tt + c
                    if kk >= 2:
                        scalar.wait_ge(s_stm, kk - 1)
                    scalar.activation(out=y2[:, (c % 2) * LTB:(c % 2 + 1) * LTB],
                                      in_=Xtile_g(X, c, tt * LTB, LTB),
                                      func=AF.Square).then_inc(s_lsq, 1)
                scalar.wait_ge(s_var, tt + 1)
                scalar.activation(out=rows[0:1, 2 * LTB:3 * LTB], in_=rows[0:1, 2 * LTB:3 * LTB],
                                  func=AF.Ln, bias=cstc[0:1, 1:2])
                scalar.activation(out=rows[0:1, 2 * LTB:3 * LTB], in_=rows[0:1, 2 * LTB:3 * LTB],
                                  func=AF.Exp, scale=cstc[0:1, 2:3]).then_inc(s_ar, 1)


def build_l1_attn(debug=False):
    nc = bass.Bass("TRN2", target_bir_lowering=False, debug=False, detect_race_conditions=DETECT_RACES)
    xin = nc.dram_tensor("xin", [D, NT], F32, kind="ExternalInput").ap()
    xout = nc.dram_tensor("xout", [D, NT], F32, kind="ExternalOutput").ap()
    es_s = ExitStack()
    with es_s:
        emit_l1_attn(nc, es_s, "", xin, xout, debug=debug)
    return nc


def Xtile_g(X, c, col0, w):
    return X[:, c * NT + col0: c * NT + col0 + w]


def _l1_attn_inputs(inputs, core):
    Wf = inputs["L1_attn_W_f"]
    odd = core % 2
    d = {
        "wf_l": Wf.reshape(16, 128, 8, 128).transpose(3, 0, 2, 1)[
            :, [c // 2 + 8 * (c % 2) for c in range(16)]].reshape(128, 16384),
        "emask": np.full((128, 1), 0.0 if odd else 1.0, np.float32),
        "omask": np.full((128, 1), 1.0 if odd else 0.0, np.float32),
        "sbias": np.full((128, 1), 0.0 if odd else 0.5, np.float32),
        "ones1": np.ones((1, 128), np.float32),
        "onesc": np.ones((128, 1), np.float32),
        "cst": np.tile(np.array([[EPS_IN, EPS_OUT, -0.5, 0.5]], np.float32), (128, 1)),
    }
    return {k: np.ascontiguousarray(np.asarray(v, dtype=np.float32)) for k, v in d.items()}


def emit_l1_ffn(nc, es_s, pfx, xin, xout, ne_used=10, debug=False, wbf=None, s_pc=None, pc_need=0):
    NE, NG, NJ = ne_used, 8, 16
    TT_ = 512
    NTT = NT // TT_
    LTB = 512
    NLT = NT // LTB
    NROUT = max(NE - 2, 0)

    if wbf is None:
        w1_l = nc.dram_tensor(pfx + "w1_l", [10, 128, NJ * NCH * 128], F32, kind="ExternalInput").ap()
        w2_l = nc.dram_tensor(pfx + "w2_l", [10, 128, NJ * NCH * 128], F32, kind="ExternalInput").ap()
        w3_l = nc.dram_tensor(pfx + "w3_l", [10, 128, NJ * NCH * 128], F32, kind="ExternalInput").ap()
    b1_c = nc.dram_tensor(pfx + "b1_c", [10, 128, NJ], F32, kind="ExternalInput").ap()
    b2_c = nc.dram_tensor(pfx + "b2_c", [10, 128, NJ], F32, kind="ExternalInput").ap()
    b3_c = nc.dram_tensor(pfx + "b3_c", [10, 128, NCH], F32, kind="ExternalInput").ap()
    gw_r = nc.dram_tensor(pfx + "gw_r", [NCH, 128, NG], F32, kind="ExternalInput").ap()
    gb_row = nc.dram_tensor(pfx + "gb_row", [128, NG], F32, kind="ExternalInput").ap()
    ident = nc.dram_tensor(pfx + "ident", [128, 128], F32, kind="ExternalInput").ap()
    ones1 = nc.dram_tensor(pfx + "ones1", [1, 128], F32, kind="ExternalInput").ap()
    onesc = nc.dram_tensor(pfx + "onesc", [128, 1], F32, kind="ExternalInput").ap()
    cst = nc.dram_tensor(pfx + "cst", [128, 4], F32, kind="ExternalInput").ap()
    acc = nc.dram_tensor(pfx + "acc", [D, NT], F32)
    xbf_d = nc.dram_tensor(pfx + "xbf_d", [D, NT], BF16)
    coef_d = nc.dram_tensor(pfx + "coef_d", [128, 256], F32)
    if debug:
        d_acc = nc.dram_tensor("d_acc", [D, NT], F32, kind="ExternalOutput").ap()
        d_coef = nc.dram_tensor("d_coef", [128, 256], F32, kind="ExternalOutput").ap()

    es = ExitStack()
    sb = lambda n, s, d=F32: es.enter_context(nc.sbuf_tensor(pfx + n, s, d))
    ps = lambda n, s, d=F32: es.enter_context(nc.psum_tensor(pfx + n, s, d))
    sem = lambda n: es_s.enter_context(nc.semaphore(pfx + n))

    W1 = sb("W1", [128, NJ * NCH * 128], BF16)
    W2 = sb("W2", [128, NJ * NCH * 128], BF16)
    W3 = sb("W3", [128, NJ * NCH * 128], BF16)
    xbt = sb("xbt", [128, 2 * NCH * TT_], BF16)
    h1 = sb("h1", [128, NJ * TT_], BF16)
    tsil = sb("tsil", [128, TT_])
    stage = sb("stage", [128, 4096])
    yln = sb("yln", [128, NCH * LTB])
    gx = yln[:, 0:1024]
    y2 = sb("y2", [128, 2 * LTB])
    rows = sb("rows", [1, 4 * LTB])
    gts = sb("gts", [128, 32 * NG])
    gsc = sb("gsc", [128, 4 * 32])
    ctm = sb("ctm", [128, 32 * NG])
    coef_sb = sb("coef_sb", [128, TT_])
    crow = sb("crow", [1, TT_])
    b1a = sb("b1a", [128, 10 * NJ])
    b2a = sb("b2a", [128, 10 * NJ])
    b3a = sb("b3a", [128, 10 * NCH])
    gwr = sb("gwr", [128, NCH * NG], BF16)
    gbr = sb("gbr", [128, NG])
    id_sb = sb("id_sb", [128, 128])
    on1 = sb("on1", [1, 128]); onc = sb("onc", [128, 1]); cstc = sb("cstc", [128, 4])

    ps_p1 = [ps(f"ps_p1{i}", [128, 512]) for i in range(2)]
    ps_p2 = [ps(f"ps_p2{i}", [128, 512]) for i in range(2)]
    ps_o = [ps(f"ps_o{i}", [128, 512]) for i in range(4)]

    s_mi = sem("s_mi"); s_xb = sem("s_xb"); s_a0 = sem("s_a0"); s_gw = sem("s_gw")
    s_gxa = sem("s_gxa"); s_w2 = sem("s_w2"); s_w3 = sem("s_w3")
    s_gx = sem("s_gx"); s_gmm = sem("s_gmm"); s_gev = sem("s_gev")
    s_gsub = sem("s_gsub"); s_gexp = sem("s_gexp"); s_gcoef = sem("s_gcoef")
    s_gtr = sem("s_gtr"); s_gtev = sem("s_gtev"); s_gd = sem("s_gd")
    s_w = sem("s_w"); s_xt = sem("s_xt")
    s_pmm = sem("s_pmm"); s_sil = sem("s_sil"); s_glu = sem("s_glu")
    s_omm = sem("s_omm"); s_cd = sem("s_cd"); s_cmm = sem("s_cmm"); s_cev = sem("s_cev")
    s_sev = sem("s_sev"); s_ad = sem("s_ad")
    s_lnl = sem("s_lnl"); s_sq = sem("s_sq"); s_stm = sem("s_stm"); s_rw = sem("s_rw")
    s_var = sem("s_var"); s_ar = sem("s_ar"); s_ab = sem("s_ab"); s_bc = sem("s_bc")
    s_ap = sem("s_ap"); s_out = sem("s_out")

    N_ACC = NE * NTT * 2 * 4 * 16

    with es, nc.Block() as block:
        @block.sync
        def _(sync):
            sync.dma_start(out=b1a[:].rearrange("p (e n) -> p e n", n=NJ),
                           in_=b1_c.rearrange("e p n -> p e n")).then_inc(s_mi, 16)
            sync.dma_start(out=b2a[:].rearrange("p (e n) -> p e n", n=NJ),
                           in_=b2_c.rearrange("e p n -> p e n")).then_inc(s_mi, 16)
            sync.dma_start(out=b3a[:].rearrange("p (e n) -> p e n", n=NCH),
                           in_=b3_c.rearrange("e p n -> p e n")).then_inc(s_mi, 16)
            sync.dma_start(out=gbr[:], in_=gb_row).then_inc(s_mi, 16)
            sync.dma_start(out=id_sb[:], in_=ident).then_inc(s_mi, 16)
            sync.dma_start(out=on1[:], in_=ones1).then_inc(s_mi, 16)
            sync.dma_start(out=onc[:], in_=onesc).then_inc(s_mi, 16)
            sync.dma_start(out=cstc[:], in_=cst).then_inc(s_mi, 16)
            sync.dma_start(out=acc.rearrange("(c p) n -> p c n", p=128),
                           in_=xin.rearrange("(c p) n -> p c n", p=128)).then_inc(s_a0, 16)
            sync.wait_ge(s_gtev, 2)
            sync.dma_start(out=coef_d.ap(), in_=ctm[:, 0:256]).then_inc(s_gd, 16)

            def coef_rows(r):
                for t in range(NTT):
                    kc = r * NTT + t
                    sync.wait_ge(s_gd, 16)
                    if kc >= 1:
                        sync.wait_ge(s_cmm, kc)
                    tl = t % 4
                    off = 128 if t >= 4 else 0
                    sync.dma_start(
                        out=crow[:],
                        in_=coef_d.ap().rearrange("a b -> (a b)")[None, :]
                            .rearrange("o (g e q) -> o g e q", e=NG, q=256)
                            [:, 4 * tl:4 * (tl + 1), r, off:off + 128]
                    ).then_inc(s_cd, 16)

            if wbf is None:
                for r in range(NROUT):
                    coef_rows(r)
            else:
                w1b, w2b, w3b = wbf
                for e in range(NE):
                    if e == 0:
                        sync.wait_ge(s_pc, pc_need)
                    else:
                        sync.wait_ge(s_pmm, 2 * NJ * NTT * e)
                    sync.dma_start(out=W1[:], in_=w1b[e]).then_inc(s_w, 16)
                    sync.dma_start(out=W2[:], in_=w2b[e]).then_inc(s_w, 16)
                    if e >= 1:
                        sync.wait_ge(s_omm, 2 * NTT * e)
                    sync.dma_start(out=W3[:], in_=w3b[e]).then_inc(s_w, 16)
                    if e >= 2:
                        coef_rows(e - 2)
            for tt in range(NLT):
                sync.wait_ge(s_ad, N_ACC)
                if tt >= 1:
                    sync.wait_ge(s_ap, NCH * tt)
                    sync.wait_ge(s_out, 16 * tt)
                sync.dma_start(
                    out=yln[:].rearrange("p (c n) -> p c n", n=LTB),
                    in_=acc.rearrange("(c p) n -> p c n", p=128)[:, :, tt * LTB:(tt + 1) * LTB]
                ).then_inc(s_lnl, 16)
                sync.wait_ge(s_ap, NCH * (tt + 1))
                sync.dma_start(
                    out=xout.rearrange("(c p) n -> p c n", p=128)[:, :, tt * LTB:(tt + 1) * LTB],
                    in_=yln[:].rearrange("p (c n) -> p c n", n=LTB)
                ).then_inc(s_out, 16)
            sync.wait_ge(s_out, 16 * NLT)
            if debug:
                sync.dma_start(out=d_acc, in_=acc.ap()).then_inc(s_out, 16)
                sync.dma_start(out=d_coef, in_=coef_d.ap()).then_inc(s_out, 16)
                sync.wait_ge(s_out, 16 * NLT + 32)

        @block.gpsimd
        def _(gpsimd):
            for cc in range(NCH):
                slab = W2 if cc < 4 else W3
                gpsimd.dma_start(
                    out=slab[:].rearrange("p (c n) -> p c n", n=NT)[:, cc % 4:cc % 4 + 1, :],
                    in_=xin.rearrange("(c p) n -> p c n", p=128)[:, cc:cc + 1, :]
                ).then_inc(s_gxa, 16)
            gpsimd.dma_start(out=gwr[:].rearrange("p (c n) -> p c n", n=NG),
                             in_=gw_r.rearrange("c p n -> p c n")).then_inc(s_gw, 16)
            for sc in range(NTT):
                gpsimd.dma_start(out=xbf_d.ap()[:, sc * TT_:(sc + 1) * TT_],
                                 in_=xin[:, sc * TT_:(sc + 1) * TT_]).then_inc(s_xb, 16)
            gpsimd.dma_start(out=W1[:], in_=w1_l[0]).then_inc(s_w, 16)

            def xload(k2):
                if k2 >= NE * NTT:
                    return
                t2 = k2 % NTT
                gpsimd.wait_ge(s_xb, 16 * (t2 + 1))
                gpsimd.dma_start(
                    out=xbt[:, (t2 % 2) * NCH * TT_:((t2 % 2) + 1) * NCH * TT_]
                        .rearrange("p (c n) -> p c n", n=TT_),
                    in_=xbf_d.ap().rearrange("(c p) n -> p c n", p=128)[:, :, t2 * TT_:(t2 + 1) * TT_]
                ).then_inc(s_xt, 16)

            xload(0)
            xload(1)
            for e in range(NE):
                if e >= 1:
                    gpsimd.wait_ge(s_pmm, 2 * NJ * NTT * e)
                    gpsimd.dma_start(out=W1[:], in_=w1_l[e]).then_inc(s_w, 16)
                    gpsimd.dma_start(out=W2[:], in_=w2_l[e]).then_inc(s_w2, 16)
                    gpsimd.wait_ge(s_omm, 2 * NTT * e)
                    gpsimd.dma_start(out=W3[:], in_=w3_l[e]).then_inc(s_w3, 16)
                for t in range(NTT):
                    k = e * NTT + t
                    if k == 0:
                        gpsimd.wait_ge(s_gmm, 1)
                        gpsimd.dma_start(out=W2[:], in_=w2_l[0]).then_inc(s_w2, 16)
                        gpsimd.dma_start(out=W3[:], in_=w3_l[0]).then_inc(s_w3, 16)
                        gpsimd.wait_ge(s_a0, 16)
                    gpsimd.wait_ge(s_pmm, 2 * NJ * (k + 1))
                    xload(k + 2)
                    for og in range(2):
                        kk = k * 2 + og
                        for oo in range(4):
                            gpsimd.wait_ge(s_sev, 4 * kk + oo + 1)
                            gpsimd.dma_start(
                                out=acc.rearrange("(c p) n -> p c n", p=128)[:, og * 4 + oo:og * 4 + oo + 1, t * TT_:(t + 1) * TT_],
                                in_=stage[:, (kk % 2) * 2048 + oo * 512:(kk % 2) * 2048 + (oo + 1) * 512]
                                    .rearrange("p (c n) -> p c n", n=TT_),
                                accum_op=ALU.add).then_inc(s_ad, 16)

        @block.tensor
        def _(tensor):
            tensor.wait_ge(s_mi, 8 * 16)
            tensor.wait_ge(s_gw, 16)
            tensor.wait_ge(s_gxa, 16 * NCH)
            for g in range(32):
                for cd in range(NCH):
                    slab = W2 if cd < 4 else W3
                    ins = tensor.matmul(ps_p1[0][0:128, g * NG:(g + 1) * NG],
                                        slab[:, (cd % 4) * NT + g * 128:(cd % 4) * NT + (g + 1) * 128],
                                        gwr[:, cd * NG:(cd + 1) * NG],
                                        start=(cd == 0), stop=(cd == NCH - 1))
            ins.then_inc(s_gmm, 1)
            tensor.wait_ge(s_gcoef, 1)
            tensor.transpose(ps_p2[0][:, 0:128], ctm[:, 0:128], id_sb[:]).then_inc(s_gtr, 1)
            tensor.wait_ge(s_gtev, 1)
            tensor.transpose(ps_p2[0][:, 0:128], ctm[:, 128:256], id_sb[:]).then_inc(s_gtr, 1)
            for e in range(NE):
                for t in range(NTT):
                    k = e * NTT + t
                    par = t % 2
                    xof = par * NCH * TT_
                    tensor.wait_ge(s_w, 16 * (e + 1))
                    tensor.wait_ge(s_w2, 16 * (e + 1))
                    tensor.wait_ge(s_xt, 16 * (k + 1))
                    if e >= 2:
                        kc = (e - 2) * NTT + t
                        tensor.wait_ge(s_cd, 16 * (kc + 1))
                        if kc >= 1:
                            tensor.wait_ge(s_cev, kc)
                        else:
                            tensor.wait_ge(s_gtev, 2)
                        tensor.matmul(ps_p1[0][:, 0:TT_], on1[0:1, :], crow[0:1, :],
                                      start=True, stop=True).then_inc(s_cmm, 1)
                    for j in range(NJ):
                        kj = k * NJ + j
                        if kj == 0:
                            tensor.wait_ge(s_gev, 1)
                        if kj >= 2:
                            tensor.wait_ge(s_sil, kj - 1)
                        if e >= 2 and kj == k * NJ:
                            tensor.wait_ge(s_cev, (e - 2) * NTT + t + 1)
                        for cd in range(NCH):
                            ins = tensor.matmul(
                                ps_p1[j % 2][:],
                                W1[:, (j * NCH + cd) * 128:(j * NCH + cd + 1) * 128],
                                xbt[:, xof + cd * TT_: xof + (cd + 1) * TT_],
                                start=(cd == 0), stop=(cd == NCH - 1))
                        ins.then_inc(s_pmm, 1)
                        if kj == 0:
                            tensor.wait_ge(s_gtev, 2)
                        if kj >= 2:
                            tensor.wait_ge(s_glu, kj - 1)
                        for cd in range(NCH):
                            ins = tensor.matmul(
                                ps_p2[j % 2][:],
                                W2[:, (j * NCH + cd) * 128:(j * NCH + cd + 1) * 128],
                                xbt[:, xof + cd * TT_: xof + (cd + 1) * TT_],
                                start=(cd == 0), stop=(cd == NCH - 1))
                        ins.then_inc(s_pmm, 1)
                    if t == 0:
                        tensor.wait_ge(s_w3, 16 * (e + 1))
                    for og in range(2):
                        kk = k * 2 + og
                        if kk >= 1:
                            tensor.wait_ge(s_sev, 4 * kk)
                        for j in range(NJ):
                            tensor.wait_ge(s_glu, k * NJ + j + 1)
                            for oo in range(4):
                                o = og * 4 + oo
                                ins = tensor.matmul(
                                    ps_o[oo][:],
                                    W3[:, (j * NCH + o) * 128:(j * NCH + o + 1) * 128],
                                    h1[:, j * TT_:(j + 1) * TT_],
                                    start=(j == 0), stop=(j == NJ - 1))
                        ins.then_inc(s_omm, 1)
            for tt in range(NLT):
                tensor.wait_ge(s_lnl, 16 * (tt + 1))
                if tt >= 1:
                    tensor.wait_ge(s_rw, tt)
                for c in range(NCH):
                    tensor.matmul(ps_p1[0][0:1, 0:LTB], onc[:, 0:1],
                                  yln[:, c * LTB:(c + 1) * LTB],
                                  start=(c == 0), stop=(c == NCH - 1))
                for c in range(NCH):
                    tensor.wait_ge(s_sq, NCH * tt + c + 1)
                    tensor.matmul(ps_p1[1][0:1, 0:LTB], onc[:, 0:1],
                                  y2[:, (c % 2) * LTB:(c % 2 + 1) * LTB],
                                  start=(c == 0), stop=(c == NCH - 1)).then_inc(s_stm, 1)
                tensor.wait_ge(s_ab, tt + 1)
                if tt >= 1:
                    tensor.wait_ge(s_ap, NCH * tt)
                tensor.matmul(ps_p2[0][:, 0:LTB], on1[0:1, :], rows[0:1, 2 * LTB:3 * LTB],
                              start=True, stop=True)
                tensor.matmul(ps_p2[1][:, 0:LTB], on1[0:1, :], rows[0:1, 3 * LTB:4 * LTB],
                              start=True, stop=True).then_inc(s_bc, 1)

        @block.vector
        def _(vector):
            vector.wait_ge(s_gmm, 1)
            vector.wait_ge(s_mi, 8 * 16)
            vector.tensor_copy(out=gts[:], in_=ps_p1[0][0:128, 0:32 * NG]).then_inc(s_gev, 1)
            vector.drain()
            g3 = gts[:].rearrange("p (g e) -> p g e", e=NG)
            a_, b_ = bass.broadcast_tensor_aps(g3, gbr[:].rearrange("p (o e) -> p o e", o=1))
            vector.tensor_tensor(out=g3, in0=a_, in1=b_, op=ALU.add)
            vector.drain()
            rmax = gsc[:, 0:32]; rsum = gsc[:, 32:64]
            cnt = gsc[:, 64:96]; gtmp = gsc[:, 96:128]
            vector.reduce_max(out=rmax, in_=g3, axis=mybir.AxisListType.X)
            vector.drain()
            a_, b_ = bass.broadcast_tensor_aps(g3, rmax.rearrange("p (g o) -> p g o", o=1))
            vector.tensor_tensor(out=g3, in0=a_, in1=b_, op=ALU.subtract).then_inc(s_gsub, 1)
            vector.wait_ge(s_gexp, 1)
            vector.reduce_sum(out=rsum, in_=g3, axis=mybir.AxisListType.X)
            vector.drain()
            vector.reciprocal(out=rsum, in_=rsum)
            vector.drain()
            a_, b_ = bass.broadcast_tensor_aps(g3, rsum.rearrange("p (g o) -> p g o", o=1))
            vector.tensor_tensor(out=g3, in0=a_, in1=b_, op=ALU.mult)
            vector.drain()
            c3 = ctm[:].rearrange("p (g e) -> p g e", e=NG)
            for ei in range(NG):
                first = True
                for j in range(NG):
                    if j == ei:
                        continue
                    wj = g3[:, :, j:j + 1]
                    we = g3[:, :, ei:ei + 1]
                    if first:
                        vector.tensor_tensor(out=cnt.rearrange("p (g o) -> p g o", o=1),
                                             in0=wj, in1=we, op=ALU.is_gt)
                        first = False
                    else:
                        vector.tensor_tensor(out=gtmp.rearrange("p (g o) -> p g o", o=1),
                                             in0=wj, in1=we, op=ALU.is_gt)
                        vector.drain()
                        vector.tensor_tensor(out=cnt, in0=cnt, in1=gtmp, op=ALU.add)
                    vector.drain()
                vector.tensor_scalar(out=gtmp, in0=cnt, scalar1=1.5, scalar2=None,
                                     op0=ALU.is_le)
                vector.drain()
                ins = vector.tensor_tensor(out=c3[:, :, ei:ei + 1],
                                           in0=g3[:, :, ei:ei + 1],
                                           in1=gtmp.rearrange("p (g o) -> p g o", o=1),
                                           op=ALU.mult)
                vector.drain()
            ins.then_inc(s_gcoef, 1)
            vector.wait_ge(s_gtr, 1)
            vector.tensor_copy(out=ctm[:, 0:128], in_=ps_p2[0][:, 0:128]).then_inc(s_gtev, 1)
            vector.wait_ge(s_gtr, 2)
            vector.tensor_copy(out=ctm[:, 128:256], in_=ps_p2[0][:, 0:128]).then_inc(s_gtev, 1)
            for e in range(NE):
                for t in range(NTT):
                    k = e * NTT + t
                    if e >= 2:
                        kc = (e - 2) * NTT + t
                        vector.wait_ge(s_cmm, kc + 1)
                        vector.tensor_copy(out=coef_sb[:], in_=ps_p1[0][:, 0:TT_]).then_inc(s_cev, 1)
                    for j in range(NJ):
                        kj = k * NJ + j
                        vector.wait_ge(s_pmm, 2 * kj + 2)
                        vector.wait_ge(s_sil, kj + 1)
                        vector.scalar_tensor_tensor(
                            out=h1[:, j * TT_:(j + 1) * TT_], in0=ps_p2[j % 2][:],
                            scalar=b2a[:, e * NJ + j:e * NJ + j + 1], in1=tsil[:],
                            op0=ALU.add, op1=ALU.mult).then_inc(s_glu, 1)
                    for og in range(2):
                        kk = k * 2 + og
                        vector.wait_ge(s_omm, kk + 1)
                        sof = (kk % 2) * 2048
                        for oo in range(4):
                            o = og * 4 + oo
                            if kk >= 2:
                                vector.wait_ge(s_ad, 16 * (4 * (kk - 2) + oo + 1))
                            if e < 2:
                                vector.tensor_scalar(
                                    out=stage[:, sof + oo * 512:sof + (oo + 1) * 512],
                                    in0=ps_o[oo][:], scalar1=b3a[:, e * NCH + o:e * NCH + o + 1],
                                    scalar2=cstc[:, 3:4], op0=ALU.add,
                                    op1=ALU.mult).then_inc(s_sev, 1)
                            else:
                                vector.scalar_tensor_tensor(
                                    out=stage[:, sof + oo * 512:sof + (oo + 1) * 512], in0=ps_o[oo][:],
                                    scalar=b3a[:, e * NCH + o:e * NCH + o + 1], in1=coef_sb[:],
                                    op0=ALU.add, op1=ALU.mult).then_inc(s_sev, 1)
            for tt in range(NLT):
                vector.wait_ge(s_stm, NCH * (tt + 1))
                vector.tensor_scalar(out=rows[0:1, 0:LTB], in0=ps_p1[0][0:1, 0:LTB],
                                     scalar1=1.0 / D, scalar2=None, op0=ALU.mult)
                vector.tensor_scalar(out=rows[0:1, LTB:2 * LTB], in0=ps_p1[1][0:1, 0:LTB],
                                     scalar1=1.0 / D, scalar2=None, op0=ALU.mult).then_inc(s_rw, 1)
                vector.tensor_tensor(out=rows[0:1, 2 * LTB:3 * LTB], in0=rows[0:1, 0:LTB],
                                     in1=rows[0:1, 0:LTB], op=ALU.mult)
                vector.tensor_tensor(out=rows[0:1, 2 * LTB:3 * LTB], in0=rows[0:1, LTB:2 * LTB],
                                     in1=rows[0:1, 2 * LTB:3 * LTB],
                                     op=ALU.subtract).then_inc(s_var, 1)
                vector.wait_ge(s_ar, tt + 1)
                vector.tensor_tensor(out=rows[0:1, 3 * LTB:4 * LTB], in0=rows[0:1, 0:LTB],
                                     in1=rows[0:1, 2 * LTB:3 * LTB], op=ALU.mult)
                vector.tensor_scalar(out=rows[0:1, 3 * LTB:4 * LTB],
                                     in0=rows[0:1, 3 * LTB:4 * LTB], scalar1=-1.0,
                                     scalar2=None, op0=ALU.mult).then_inc(s_ab, 1)
                vector.wait_ge(s_bc, tt + 1)
                for c in range(NCH):
                    vector.tensor_tensor(out=stage[:, 0:LTB],
                                         in0=yln[:, c * LTB:(c + 1) * LTB],
                                         in1=ps_p2[0][:, 0:LTB], op=ALU.mult)
                    vector.tensor_tensor(out=yln[:, c * LTB:(c + 1) * LTB],
                                         in0=stage[:, 0:LTB],
                                         in1=ps_p2[1][:, 0:LTB], op=ALU.add).then_inc(s_ap, 1)

        @block.scalar
        def _(scalar):
            scalar.wait_ge(s_gsub, 1)
            scalar.activation(out=gts[:], in_=gts[:], func=AF.Exp).then_inc(s_gexp, 1)
            for e in range(NE):
                for t in range(NTT):
                    k = e * NTT + t
                    for j in range(NJ):
                        kj = k * NJ + j
                        scalar.wait_ge(s_pmm, 2 * kj + 1)
                        if kj >= 1:
                            scalar.wait_ge(s_glu, kj)
                        if k >= 1 and j == 0:
                            scalar.wait_ge(s_omm, 2 * k)
                        scalar.activation(out=tsil[:], in_=ps_p1[j % 2][:], func=AF.Silu,
                                          bias=b1a[:, e * NJ + j:e * NJ + j + 1]).then_inc(s_sil, 1)
            for tt in range(NLT):
                for c in range(NCH):
                    scalar.wait_ge(s_lnl, 16 * (tt + 1))
                    kk = NCH * tt + c
                    if kk >= 2:
                        scalar.wait_ge(s_stm, kk - 1)
                    scalar.activation(out=y2[:, (c % 2) * LTB:(c % 2 + 1) * LTB],
                                      in_=yln[:, c * LTB:(c + 1) * LTB],
                                      func=AF.Square).then_inc(s_sq, 1)
                scalar.wait_ge(s_var, tt + 1)
                scalar.activation(out=rows[0:1, 2 * LTB:3 * LTB], in_=rows[0:1, 2 * LTB:3 * LTB],
                                  func=AF.Ln, bias=cstc[0:1, 1:2])
                scalar.activation(out=rows[0:1, 2 * LTB:3 * LTB], in_=rows[0:1, 2 * LTB:3 * LTB],
                                  func=AF.Exp, scale=cstc[0:1, 2:3]).then_inc(s_ar, 1)


def build_l1_ffn(ne_used=10, debug=False):
    nc = bass.Bass("TRN2", target_bir_lowering=False, debug=False, detect_race_conditions=DETECT_RACES)
    xin = nc.dram_tensor("xin", [D, NT], F32, kind="ExternalInput").ap()
    xout = nc.dram_tensor("xout", [D, NT], F32, kind="ExternalOutput").ap()
    es_s = ExitStack()
    with es_s:
        emit_l1_ffn(nc, es_s, "", xin, xout, ne_used=ne_used, debug=debug)
    return nc


def build_fused(taps=False):
    nc = bass.Bass("TRN2", target_bir_lowering=False, debug=False, detect_race_conditions=DETECT_RACES)
    xin = nc.dram_tensor("xin", [D, NT], F32, kind="ExternalInput").ap()
    xout = nc.dram_tensor("xout", [D, NT], F32, kind="ExternalOutput").ap()
    kind = "ExternalOutput" if taps else "Internal"
    xb1 = nc.dram_tensor("xb1", [D, NT], F32, kind=kind)
    xb2 = nc.dram_tensor("xb2", [D, NT], F32, kind=kind)
    xb3 = nc.dram_tensor("xb3", [D, NT], F32, kind=kind)
    emits = [
        (emit_l0_attn, "a0", xin, xb1.ap(), {}),
        (emit_l0_ffn, "f0", xb1.ap(), xb2.ap(), {}),
        (emit_l1_attn, "a1", xb2.ap(), xb3.ap(), {}),
        (emit_l1_ffn, "f1", xb3.ap(), xout, {}),
    ]
    for fn, pfx, i_ap, o_ap, kw in emits:
        with nc.cleanup_on_exit():
            fn(nc, ExitStack(), pfx, i_ap, o_ap, **kw)
            nc.all_engine_barrier()
    return nc


def _l1_ffn_inputs(inputs, core):
    NJ = 16
    sw1, sw2, sw3 = inputs["L1_ffn_sw1"], inputs["L1_ffn_sw2"], inputs["L1_ffn_sw3"]
    rw1, rw2, rw3 = inputs["L1_ffn_rw1"], inputs["L1_ffn_rw2"], inputs["L1_ffn_rw3"]
    sb1, sb2, sb3 = inputs["L1_ffn_sb1"], inputs["L1_ffn_sb2"], inputs["L1_ffn_sb3"]
    rb1, rb2, rb3 = inputs["L1_ffn_rb1"], inputs["L1_ffn_rb2"], inputs["L1_ffn_rb3"]
    w1 = np.concatenate([sw1, rw1], 0)
    w2 = np.concatenate([sw2, rw2], 0)
    w3 = np.concatenate([sw3, rw3], 0)
    b1 = np.concatenate([sb1, rb1], 0)
    b2 = np.concatenate([sb2, rb2], 0)
    b3 = np.concatenate([sb3, rb3], 0)
    d = {
        "w1_l": w1.reshape(10, NCH, 128, NJ, 128).transpose(0, 2, 3, 1, 4).reshape(10, 128, NJ * NCH * 128),
        "w2_l": w2.reshape(10, NCH, 128, NJ, 128).transpose(0, 2, 3, 1, 4).reshape(10, 128, NJ * NCH * 128),
        "w3_l": w3.reshape(10, NJ, 128, NCH, 128).transpose(0, 2, 1, 3, 4).reshape(10, 128, NJ * NCH * 128),
        "b1_c": b1.reshape(10, NJ, 128).transpose(0, 2, 1),
        "b2_c": b2.reshape(10, NJ, 128).transpose(0, 2, 1),
        "b3_c": b3.reshape(10, NCH, 128).transpose(0, 2, 1),
        "gw_r": inputs["L1_ffn_gate_w"].reshape(NCH, 128, 8),
        "gb_row": np.tile(inputs["L1_ffn_gate_b"].reshape(1, 8), (128, 1)),
        "ident": np.eye(128, dtype=np.float32),
        "ones1": np.ones((1, 128), np.float32),
        "onesc": np.ones((128, 1), np.float32),
        "cst": np.tile(np.array([[EPS_IN, EPS_OUT, -0.5, 0.5]], np.float32), (128, 1)),
    }
    return {k: np.ascontiguousarray(np.asarray(v, dtype=np.float32)) for k, v in d.items()}
